# version 11
# baseline (speedup 1.0000x reference)
import contextlib
import math

import numpy as np

import concourse.bass as bass
import concourse.mybir as mybir
from concourse.bass_utils import run_bass_kernel_spmd

F32 = mybir.dt.float32
BF16 = mybir.dt.bfloat16
AF = mybir.ActivationFunctionType
ALU = mybir.AluOpType
AX = mybir.AxisListType

MAGIC = 12582912.0
TOK = 2048
D = 1024
NT = TOK // 128
DC = D // 128
N_ITEMS = 4
N_CORES = 8

ENGS = ("T", "V", "S", "P", "Y")
ESEM = {e: "e" + e for e in ENGS}
DSEM = {"Y": "dY", "S": "dS", "P": "dG"}
ALLSEMS = [ESEM[e] for e in ENGS] + list(DSEM.values())

_gid = [0]


class Tok:
    __slots__ = ("sem", "gid", "need", "val", "amt")

    def __init__(self, sem, amt):
        self.sem = sem
        self.amt = amt
        self.need = False
        self.val = None
        _gid[0] += 1
        self.gid = _gid[0]


class Buf:
    __slots__ = ("w", "r", "name")

    def __init__(self, name=""):
        self.w = None
        self.r = {}
        self.name = name


class V:
    __slots__ = ("ap", "buf")

    def __init__(self, ap, buf):
        self.ap = ap
        self.buf = buf

    def __getitem__(self, idx):
        return V(self.ap[idx], self.buf)

    def rearrange(self, *a, **k):
        return V(self.ap.rearrange(*a, **k), self.buf)

    def view(self, ap):
        return V(ap, self.buf)


def carve(v, lo, hi, name):
    return V(v.ap[:, lo:hi], Buf(name))


class P:
    def __init__(self, nc):
        self.nc = nc
        self.es = contextlib.ExitStack()
        self.recs = {e: [] for e in ENGS}
        self.semh = {}
        self.last = {}
        self.qlast = {}

    def sb(self, name, shape, dt):
        h = self.es.enter_context(self.nc.sbuf_tensor(name, list(shape), dt))
        return V(h[:], Buf(name)), h

    def ps(self, name, shape, dt):
        h = self.es.enter_context(self.nc.psum_tensor(name, list(shape), dt))
        return V(h[:], Buf(name)), h

    def _deps(self, eng, reads, writes):
        deps = {}
        for v in reads:
            b = v.buf
            if b.w is not None:
                t = b.w
                if t.gid > deps.get(t.sem, (None, -1))[1]:
                    deps[t.sem] = (t, t.gid)
        for v in writes:
            b = v.buf
            cand = ([b.w] if b.w is not None else []) + list(b.r.values())
            for t in cand:
                if t.gid > deps.get(t.sem, (None, -1))[1]:
                    deps[t.sem] = (t, t.gid)
        out = []
        for sem, (t, _) in deps.items():
            if eng == "T" and sem == "eT":
                continue
            out.append(t)
        return out

    def _mark(self, eng, tok, reads, writes):
        for v in writes:
            v.buf.w = tok
            v.buf.r = {}
        for v in reads:
            prev = v.buf.r.get(tok.sem)
            if prev is None or prev.gid < tok.gid:
                v.buf.r[tok.sem] = tok

    def op(self, eng, meth, *args, reads=(), writes=(), **kw):
        for t in self._deps(eng, reads, writes):
            t.need = True
            self.recs[eng].append(("w", t))
        tok = Tok(ESEM[eng], 1)
        args = tuple(a.ap if isinstance(a, V) else a for a in args)
        kw = {k: (v.ap if isinstance(v, V) else v) for k, v in kw.items()}
        self.recs[eng].append(("op", meth, args, kw, tok))
        self.last[tok.sem] = tok
        self._mark(eng, tok, reads, writes)
        return tok

    def dma(self, eng, out, in_, reads=(), writes=(), meth="dma_start", **kw):
        reads = list(reads)
        writes = list(writes)
        if isinstance(out, V):
            writes.append(out)
        if isinstance(in_, V):
            reads.append(in_)
        for t in self._deps(eng, reads, writes):
            t.need = True
            self.recs[eng].append(("w", t))
        q = DSEM[eng]
        if q in self.qlast:
            self.recs[eng].append(("w", self.qlast[q]))
        tok = Tok(DSEM[eng], 16)
        self.qlast[q] = tok
        o = out.ap if isinstance(out, V) else out
        i = in_.ap if isinstance(in_, V) else in_
        self.recs[eng].append(("op", meth, (), {"out": o, "in_": i, **kw}, tok))
        tok.need = True
        self.last[tok.sem] = tok
        self._mark(eng, tok, reads, writes)
        return tok

    def wait_on(self, eng, toks):
        for t in toks:
            t.need = True
            self.recs[eng].append(("w", t))

    def barrier(self):
        toks = [t for t in self.last.values()]
        for e in ENGS:
            self.wait_on(e, toks)

    def finalize_and_replay(self, block):
        cnt = {s: 0 for s in ALLSEMS}
        for e in ENGS:
            for r in self.recs[e]:
                if r[0] == "op":
                    tok = r[-1]
                    if tok.need:
                        cnt[tok.sem] += tok.amt
                        tok.val = cnt[tok.sem]
        for s in ALLSEMS:
            self.semh[s] = self.es.enter_context(self.nc.semaphore(name="sem_" + s))
        secs = {"T": block.tensor, "V": block.vector, "S": block.scalar,
                "P": block.gpsimd, "Y": block.sync}
        for e in ENGS:
            recs = self.recs[e]
            if not recs:
                continue
            hwm = {}

            def run(eng_h, recs=recs, hwm=hwm):
                for r in recs:
                    if r[0] == "w":
                        t = r[1]
                        if t.val is None:
                            continue
                        if hwm.get(t.sem, 0) >= t.val:
                            continue
                        eng_h.wait_ge(self.semh[t.sem], t.val)
                        hwm[t.sem] = t.val
                    else:
                        _, meth, args, kw, tok = r
                        inst = getattr(eng_h, meth)(*args, **kw)
                        if tok.need:
                            inst.then_inc(self.semh[tok.sem], tok.amt)
            secs[e](run)


class Arena:

    def __init__(self, p, name, kb):
        self.p = p
        self.n = kb * 512
        v, h = p.sb(name, [128, self.n], BF16)
        self.t = h
        self.off = 0

    def _take(self, n_bf16, name):
        off = (self.off + 1) & ~1
        assert off + n_bf16 <= self.n, f"arena overflow at {name}: {off}+{n_bf16}>{self.n}"
        self.off = off + n_bf16
        return off

    @staticmethod
    def _shape(ap, shape):
        if len(shape) == 2:
            return ap.rearrange("p (a b) -> p a b", b=shape[-1])
        if len(shape) == 3:
            return ap.rearrange("p (a b c) -> p a b c", b=shape[1], c=shape[2])
        return ap

    def bf16(self, name, shape):
        n = int(np.prod(shape))
        off = self._take(n, name)
        return V(self._shape(self.t[:, off:off + n], shape), Buf(name))

    def f32(self, name, shape):
        n = int(np.prod(shape))
        off = self._take(2 * n, name)
        return V(self._shape(self.t[:, off:off + 2 * n].bitcast(F32), shape), Buf(name))

    def at_bf16(self, off, name, shape):
        n = int(np.prod(shape))
        assert off + n <= self.n
        return V(self._shape(self.t[:, off:off + n], shape), Buf(name))

    def at_f32(self, off, name, shape):
        n = int(np.prod(shape))
        assert off + 2 * n <= self.n
        return V(self._shape(self.t[:, off:off + 2 * n].bitcast(F32), shape), Buf(name))

    def mark(self):
        return self.off

    def reset(self, to=None):
        self.p.barrier()
        self.off = 0 if to is None else to


def ceil_div(a, b):
    return (a + b - 1) // b



WNAMES = {
    "l0a": ["L0_attn_Wq", "L0_attn_bq", "L0_attn_Wk", "L0_attn_bk", "L0_attn_Wv",
            "L0_attn_bv", "L0_attn_Wo", "L0_attn_bo", "L0_attn_Wtq", "L0_attn_btq",
            "L0_attn_Wfq", "L0_attn_bfq"],
    "l0f": ["L0_ffn_U1", "L0_ffn_V1", "L0_ffn_b1", "L0_ffn_U2", "L0_ffn_V2", "L0_ffn_b2"],
    "l1a": ["L1_attn_Wq", "L1_attn_Wk", "L1_attn_Wv"],
    "l1f": ["L1_ffn_w_gate", "L1_ffn_w_up", "L1_ffn_w_down"],
}

WSHAPES = {"L0_attn_Wq": [D, D], "L0_attn_Wk": [D, D], "L0_attn_Wv": [D, D],
           "L0_attn_Wo": [D, D], "L0_attn_Wtq": [2, D], "L0_attn_Wfq": [D, 2],
           "L0_attn_bq": [D], "L0_attn_bk": [D], "L0_attn_bv": [D],
           "L0_attn_bo": [D], "L0_attn_btq": [2], "L0_attn_bfq": [D],
           "L0_ffn_U1": [D, 256], "L0_ffn_V1": [256, 4096], "L0_ffn_b1": [4096],
           "L0_ffn_U2": [4096, 256], "L0_ffn_V2": [256, D], "L0_ffn_b2": [D],
           "L1_attn_Wq": [D, D], "L1_attn_Wk": [D, D], "L1_attn_Wv": [D, D],
           "L1_ffn_w_gate": [2816, D], "L1_ffn_w_up": [2816, D],
           "L1_ffn_w_down": [D, 2816]}


def build_program(blocks, n_items=N_ITEMS):
    nc = bass.Bass("TRN2", target_bir_lowering=False)
    p = P(nc)
    c = {}

    x_in = nc.dram_tensor("x", [n_items * TOK, D], F32, kind="ExternalInput").ap()
    y_out = nc.dram_tensor("y", [n_items * TOK, D], F32, kind="ExternalOutput").ap()
    names = []
    w = {}
    for b in blocks:
        names.extend(WNAMES[b])
    for nm in names:
        w[nm] = nc.dram_tensor(nm, WSHAPES[nm], F32, kind="ExternalInput").ap()
    c["w"] = w

    scr = {}
    if "l0a" in blocks:
        scr["g0"] = nc.dram_tensor("s_g0", [D, D], BF16, kind="Internal").ap()
        scr["wvT0"] = nc.dram_tensor("s_wvT0", [D, D], BF16, kind="Internal").ap()
        scr["woT0"] = nc.dram_tensor("s_woT0", [D, D], BF16, kind="Internal").ap()
    if "l1a" in blocks:
        scr["g1"] = nc.dram_tensor("s_g1", [D, D], BF16, kind="Internal").ap()
        scr["wvT1"] = nc.dram_tensor("s_wvT1", [D, D], BF16, kind="Internal").ap()
    if "l1f" in blocks:
        scr["wgT"] = nc.dram_tensor("s_wgT", [D, 2816], BF16, kind="Internal").ap()
        scr["wuT"] = nc.dram_tensor("s_wuT", [D, 2816], BF16, kind="Internal").ap()
        scr["wdT"] = nc.dram_tensor("s_wdT", [2816, D], BF16, kind="Internal").ap()
    c["scr"] = scr

    c["xr"], _ = p.sb("xr", [128, NT, D], F32)
    c["id_bf"], _ = p.sb("id_bf", [128, 128], BF16)
    c["id_f"], _ = p.sb("id_f", [128, 128], F32)
    c["stat"], _ = p.sb("stat", [128, 192], F32)
    c["qt"], _ = p.sb("qt", [128, 48], F32)
    c["crec"], _ = p.sb("crec", [128, NT], F32)
    c["s01"], _ = p.sb("s01", [128, NT], F32)
    c["consts"], _ = p.sb("consts", [128, 12], F32)
    c["smalls"], _ = p.sb("smalls", [128, 160], F32)
    c["ones128"], _ = p.sb("ones128", [128, 1], F32)
    c["b1_pp"], _ = p.sb("b1_pp", [128, 32], F32)
    c["wtqT"], _ = p.sb("wtqT", [128, 16], F32)
    c["wfq_pp"], _ = p.sb("wfq_pp", [128, 16], F32)
    c["bq_pp"], _ = p.sb("bq_pp", [128, 8], F32)
    c["w1_ppb"], _ = p.sb("w1_ppb", [128, 8], BF16)
    c["bv_ppb"], _ = p.sb("bv_ppb", [128, 8], BF16)
    c["stripeA"], _ = p.sb("stripeA", [128, 2048], F32)
    c["stripeC"], _ = p.sb("stripeC", [128, 1536], F32)
    c["rows_bf"], _ = p.sb("rows_bf", [128, 2176], BF16)
    c["bc_bc"], _ = p.sb("bc_bc", [128, 1024], F32)
    c["cmask"], _ = p.sb("cmask", [128, 4, 512], BF16)

    c["pb"] = []
    for i in range(7):
        v, _ = p.ps("pb%d" % i, [128, 512], F32)
        c["pb"].append(v)
    c["pstb"], _ = p.ps("pstb", [128, 1024], BF16)

    ar = Arena(p, "arena", 114)
    c["ar"] = ar
    c["p"] = p
    c["nc"] = nc

    emit_prep(p, c, blocks)
    ar.reset()

    for item in range(n_items):
        xs = x_in[item * TOK:(item + 1) * TOK, :].rearrange("(t p) d -> p t d", p=128)
        p.dma("Y", c["xr"], xs)
        for b in blocks:
            if b == "l0a":
                emit_l0_attn(p, c)
            elif b == "l0f":
                emit_l0_ffn(p, c)
            elif b == "l1a":
                emit_l1_attn(p, c)
            elif b == "l1f":
                emit_l1_ffn(p, c)
            ar.reset()
        ys = y_out[item * TOK:(item + 1) * TOK, :].rearrange("(t p) d -> p t d", p=128)
        p.dma("Y", ys, c["xr"], writes=())
        ar.reset()

    if DBG_STATICS:
        aap = nc.dram_tensor("dbg_arena", [128, ar.n], BF16, kind="ExternalOutput").ap()
        p.dma("Y", aap, V(ar.t[:], Buf("arena_dump")), writes=())
        for nm in ("stat", "smalls", "stripeA", "crec", "s01", "qt", "rows_bf", "bc_bc"):
            v = c[nm]
            shp = list(v.ap.shape)
            dt = v.ap.dtype
            dap = nc.dram_tensor("dbg_" + nm, shp, dt, kind="ExternalOutput").ap()
            p.dma("Y", dap, v, writes=())
    p.barrier()
    with nc.Block() as block:
        p.finalize_and_replay(block)
    p.es.close()
    return nc, names



def emit_prep(p, c, blocks):
    ar = c["ar"]
    w = c["w"]
    scr = c["scr"]
    pb = c["pb"]
    sA = c["stripeA"]
    sC = c["stripeC"]

    p.op("P", "memset", c["smalls"], 1.0, writes=[c["smalls"]])
    p.op("P", "affine_select", c["id_f"], c["smalls"][:, 0:128], pattern=[[1, 128]],
         compare_op=ALU.is_equal, fill=0.0, base=0, channel_multiplier=-1,
         reads=[c["smalls"]], writes=[c["id_f"]])
    p.op("V", "tensor_copy", c["id_bf"], c["id_f"], reads=[c["id_f"]], writes=[c["id_bf"]])
    p.op("V", "memset", c["ones128"], 1.0, writes=[c["ones128"]])
    for i, val in ((0, 1e-5), (1, -0.5), (2, 1.0 / 32.0), (3, 1.0),
                   (4, -128.0), (5, -32.0), (6, MAGIC), (7, -MAGIC * 0.5),
                   (8, -2.0 * MAGIC), (9, 0.5), (10, 2.0)):
        p.op("V", "memset", c["consts"][:, i:i + 1], val, writes=[c["consts"]])
    p.op("V", "memset", sC.view(sC.ap[0:1, 1024:1536]), 1.0, writes=[sC])
    p.op("V", "memset", c["rows_bf"].view(c["rows_bf"].ap[0:1, 2048:2176]), 1.0,
         writes=[c["rows_bf"]])
    p.op("V", "memset", c["cmask"], 1.0, writes=[c["cmask"]])
    for d in range(4):
        p.op("P", "affine_select", c["cmask"][:, d, :], c["cmask"][:, d, :],
             pattern=[[-1, 512]], compare_op=ALU.is_ge, fill=0.0, base=d * 128,
             channel_multiplier=1, reads=[c["cmask"]], writes=[c["cmask"]])

    stage_f = ar.f32("prep_stage_f", (2816,))
    stage_b = [ar.bf16("prep_stage_b0", (2816,)), ar.bf16("prep_stage_b1", (2816,))]
    tstage = [ar.bf16("prep_tstage0", (22, 128)), ar.bf16("prep_tstage1", (22, 128))]
    qb = ar.f32("prep_qb", (3, 512))
    sc = ar.f32("prep_sc", (32,))
    rc = ar.f32("prep_rc", (32,))
    wqb = ar.bf16("prep_wqb", (DC, 1024))
    wkb = ar.bf16("prep_wkb", (DC, 1024))
    gstage = ar.bf16("prep_gstage", (1024,))

    def wT_prep(src_ap, dst_ap, rows, cols):
        nrc = rows // 128
        ncc = cols // 128
        dview = dst_ap.rearrange("(c p) o -> p c o", p=128)
        for rci in range(nrc):
            par = rci % 2
            st_ = stage_b[par].view(stage_b[par].ap[:, 0:cols])
            p.dma("P", st_, src_ap[rci * 128:(rci + 1) * 128, :])

            def dv(cc, par=par):
                return tstage[par].view(tstage[par].ap[:, cc, :])
            pe_transpose_sb(p, c, st_, dv, ncc)
            for t in range(ncc):
                p.dma("Y" if t % 2 == 0 else "P",
                      dview[:, t, rci * 128:(rci + 1) * 128],
                      tstage[par].view(tstage[par].ap[:, t, :]))

    def load_wqk(nmq, nmk):
        for rc_ in range(DC):
            p.dma("P", wqb[:, rc_, :], w[nmq][rc_ * 128:(rc_ + 1) * 128, :])
            p.dma("P", wkb[:, rc_, :], w[nmk][rc_ * 128:(rc_ + 1) * 128, :])

    def build_g(dst_ap):
        gview = dst_ap.rearrange("(c p) o -> p c o", p=128)
        for d1c in range(DC):
            for h in range(2):
                bank = pb[(d1c % 2) * 2 + h]
                for oc in range(DC):
                    p.op("T", "matmul", bank, wqb[:, oc, d1c * 128:(d1c + 1) * 128],
                         wkb[:, oc, h * 512:(h + 1) * 512],
                         start=(oc == 0), stop=(oc == DC - 1),
                         reads=[wqb, wkb], writes=[bank])
                p.op("V", "tensor_copy", gstage[:, h * 512:(h + 1) * 512], bank,
                     reads=[bank], writes=[gstage])
            p.dma("Y", gview[:, d1c, :], gstage)

    def row_to_pp(src_row, pp_f32, pp_bf):
        for oc in range(DC):
            p.op("T", "transpose", pb[6].view(pb[6].ap[:, oc:oc + 1]),
                 src_row.view(src_row.ap[0:1, oc * 128:(oc + 1) * 128]),
                 c["id_f"][0:1, 0:1], reads=[src_row, c["id_f"]], writes=[pb[6]])
        if pp_f32 is not None:
            p.op("V", "tensor_copy", pp_f32, pb[6].view(pb[6].ap[:, 0:8]),
                 reads=[pb[6]], writes=[pp_f32])
        p.op("V", "tensor_copy", pp_bf, pb[6].view(pb[6].ap[:, 0:8]),
             reads=[pb[6]], writes=[pp_bf])

    if "l0a" in blocks:
        wT_prep(w["L0_attn_Wv"], scr["wvT0"], D, D)
        wT_prep(w["L0_attn_Wo"], scr["woT0"], D, D)
        st_ = stage_f.view(stage_f.ap[0:8, 0:128])
        p.dma("Y", st_, w["L0_attn_bq"].rearrange("(c p) -> c p", p=128))
        p.op("T", "transpose", pb[6].view(pb[6].ap[:, 0:8]), st_, c["id_f"][0:8, 0:8],
             reads=[st_, c["id_f"]], writes=[pb[6]])
        p.op("V", "tensor_copy", c["bq_pp"], pb[6].view(pb[6].ap[:, 0:8]),
             reads=[pb[6]], writes=[c["bq_pp"]])
        p.dma("Y", st_, w["L0_attn_bv"].rearrange("(c p) -> c p", p=128))
        p.op("T", "transpose", pb[6].view(pb[6].ap[:, 0:8]), st_, c["id_f"][0:8, 0:8],
             reads=[st_, c["id_f"]], writes=[pb[6]])
        p.op("V", "tensor_copy", c["bv_ppb"], pb[6].view(pb[6].ap[:, 0:8]),
             reads=[pb[6]], writes=[c["bv_ppb"]])
        load_wqk("L0_attn_Wq", "L0_attn_Wk")
        build_g(scr["g0"])
        bqb = ar.bf16("prep_bqb", (8,))
        p.op("V", "tensor_copy", bqb, c["bq_pp"], reads=[c["bq_pp"]], writes=[bqb])
        for h in range(2):
            bank = pb[4 + h]
            for oc in range(DC):
                p.op("T", "matmul", bank.view(bank.ap[0:1, :]),
                     bqb[:, oc:oc + 1],
                     wkb[:, oc, h * 512:(h + 1) * 512],
                     start=(oc == 0), stop=(oc == DC - 1),
                     reads=[bqb, wkb], writes=[bank])
            p.op("V", "tensor_copy", sA.view(sA.ap[0:1, h * 512:(h + 1) * 512]),
                 bank.view(bank.ap[0:1, :]), reads=[bank], writes=[sA])
        row_to_pp(sA.view(sA.ap[0:1, 0:1024]), None, c["w1_ppb"])
        woTsl = ar.bf16("prep_woTsl", (DC, 1024))
        p.dma("Y", woTsl, scr["woT0"].rearrange("(c p) o -> p c o", p=128),
              writes=[tstage[0], tstage[1]])
        for h in range(2):
            bank = pb[4 + h]
            for oc in range(DC):
                p.op("T", "matmul", bank.view(bank.ap[0:1, :]),
                     c["bv_ppb"][:, oc:oc + 1], woTsl[:, oc, h * 512:(h + 1) * 512],
                     start=(oc == 0), stop=(oc == DC - 1),
                     reads=[c["bv_ppb"], woTsl], writes=[bank])
        p.dma("Y", sA.view(sA.ap[64:65, 0:1024]), w["L0_attn_bfq"].unsqueeze(0))
        p.dma("Y", sA.view(sA.ap[64:65, 1024:2048]), w["L0_attn_bo"].unsqueeze(0))
        p.op("V", "scalar_tensor_tensor", sA.view(sA.ap[0:1, 0:1024]),
             sA.view(sA.ap[64:65, 0:1024]), 0.1, sA.view(sA.ap[64:65, 1024:2048]),
             op0=ALU.mult, op1=ALU.add, reads=[sA], writes=[sA])
        for h in range(2):
            p.op("V", "tensor_tensor", sA.view(sA.ap[0:1, h * 512:(h + 1) * 512]),
                 sA.view(sA.ap[0:1, h * 512:(h + 1) * 512]),
                 pb[4 + h].view(pb[4 + h].ap[0:1, :]), op=ALU.add,
                 reads=[sA, pb[4 + h]], writes=[sA])
        for h in range(2):
            bank = pb[h]
            p.op("T", "matmul", bank, sC.view(sC.ap[0:1, 1024:1152]),
                 sA.view(sA.ap[0:1, h * 512:(h + 1) * 512]), start=True, stop=True,
                 reads=[sC, sA], writes=[bank])
            p.op("V", "tensor_copy", c["bc_bc"][:, h * 512:(h + 1) * 512], bank,
                 reads=[bank], writes=[c["bc_bc"]])
        st2 = stage_f.view(stage_f.ap[0:2, 0:1024])
        p.dma("Y", st2, w["L0_attn_Wtq"])
        for cc in range(DC):
            p.op("T", "transpose", pb[6].view(pb[6].ap[:, 2 * cc:2 * cc + 2]),
                 st2.view(st2.ap[:, cc * 128:(cc + 1) * 128]), c["id_f"][0:2, 0:2],
                 reads=[st2, c["id_f"]], writes=[pb[6]])
        p.op("V", "tensor_copy", c["wtqT"], pb[6].view(pb[6].ap[:, 0:16]),
             reads=[pb[6]], writes=[c["wtqT"]])
        p.dma("Y", c["wfq_pp"].rearrange("p (a b) -> p a b", b=2),
              w["L0_attn_Wfq"].rearrange("(c p) t -> p c t", p=128),
              allow_slow_non_contiguous=True)
        for cc in range(DC):
            p.op("T", "transpose", pb[6].view(pb[6].ap[0:2, 0:128]),
                 c["wfq_pp"].view(c["wfq_pp"].ap.rearrange("p (a b) -> p a b", b=2)[:, cc, :]),
                 c["id_f"], reads=[c["wfq_pp"], c["id_f"]], writes=[pb[6]])
            p.op("V", "tensor_copy", sC.view(sC.ap[0:2, cc * 128:(cc + 1) * 128]),
                 pb[6].view(pb[6].ap[0:2, 0:128]), reads=[pb[6]], writes=[sC])
        p.dma("Y", c["stat"].view(c["stat"].ap[0:1, 60:62]), w["L0_attn_btq"].unsqueeze(0))

    if "l0f" in blocks:
        st_ = stage_f.view(stage_f.ap[0:32, 0:128])
        p.dma("Y", st_, w["L0_ffn_b1"].rearrange("(c p) -> c p", p=128))
        p.op("T", "transpose", pb[6].view(pb[6].ap[:, 0:32]), st_, c["id_f"][0:32, 0:32],
             reads=[st_, c["id_f"]], writes=[pb[6]])
        p.op("V", "tensor_copy", c["b1_pp"], pb[6].view(pb[6].ap[:, 0:32]),
             reads=[pb[6]], writes=[c["b1_pp"]])

    if "l1a" in blocks:
        wT_prep(w["L1_attn_Wv"], scr["wvT1"], D, D)
        load_wqk("L1_attn_Wq", "L1_attn_Wk")
        build_g(scr["g1"])

    if "l1f" in blocks:
        def quant_w(src_ap, dst_ap, rows, cols):
            nrc = rows // 128
            ncc = cols // 128
            dview = dst_ap.rearrange("(c p) o -> p c o", p=128)
            for rci in range(nrc):
                par = rci % 2
                sf = stage_f.view(stage_f.ap[:, 0:cols])
                p.dma("Y", sf, src_ap[rci * 128:(rci + 1) * 128, :])
                sb_ = stage_b[par].view(stage_b[par].ap[:, 0:cols])
                for q0 in range(0, cols, 512):
                    qn = min(512, cols - q0)
                    emit_quant(p, c, sf[:, q0:q0 + qn], sb_[:, q0:q0 + qn],
                               qb, sc, rc, qn)

                def dv(t, par=par):
                    return tstage[par].view(tstage[par].ap[:, t, :])
                pe_transpose_sb(p, c, sb_, dv, ncc)
                for t in range(ncc):
                    p.dma("Y" if t % 2 == 0 else "P",
                          dview[:, t, rci * 128:(rci + 1) * 128],
                          tstage[par].view(tstage[par].ap[:, t, :]))
        quant_w(w["L1_ffn_w_gate"], scr["wgT"], 2816, D)
        quant_w(w["L1_ffn_w_up"], scr["wuT"], 2816, D)
        quant_w(w["L1_ffn_w_down"], scr["wdT"], D, 2816)



def emit_quant(p, c, src, out, qb, sc, rc, F):
    nb = F // 16
    A = qb[:, 0, 0:F]
    B1 = qb[:, 1, 0:F]
    B2 = qb[:, 2, 0:F]
    scv = sc[:, 0:nb]
    rcv = rc[:, 0:nb]
    s3 = src.rearrange("p (b k) -> p b k", k=16)
    bc = lambda vv: vv.view(vv.ap.unsqueeze(2).broadcast_to([128, nb, 16]))
    p.op("V", "tensor_reduce", scv, s3, axis=AX.X, op=ALU.max,
         apply_absolute_value=True, reads=[src], writes=[sc])
    p.op("V", "tensor_scalar", scv, scv, 1.0 / 6.0, 1e-12, ALU.mult, ALU.max,
         reads=[sc], writes=[sc])
    p.op("V", "reciprocal", rcv, scv, reads=[sc], writes=[rc])
    p.op("S", "activation", A, src, AF.Abs, reads=[src], writes=[qb])
    p.op("V", "tensor_tensor", B1.rearrange("p (b k) -> p b k", k=16),
         A.rearrange("p (b k) -> p b k", k=16), bc(rcv), op=ALU.mult,
         reads=[qb, rc], writes=[qb])
    cst = c["consts"]
    p.op("V", "tensor_scalar", B2, B1, 2.0, 2.0, ALU.min, ALU.mult, reads=[qb], writes=[qb])
    p.op("S", "activation", B2, B2, AF.Identity, bias=cst[:, 6:7],
         reads=[qb, cst], writes=[qb])
    p.op("S", "activation", B2, B2, AF.Identity, scale=cst[:, 9:10], bias=cst[:, 7:8],
         reads=[qb, cst], writes=[qb])
    p.op("V", "tensor_scalar", A, B1, 2.0, 0.0, ALU.subtract, ALU.max, reads=[qb], writes=[qb])
    p.op("S", "activation", A, A, AF.Identity, bias=cst[:, 6:7],
         reads=[qb, cst], writes=[qb])
    p.op("V", "tensor_scalar", A, A, MAGIC, 2.0, ALU.subtract, ALU.min, reads=[qb], writes=[qb])
    p.op("V", "tensor_tensor", B2, B2, A, op=ALU.add, reads=[qb], writes=[qb])
    p.op("V", "tensor_scalar", A, B1, 4.5, 0.0, ALU.subtract, ALU.max, reads=[qb], writes=[qb])
    p.op("V", "tensor_scalar", A, A, 1.2, MAGIC, ALU.min, ALU.add, reads=[qb], writes=[qb])
    p.op("S", "activation", A, A, AF.Identity, scale=cst[:, 10:11], bias=cst[:, 8:9],
         reads=[qb, cst], writes=[qb])
    p.op("V", "tensor_tensor", B2, B2, A, op=ALU.add, reads=[qb], writes=[qb])
    p.op("S", "activation", A, src, AF.Sign, reads=[src], writes=[qb])
    p.op("V", "tensor_tensor", B1.rearrange("p (b k) -> p b k", k=16),
         A.rearrange("p (b k) -> p b k", k=16), bc(scv), op=ALU.mult,
         reads=[qb, sc], writes=[qb])
    p.op("V", "tensor_tensor", out, B2, B1, op=ALU.mult, reads=[qb], writes=[out])


def pe_transpose_sb(p, c, src_v, dst_view1, ncc, par=0):
    pstb = c["pstb"]
    for base in range(0, ncc, 8):
        nb = min(8, ncc - base)
        for b in range(nb):
            p.op("T", "transpose", pstb.view(pstb.ap[:, b * 128:(b + 1) * 128]),
                 src_v.view(src_v.ap[:, (base + b) * 128:(base + b + 1) * 128]), c["id_bf"],
                 reads=[src_v, c["id_bf"]], writes=[pstb])
        for b in range(nb):
            p.op("S", "copy", dst_view1(base + b),
                 pstb.view(pstb.ap[:, b * 128:(b + 1) * 128]),
                 reads=[pstb], writes=[dst_view1(base + b)])


def emit_xT(p, c, xT, src=None, bf_src=None):
    ar = c["ar"]
    mk = ar.mark()
    xbt = ar.bf16("xbt", (1024,))
    src = src if src is not None else c["xr"]
    for t in range(NT):
        if bf_src is None:
            s = xbt
            p.op("V", "tensor_copy", s, src[:, t, :], reads=[src], writes=[xbt])
        else:
            s = bf_src[:, t, :]

        def dv(cc, t=t):
            return xT.view(xT.ap[:, cc, t * 128:(t + 1) * 128])
        pe_transpose_sb(p, c, s, dv, DC)
    ar.off = mk


def emit_ln2(p, c, xsl, bA, bB, par):
    st = c["stat"]
    base = 16 + par * 8
    mu = st[:, base:base + 1]
    msq = st[:, base + 1:base + 2]
    t1 = st[:, base + 2:base + 3]
    var = st[:, base + 3:base + 4]
    rstd = st[:, base + 4:base + 5]
    nb = st[:, base + 5:base + 6]
    t2 = st[:, base + 6:base + 7]
    m2 = st[:, base + 7:base + 8]
    x0 = xsl.view(xsl.ap[:, 0:512])
    x1 = xsl.view(xsl.ap[:, 512:1024])
    p.op("S", "activation", bA, x0, AF.Square, accum_out=msq.ap,
         reads=[xsl], writes=[bA, st])
    p.op("S", "activation", bB, x1, AF.Square, accum_out=t2.ap,
         reads=[xsl], writes=[bB, st])
    p.op("S", "activation", bA, x0, AF.Identity, accum_out=mu.ap,
         reads=[xsl], writes=[bA, st])
    p.op("S", "activation", bB, x1, AF.Identity, accum_out=m2.ap,
         reads=[xsl], writes=[bB, st])
    p.op("V", "tensor_tensor", mu, mu, m2, op=ALU.add, reads=[st], writes=[st])
    p.op("V", "tensor_scalar", mu, mu, 1.0 / D, None, ALU.mult, reads=[st], writes=[st])
    p.op("V", "tensor_tensor", msq, msq, t2, op=ALU.add, reads=[st], writes=[st])
    p.op("V", "tensor_scalar", msq, msq, 1.0 / D, None, ALU.mult, reads=[st], writes=[st])
    p.op("V", "tensor_tensor", t1, mu, mu, op=ALU.mult, reads=[st], writes=[st])
    p.op("V", "tensor_tensor", var, msq, t1, op=ALU.subtract, reads=[st], writes=[st])
    p.op("S", "activation", t1, var, AF.Ln, bias=c["consts"][:, 0:1],
         reads=[st, c["consts"]], writes=[st])
    p.op("S", "activation", rstd, t1, AF.Exp, scale=c["consts"][:, 1:2],
         reads=[st, c["consts"]], writes=[st])
    p.op("V", "tensor_tensor", t1, mu, rstd, op=ALU.mult, reads=[st], writes=[st])
    p.op("V", "tensor_scalar", nb, t1, -1.0, None, ALU.mult, reads=[st], writes=[st])
    p.op("V", "tensor_scalar", xsl, xsl, rstd, nb, ALU.mult, ALU.add,
         reads=[xsl, st], writes=[xsl])



L0A_NT = NT
L0A_BAR = 0
L1F_BAR = 0
DBG_STATICS = False


def emit_l0_attn(p, c):
    ar = c["ar"]
    w = c["w"]
    scr = c["scr"]
    pb = c["pb"]
    pstb = c["pstb"]
    xr = c["xr"]
    st = c["stat"]
    sm = c["smalls"]
    sA = c["stripeA"]
    sC = c["stripeC"]
    rbf = c["rows_bf"]

    qp2 = sA.view(sA.ap[0:2, 0:2048])
    qp3 = sA.view(sA.ap[0:3, 0:2048])
    onesf = sC.view(sC.ap[0:1, 1024:1536])
    r1b = rbf.view(rbf.ap[0:1, 0:2048])
    onesb = rbf.view(rbf.ap[0:1, 2048:2176])
    wfqT = sC.view(sC.ap[0:2, 0:1024])
    btq = st[0:1, 60:62]

    xT = ar.bf16("xT", (DC, TOK))
    v_sb = ar.bf16("v_sb", (NT, D))
    g0s = ar.bf16("g0s", (DC, 1024))
    woTs = ar.bf16("woTs", (DC, 1024))
    off_wvT = (ar.mark() + 1) & ~1
    wvTs = ar.bf16("wvTs", (DC, 1024))
    xt32 = ar.at_f32(off_wvT, "xt32", (DC, 512))

    p.dma("Y", g0s, scr["g0"].rearrange("(c p) o -> p c o", p=128))
    p.dma("S", woTs, scr["woT0"].rearrange("(c p) o -> p c o", p=128))

    for g in range(4):
        for tt in range(4):
            tl = g * 4 + tt
            for cc in range(DC):
                bank = pb[3 + cc // 4]
                p.op("T", "transpose", bank.view(bank.ap[:, (cc % 4) * 128:(cc % 4 + 1) * 128]),
                     xr[:, tl, cc * 128:(cc + 1) * 128], c["id_f"],
                     reads=[xr, c["id_f"]], writes=[bank])
            for cc in range(DC):
                bank = pb[3 + cc // 4]
                p.op("V", "tensor_copy", xt32[:, cc, tt * 128:(tt + 1) * 128],
                     bank.view(bank.ap[:, (cc % 4) * 128:(cc % 4 + 1) * 128]),
                     reads=[bank], writes=[xt32])
        p.op("T", "matmul", pb[2].view(pb[2].ap[0:2, :]), btq, onesf,
             start=True, stop=False, reads=[st, sC], writes=[pb[2]])
        for cc in range(DC):
            p.op("T", "matmul", pb[2].view(pb[2].ap[0:2, :]),
                 c["wtqT"].view(c["wtqT"].ap.rearrange("p (a b) -> p a b", b=2)[:, cc, :]),
                 xt32[:, cc, :], start=False, stop=(cc == DC - 1),
                 reads=[c["wtqT"], xt32], writes=[pb[2]])
        p.op("V", "tensor_copy", qp2.view(qp2.ap[:, g * 512:(g + 1) * 512]),
             pb[2].view(pb[2].ap[0:2, :]), reads=[pb[2]], writes=[sA])

    qtv = c["qt"]
    qt32 = qtv.ap[:, 0:32].rearrange("p (t two) -> p t two", two=2)
    for t in range(NT):
        p.op("T", "transpose", pb[0].view(pb[0].ap[:, 2 * t:2 * t + 2]),
             qp2.view(qp2.ap[:, t * 128:(t + 1) * 128]), c["id_f"][0:2, 0:2],
             reads=[sA, c["id_f"]], writes=[pb[0]])
    p.op("V", "tensor_copy", qtv[:, 0:32], pb[0].view(pb[0].ap[:, 0:32]),
         reads=[pb[0]], writes=[qtv])
    a_t = qtv.view(qt32[:, :, 0])
    b_t = qtv.view(qt32[:, :, 1])
    r_t = qtv[:, 32:48]
    sq2 = st.view(st.ap[:, 16:48].rearrange("p (t two) -> p t two", two=2))
    p.op("V", "tensor_tensor", sq2, qtv.view(qt32), qtv.view(qt32), op=ALU.mult,
         reads=[qtv], writes=[st])
    p.op("V", "tensor_tensor", r_t, sq2.view(sq2.ap[:, :, 0]), sq2.view(sq2.ap[:, :, 1]),
         op=ALU.add, reads=[st], writes=[qtv])
    rstage = ar.at_f32(off_wvT, "rstage", (2048,))
    for t in range(NT):
        p.op("T", "transpose", pb[2].view(pb[2].ap[0:1, 0:128]), r_t[:, t:t + 1],
             c["id_f"], reads=[qtv, c["id_f"]], writes=[pb[2]])
        p.op("V", "tensor_copy", rstage.view(rstage.ap[0:1, t * 128:(t + 1) * 128]),
             pb[2].view(pb[2].ap[0:1, 0:128]), reads=[pb[2]], writes=[rstage])
    for hh in range(2):
        p.dma("Y", sA.view(sA.ap[2:3, hh * 1024:(hh + 1) * 1024]),
              rstage.view(rstage.ap[0:1, hh * 1024:(hh + 1) * 1024]))

    stc = st[:, 48:57]
    tmp16 = st[:, 16:32]
    p.op("V", "tensor_reduce", stc[:, 0:1], a_t, axis=AX.X, op=ALU.add, reads=[qtv], writes=[st])
    p.op("V", "tensor_reduce", stc[:, 1:2], b_t, axis=AX.X, op=ALU.add, reads=[qtv], writes=[st])
    p.op("V", "tensor_reduce", stc[:, 2:3], r_t, axis=AX.X, op=ALU.add, reads=[qtv], writes=[st])
    p.op("V", "tensor_tensor", tmp16, a_t, a_t, op=ALU.mult, reads=[qtv], writes=[st])
    p.op("V", "tensor_reduce", stc[:, 3:4], tmp16, axis=AX.X, op=ALU.add, reads=[st], writes=[st])
    p.op("V", "tensor_tensor", tmp16, b_t, b_t, op=ALU.mult, reads=[qtv], writes=[st])
    p.op("V", "tensor_reduce", stc[:, 4:5], tmp16, axis=AX.X, op=ALU.add, reads=[st], writes=[st])
    p.op("V", "tensor_tensor", tmp16, a_t, b_t, op=ALU.mult, reads=[qtv], writes=[st])
    p.op("V", "tensor_reduce", stc[:, 5:6], tmp16, axis=AX.X, op=ALU.add, reads=[st], writes=[st])
    p.op("V", "tensor_tensor", tmp16, r_t, r_t, op=ALU.mult, reads=[qtv], writes=[st])
    p.op("V", "tensor_reduce", stc[:, 6:7], tmp16, axis=AX.X, op=ALU.add, reads=[st], writes=[st])
    p.op("V", "tensor_tensor", tmp16, r_t, a_t, op=ALU.mult, reads=[qtv], writes=[st])
    p.op("V", "tensor_reduce", stc[:, 7:8], tmp16, axis=AX.X, op=ALU.add, reads=[st], writes=[st])
    p.op("V", "tensor_tensor", tmp16, r_t, b_t, op=ALU.mult, reads=[qtv], writes=[st])
    p.op("V", "tensor_reduce", stc[:, 8:9], tmp16, axis=AX.X, op=ALU.add, reads=[st], writes=[st])
    p.op("T", "matmul", pb[2].view(pb[2].ap[0:1, 0:9]), c["ones128"], stc,
         start=True, stop=True, reads=[c["ones128"], st], writes=[pb[2]])
    Srow = sm[32:33, 0:9]
    p.op("V", "tensor_copy", Srow, pb[2].view(pb[2].ap[0:1, 0:9]), reads=[pb[2]], writes=[sm])
    N = float(TOK)
    M2 = N * N

    def s_(i):
        return sm[32:33, 16 + i:17 + i]

    def S_(i):
        return sm[32:33, i:i + 1]
    for i in range(9):
        p.op("V", "tensor_scalar", S_(i), S_(i), 1.0 / N, None, ALU.mult, reads=[sm], writes=[sm])
    p.op("V", "tensor_tensor", s_(0), S_(0), S_(0), op=ALU.mult, reads=[sm], writes=[sm])
    p.op("V", "tensor_tensor", s_(1), S_(1), S_(1), op=ALU.mult, reads=[sm], writes=[sm])
    p.op("V", "tensor_tensor", s_(2), s_(0), s_(1), op=ALU.add, reads=[sm], writes=[sm])
    p.op("V", "tensor_scalar", s_(3), S_(2), 2.0, None, ALU.mult, reads=[sm], writes=[sm])
    p.op("V", "scalar_tensor_tensor", s_(4), s_(2), -2.0, s_(3), op0=ALU.mult, op1=ALU.add,
         reads=[sm], writes=[sm])
    p.op("V", "tensor_tensor", s_(5), S_(3), S_(3), op=ALU.mult, reads=[sm], writes=[sm])
    p.op("V", "tensor_tensor", s_(6), S_(4), S_(4), op=ALU.mult, reads=[sm], writes=[sm])
    p.op("V", "tensor_tensor", s_(7), S_(5), S_(5), op=ALU.mult, reads=[sm], writes=[sm])
    p.op("V", "tensor_tensor", s_(8), s_(5), s_(6), op=ALU.add, reads=[sm], writes=[sm])
    p.op("V", "scalar_tensor_tensor", s_(8), s_(7), 2.0, s_(8), op0=ALU.mult, op1=ALU.add,
         reads=[sm], writes=[sm])
    p.op("V", "tensor_tensor", s_(9), S_(0), S_(7), op=ALU.mult, reads=[sm], writes=[sm])
    p.op("V", "tensor_tensor", s_(10), S_(1), S_(8), op=ALU.mult, reads=[sm], writes=[sm])
    p.op("V", "tensor_tensor", s_(9), s_(9), s_(10), op=ALU.add, reads=[sm], writes=[sm])
    p.op("V", "tensor_tensor", s_(11), S_(2), S_(2), op=ALU.mult, reads=[sm], writes=[sm])
    p.op("V", "tensor_scalar", s_(12), S_(6), 2.0, None, ALU.mult, reads=[sm], writes=[sm])
    p.op("V", "scalar_tensor_tensor", s_(12), s_(11), 2.0, s_(12), op0=ALU.mult, op1=ALU.add,
         reads=[sm], writes=[sm])
    p.op("V", "scalar_tensor_tensor", s_(12), s_(8), 4.0, s_(12), op0=ALU.mult, op1=ALU.add,
         reads=[sm], writes=[sm])
    p.op("V", "scalar_tensor_tensor", s_(12), s_(9), -8.0, s_(12), op0=ALU.mult, op1=ALU.add,
         reads=[sm], writes=[sm])
    p.op("V", "tensor_tensor", s_(13), s_(4), s_(4), op=ALU.mult, reads=[sm], writes=[sm])
    p.op("V", "tensor_tensor", s_(13), s_(12), s_(13), op=ALU.subtract, reads=[sm], writes=[sm])
    p.op("V", "tensor_scalar", s_(13), s_(13), M2 / (M2 - 1.0), 1e-30, ALU.mult, ALU.max,
         reads=[sm], writes=[sm])
    p.op("S", "activation", s_(14), s_(13), AF.Sqrt, reads=[sm], writes=[sm])
    p.op("V", "tensor_scalar", s_(10), s_(14), 1e-20, None, ALU.max, reads=[sm], writes=[sm])
    p.op("V", "reciprocal", s_(10), s_(10), reads=[sm], writes=[sm])
    p.op("V", "tensor_tensor", s_(10), s_(13), s_(10), op=ALU.mult, reads=[sm], writes=[sm])
    p.op("V", "tensor_tensor", s_(14), s_(14), s_(10), op=ALU.add, reads=[sm], writes=[sm])
    p.op("V", "tensor_scalar", s_(14), s_(14), 0.5, None, ALU.mult, reads=[sm], writes=[sm])
    thr = sm[0:1, 140:141]
    p.op("V", "scalar_tensor_tensor", s_(15), s_(14), 1.25, s_(4),
         op0=ALU.mult, op1=ALU.add, reads=[sm], writes=[sm])
    p.op("V", "tensor_copy", thr, s_(15), reads=[sm], writes=[sm])

    p.op("V", "tensor_scalar", tmp16, r_t, 1e-24, None, ALU.max, reads=[qtv], writes=[st])
    p.op("S", "activation", tmp16, tmp16, AF.Ln, reads=[st], writes=[st])
    p.op("S", "activation", tmp16, tmp16, AF.Exp, scale=c["consts"][:, 1:2],
         reads=[st, c["consts"]], writes=[st])
    p.op("V", "tensor_scalar", c["s01"], tmp16, 0.1, None, ALU.mult,
         reads=[st], writes=[c["s01"]])
    p.op("T", "matmul", pb[2].view(pb[2].ap[:, 0:1]), sC.view(sC.ap[0:1, 1024:1152]),
         sm[0:1, 140:141], start=True, stop=True, reads=[sC, sm], writes=[pb[2]])
    p.op("V", "tensor_copy", st[:, 58:59], pb[2].view(pb[2].ap[:, 0:1]),
         reads=[pb[2]], writes=[st])

    emit_xT(p, c, xT)
    p.dma("Y", wvTs, scr["wvT0"].rearrange("(c p) o -> p c o", p=128),
          writes=[xt32, rstage])

    for t in range(NT):
        for h in range(2):
            bank = pb[5 + h]
            for cc in range(DC):
                p.op("T", "matmul", bank, xT[:, cc, t * 128:(t + 1) * 128],
                     wvTs[:, cc, h * 512:(h + 1) * 512],
                     start=(cc == 0), stop=(cc == DC - 1),
                     reads=[xT, wvTs], writes=[bank])
            p.op("S", "copy", v_sb[:, t, h * 512:(h + 1) * 512], bank,
                 reads=[bank], writes=[v_sb])

    for g in range(4):
        for cc in range(DC):
            p.op("T", "matmul", pb[2].view(pb[2].ap[0:1, :]),
                 c["w1_ppb"][:, cc:cc + 1], xT[:, cc, g * 512:(g + 1) * 512],
                 start=(cc == 0), stop=(cc == DC - 1),
                 reads=[c["w1_ppb"], xT], writes=[pb[2]])
        p.op("V", "tensor_copy", r1b.view(r1b.ap[:, g * 512:(g + 1) * 512]),
             pb[2].view(pb[2].ap[0:1, :]), reads=[pb[2]], writes=[rbf])

    p.barrier()

    o = off_wvT
    PT = ar.at_bf16(o, "PT", (NT, 128)); o += 2048
    q1 = ar.at_bf16(o, "q1", (DC, 128)); o += 1024
    ctx_sb = ar.at_bf16(o, "ctx_sb", (1024,)); o += 1024
    ctxT = ar.at_bf16(o, "ctxT", (DC, 128)); o += 1024
    Pp = [ar.at_bf16(o, "P0", (512,)), ar.at_bf16(o + 512, "P1", (512,))]; o += 1024
    t_sb = ar.at_f32(o, "t_sb", (512,)); o += 1024
    e_sb = ar.at_f32(o, "e_sb", (512,)); o += 1024
    assert o <= off_wvT + 8192

    qpm2t = st.view(st.ap[0:2, 64:192])
    qpm3t = st.view(st.ap[0:3, 64:192])
    r_t16 = c["qt"][:, 32:48]
    p.op("V", "memset", qpm3t, 1.0, writes=[st])

    for qi in range(L0A_NT):
        if L0A_BAR >= 1:
            p.barrier()
        ng = ceil_div(128 * (qi + 1), 512)
        par = qi % 2
        p.op("V", "tensor_scalar", qpm2t, qp2.view(qp2.ap[:, qi * 128:(qi + 1) * 128]),
             -2.0, None, ALU.mult, reads=[sA], writes=[st])
        for d2c in range(DC):
            bank = pb[3 + d2c // 4]
            for d1c in range(DC):
                p.op("T", "matmul", bank.view(bank.ap[:, (d2c % 4) * 128:(d2c % 4 + 1) * 128]),
                     g0s[:, d1c, d2c * 128:(d2c + 1) * 128],
                     xT[:, d1c, qi * 128:(qi + 1) * 128],
                     start=(d2c % 4 == 0 and d1c == 0),
                     stop=(d2c % 4 == 3 and d1c == DC - 1),
                     reads=[g0s, xT], writes=[bank])
        q1f = q1.view(q1.ap.rearrange("p a b -> p (a b)"))
        p.op("S", "copy", q1f.view(q1f.ap[:, 0:512]), pb[3], reads=[pb[3]], writes=[q1])
        p.op("S", "copy", q1f.view(q1f.ap[:, 512:1024]), pb[4], reads=[pb[4]], writes=[q1])

        for kg in range(ng):
            if L0A_BAR >= 2:
                p.barrier()
            sl = slice(kg * 512, (kg + 1) * 512)
            sbank = pb[kg % 2]
            p.op("T", "matmul", pb[2], qpm3t, qp3.view(qp3.ap[:, sl]),
                 start=True, stop=True, reads=[st, sA], writes=[pb[2]])
            for cc in range(DC):
                p.op("T", "matmul", sbank, q1[:, cc, :], xT[:, cc, sl],
                     start=(cc == 0), stop=False, reads=[q1, xT], writes=[sbank])
            p.op("T", "matmul", sbank, onesb.view(onesb.ap[:, 0:128]),
                 r1b.view(r1b.ap[:, sl]), start=False, stop=True,
                 reads=[rbf], writes=[sbank])
            p.op("V", "tensor_scalar", t_sb, pb[2], r_t16[:, qi:qi + 1].ap, st[:, 58:59].ap,
                 ALU.add, ALU.is_le, reads=[pb[2], c["qt"], st], writes=[t_sb])
            p.op("V", "scalar_tensor_tensor", pb[2], t_sb, 4096.0, sbank,
                 op0=ALU.mult, op1=ALU.add, reads=[t_sb, sbank], writes=[pb[2]])
            p.op("S", "activation", Pp[kg % 2], pb[2], AF.Exp,
                 bias=c["consts"][:, 4:5], scale=c["consts"][:, 2:3],
                 reads=[pb[2], c["consts"]], writes=[Pp[kg % 2]])
            if kg == ng - 1:
                dlt = qi - 4 * kg
                p.op("V", "tensor_tensor", Pp[kg % 2], Pp[kg % 2],
                     c["cmask"][:, dlt, :], op=ALU.mult,
                     reads=[Pp[kg % 2], c["cmask"]], writes=[Pp[kg % 2]])
            acc = st[:, par * 2:par * 2 + 1]
            tmp = st[:, par * 2 + 1:par * 2 + 2]
            dst = acc if kg == 0 else tmp
            p.op("V", "tensor_reduce", dst, Pp[kg % 2], axis=AX.X, op=ALU.add,
                 reads=[Pp[kg % 2]], writes=[st])
            if kg > 0:
                p.op("V", "tensor_tensor", acc, acc, tmp, op=ALU.add, reads=[st], writes=[st])
            nch = min(4, qi + 1 - 4 * kg)
            for j in range(nch):
                p.op("T", "transpose", pstb.view(pstb.ap[:, j * 128:(j + 1) * 128]),
                     Pp[kg % 2].view(Pp[kg % 2].ap[:, j * 128:(j + 1) * 128]), c["id_bf"],
                     reads=[Pp[kg % 2], c["id_bf"]], writes=[pstb])
            for j in range(nch):
                p.op("V", "tensor_copy", PT[:, 4 * kg + j, :],
                     pstb.view(pstb.ap[:, j * 128:(j + 1) * 128]),
                     reads=[pstb], writes=[PT])
        p.op("V", "reciprocal", c["crec"][:, qi:qi + 1], st[:, par * 2:par * 2 + 1],
             reads=[st], writes=[c["crec"]])
        for h in range(2):
            bank = pb[3 + h]
            for b in range(qi + 1):
                p.op("T", "matmul", bank, PT[:, b, :], v_sb[:, b, h * 512:(h + 1) * 512],
                     start=(b == 0), stop=(b == qi), reads=[PT, v_sb], writes=[bank])
        p.op("S", "copy", ctx_sb.view(ctx_sb.ap[:, 0:512]), pb[3],
             reads=[pb[3]], writes=[ctx_sb])
        p.op("S", "copy", ctx_sb.view(ctx_sb.ap[:, 512:1024]), pb[4],
             reads=[pb[4]], writes=[ctx_sb])
        for h in range(2):
            p.op("T", "matmul", pb[3 + h], qp2.view(qp2.ap[:, qi * 128:(qi + 1) * 128]),
                 wfqT.view(wfqT.ap[:, h * 512:(h + 1) * 512]), start=True, stop=True,
                 reads=[sA, sC], writes=[pb[3 + h]])
        def dvc(cc):
            return ctxT.view(ctxT.ap[:, cc, :])
        pe_transpose_sb(p, c, ctx_sb, dvc, DC)
        for h in range(2):
            bank = pb[5 + h]
            for cc in range(DC):
                p.op("T", "matmul", bank, ctxT[:, cc, :], woTs[:, cc, h * 512:(h + 1) * 512],
                     start=(cc == 0), stop=(cc == DC - 1),
                     reads=[ctxT, woTs], writes=[bank])
        for h in range(2):
            hsl = slice(h * 512, (h + 1) * 512)
            p.op("V", "scalar_tensor_tensor", e_sb, pb[3 + h], c["s01"][:, qi:qi + 1],
                 c["bc_bc"][:, hsl], op0=ALU.mult, op1=ALU.add,
                 reads=[pb[3 + h], c["s01"], c["bc_bc"]], writes=[e_sb])
            p.op("V", "scalar_tensor_tensor", pb[2], pb[5 + h], c["crec"][:, qi:qi + 1],
                 e_sb, op0=ALU.mult, op1=ALU.add,
                 reads=[pb[5 + h], c["crec"], e_sb], writes=[pb[2]])
            xsl = xr[:, qi, hsl]
            p.op("V", "scalar_tensor_tensor", xsl, xsl, 2.0, pb[2],
                 op0=ALU.mult, op1=ALU.add, reads=[xr, pb[2]], writes=[xr])
        emit_ln2(p, c, xr[:, qi, :], pb[5], pb[6], par)



def emit_l0_ffn(p, c):
    ar = c["ar"]
    w = c["w"]
    pb = c["pb"]
    xr = c["xr"]

    xT = ar.bf16("xT", (DC, TOK))
    u1 = ar.bf16("u1", (DC, 256))
    v1 = ar.bf16("v1", (2, 4096))
    u2 = ar.bf16("u2", (32, 256))
    v2 = ar.bf16("v2", (2, 1024))
    b2r = ar.f32("b2r", (1024,))
    b2_row = b2r[0:1, 0:1024]
    ones1 = c["stripeC"].view(c["stripeC"].ap[0:1, 1024:1152])
    PT1 = ar.bf16("PT1", (2, 512))
    H = ar.bf16("H", (32, 512))
    QT = ar.bf16("QT", (2, 512))

    p.dma("P", u1, w["L0_ffn_U1"].rearrange("(c p) r -> p c r", p=128))
    for rh in range(2):
        p.dma("P", v1[:, rh, :], w["L0_ffn_V1"][rh * 128:(rh + 1) * 128, :])
    u2v = w["L0_ffn_U2"].rearrange("(c p) r -> p c r", p=128)
    for gg in range(4):
        p.dma("P", u2[:, gg * 8:(gg + 1) * 8, :], u2v[:, gg * 8:(gg + 1) * 8, :])
    for rh in range(2):
        p.dma("P", v2[:, rh, :], w["L0_ffn_V2"][rh * 128:(rh + 1) * 128, :])
    p.dma("Y", b2_row, w["L0_ffn_b2"].unsqueeze(0))
    emit_xT(p, c, xT)

    for g in range(4):
        gsl = slice(g * 512, (g + 1) * 512)
        for rh in range(2):
            pw = pb[4 + rh]
            for cc in range(DC):
                p.op("T", "matmul", pw, u1[:, cc, rh * 128:(rh + 1) * 128],
                     xT[:, cc, gsl], start=(cc == 0), stop=(cc == DC - 1),
                     reads=[u1, xT], writes=[pw])
            p.op("S", "copy", PT1[:, rh, :], pw, reads=[pw], writes=[PT1])
        for fc in range(32):
            pw = pb[fc % 4]
            for rh in range(2):
                p.op("T", "matmul", pw, v1[:, rh, fc * 128:(fc + 1) * 128],
                     PT1[:, rh, :], start=(rh == 0), stop=(rh == 1),
                     reads=[v1, PT1], writes=[pw])
            p.op("S", "activation", H[:, fc, :], pw, AF.Gelu_apprx_tanh,
                 bias=c["b1_pp"][:, fc:fc + 1], reads=[pw, c["b1_pp"]], writes=[H])
        for rh in range(2):
            pw = pb[4 + rh]
            for fc in range(32):
                p.op("T", "matmul", pw, u2[:, fc, rh * 128:(rh + 1) * 128],
                     H[:, fc, :], start=(fc == 0), stop=(fc == 31),
                     reads=[u2, H], writes=[pw])
            p.op("S", "copy", QT[:, rh, :], pw, reads=[pw], writes=[QT])
        for tt in range(4):
            t = g * 4 + tt
            for h in range(2):
                pw = pb[(tt % 2) * 2 + h]
                p.op("T", "matmul", pw, ones1, b2_row.view(b2_row.ap[:, h * 512:(h + 1) * 512]),
                     start=True, stop=False, reads=[b2r, c["stripeC"]], writes=[pw])
                for rh in range(2):
                    p.op("T", "matmul", pw, QT[:, rh, tt * 128:(tt + 1) * 128],
                         v2[:, rh, h * 512:(h + 1) * 512], start=False, stop=(rh == 1),
                         reads=[QT, v2], writes=[pw])
                xsl = xr[:, t, h * 512:(h + 1) * 512]
                p.op("V", "tensor_tensor", xsl, xsl, pw, op=ALU.add,
                     reads=[xr, pw], writes=[xr])
            emit_ln2(p, c, xr[:, t, :], pb[4], pb[5], t % 2)



def emit_l1_attn(p, c):
    ar = c["ar"]
    scr = c["scr"]
    pb = c["pb"]
    pstb = c["pstb"]
    xr = c["xr"]
    st = c["stat"]

    xT = ar.bf16("xT", (DC, TOK))
    v_sb = ar.bf16("v_sb", (NT, D))
    g1s = ar.bf16("g1s", (DC, 1024))
    off_wvT = (ar.mark() + 1) & ~1
    wvTs = ar.bf16("wvT1s", (DC, 1024))

    p.dma("Y", g1s, scr["g1"].rearrange("(c p) o -> p c o", p=128))
    p.dma("S", wvTs, scr["wvT1"].rearrange("(c p) o -> p c o", p=128))

    emit_xT(p, c, xT)
    for t in range(NT):
        for h in range(2):
            bank = pb[5 + h]
            for cc in range(DC):
                p.op("T", "matmul", bank, xT[:, cc, t * 128:(t + 1) * 128],
                     wvTs[:, cc, h * 512:(h + 1) * 512],
                     start=(cc == 0), stop=(cc == DC - 1),
                     reads=[xT, wvTs], writes=[bank])
            p.op("S", "copy", v_sb[:, t, h * 512:(h + 1) * 512], bank,
                 reads=[bank], writes=[v_sb])

    p.barrier()
    o = off_wvT
    PT = ar.at_bf16(o, "PT1a", (NT, 128)); o += 2048
    q1 = ar.at_bf16(o, "q1a", (DC, 128)); o += 1024
    Pp = [ar.at_bf16(o, "P0a", (512,)), ar.at_bf16(o + 512, "P1a", (512,))]; o += 1024
    e_sb = ar.at_f32(o, "e1a", (512,)); o += 1024
    assert o <= off_wvT + 8192

    for qi in range(NT):
        par = qi % 2
        for d2c in range(DC):
            bank = pb[3 + d2c // 4]
            for d1c in range(DC):
                p.op("T", "matmul", bank.view(bank.ap[:, (d2c % 4) * 128:(d2c % 4 + 1) * 128]),
                     g1s[:, d1c, d2c * 128:(d2c + 1) * 128],
                     xT[:, d1c, qi * 128:(qi + 1) * 128],
                     start=(d2c % 4 == 0 and d1c == 0),
                     stop=(d2c % 4 == 3 and d1c == DC - 1),
                     reads=[g1s, xT], writes=[bank])
        q1f = q1.view(q1.ap.rearrange("p a b -> p (a b)"))
        p.op("S", "copy", q1f.view(q1f.ap[:, 0:512]), pb[3], reads=[pb[3]], writes=[q1])
        p.op("S", "copy", q1f.view(q1f.ap[:, 512:1024]), pb[4], reads=[pb[4]], writes=[q1])
        for kg in range(4):
            sl = slice(kg * 512, (kg + 1) * 512)
            sbank = pb[kg % 2]
            for cc in range(DC):
                p.op("T", "matmul", sbank, q1[:, cc, :], xT[:, cc, sl],
                     start=(cc == 0), stop=(cc == DC - 1), reads=[q1, xT], writes=[sbank])
            p.op("S", "activation", Pp[kg % 2], sbank, AF.Exp,
                 bias=c["consts"][:, 5:6], scale=c["consts"][:, 2:3],
                 reads=[sbank, c["consts"]], writes=[Pp[kg % 2]])
            acc = st[:, par * 2:par * 2 + 1]
            tmp = st[:, par * 2 + 1:par * 2 + 2]
            dst = acc if kg == 0 else tmp
            p.op("V", "tensor_reduce", dst, Pp[kg % 2], axis=AX.X, op=ALU.add,
                 reads=[Pp[kg % 2]], writes=[st])
            if kg > 0:
                p.op("V", "tensor_tensor", acc, acc, tmp, op=ALU.add, reads=[st], writes=[st])
            for j in range(4):
                p.op("T", "transpose", pstb.view(pstb.ap[:, j * 128:(j + 1) * 128]),
                     Pp[kg % 2].view(Pp[kg % 2].ap[:, j * 128:(j + 1) * 128]), c["id_bf"],
                     reads=[Pp[kg % 2], c["id_bf"]], writes=[pstb])
            for j in range(4):
                p.op("V", "tensor_copy", PT[:, 4 * kg + j, :],
                     pstb.view(pstb.ap[:, j * 128:(j + 1) * 128]),
                     reads=[pstb], writes=[PT])
        p.op("V", "reciprocal", c["crec"][:, qi:qi + 1], st[:, par * 2:par * 2 + 1],
             reads=[st], writes=[c["crec"]])
        for h in range(2):
            bank = pb[5 + h]
            for b in range(NT):
                p.op("T", "matmul", bank, PT[:, b, :], v_sb[:, b, h * 512:(h + 1) * 512],
                     start=(b == 0), stop=(b == NT - 1), reads=[PT, v_sb], writes=[bank])
            p.op("V", "tensor_scalar", e_sb, bank, c["crec"][:, qi:qi + 1], None,
                 ALU.mult, reads=[bank, c["crec"]], writes=[e_sb])
            xsl = xr[:, qi, h * 512:(h + 1) * 512]
            p.op("V", "tensor_tensor", xsl, xsl, e_sb, op=ALU.add, reads=[xr, e_sb], writes=[xr])
        emit_ln2(p, c, xr[:, qi, :], pb[5], pb[6], par)



def emit_l1_ffn(p, c):
    ar = c["ar"]
    scr = c["scr"]
    pb = c["pb"]
    xr = c["xr"]
    NI = 2816
    NCH = [512, 512, 512, 512, 512, 256]

    xqT = ar.bf16("xqT", (DC, TOK))
    hqT = ar.bf16("hqT", (4, 22, 128))
    wg = ar.bf16("wg", (DC, 512))
    wu = ar.bf16("wu", (DC, 512))
    wdq = [ar.bf16("wd0", (22, 256)), ar.bf16("wd1", (22, 256))]
    qtmp = ar.f32("qtmp", (3, 512))
    scq = ar.f32("scq", (32,))
    rcq = ar.f32("rcq", (32,))
    xq_t = ar.bf16("xq_t", (1024,))
    h_sb = [ar.f32("h0", (512,)), ar.f32("h1", (512,))]
    hq_c = [ar.bf16("hq0", (512,)), ar.bf16("hq1", (512,))]

    gview = scr["wgT"].rearrange("(c p) n -> p c n", p=128)
    uview = scr["wuT"].rearrange("(c p) n -> p c n", p=128)
    dview = scr["wdT"].rearrange("(c p) n -> p c n", p=128)

    for t in range(NT):
        for hh in range(2):
            emit_quant(p, c, xr[:, t, hh * 512:(hh + 1) * 512],
                       xq_t[:, hh * 512:(hh + 1) * 512], qtmp, scq, rcq, 512)

        def dv(cc, t=t):
            return xqT.view(xqT.ap[:, cc, t * 128:(t + 1) * 128])
        pe_transpose_sb(p, c, xq_t, dv, DC)

    if L1F_BAR >= 1:
        p.barrier()
    for Q in range(4):
        if L1F_BAR >= 2:
            p.barrier()
        tiles = [Q * 4 + i for i in range(4)]
        for ngi in range(6):
            if L1F_BAR >= 3:
                p.barrier()
            nn = NCH[ngi]
            n0 = ngi * 512
            for cc in range(DC):
                p.dma("Y", wg[:, cc, 0:nn], gview[:, cc, n0:n0 + nn])
                p.dma("P", wu[:, cc, 0:nn], uview[:, cc, n0:n0 + nn])
            for ti, t in enumerate(tiles):
                pg = pb[4] if ti % 2 == 0 else pb[2]
                pu = pb[5] if ti % 2 == 0 else pb[3]
                pgv = pg.view(pg.ap[:, 0:nn])
                puv = pu.view(pu.ap[:, 0:nn])
                for cc in range(DC):
                    p.op("T", "matmul", pgv, xqT[:, cc, t * 128:(t + 1) * 128],
                         wg[:, cc, 0:nn], start=(cc == 0), stop=(cc == DC - 1),
                         reads=[xqT, wg], writes=[pg])
                for cc in range(DC):
                    p.op("T", "matmul", puv, xqT[:, cc, t * 128:(t + 1) * 128],
                         wu[:, cc, 0:nn], start=(cc == 0), stop=(cc == DC - 1),
                         reads=[xqT, wu], writes=[pu])
                par = ti % 2
                p.op("S", "activation", h_sb[par][:, 0:nn], pgv, AF.Silu,
                     reads=[pg], writes=[h_sb[par]])
                p.op("V", "tensor_tensor", h_sb[par][:, 0:nn], h_sb[par][:, 0:nn], puv,
                     op=ALU.mult, reads=[h_sb[par], pu], writes=[h_sb[par]])
                emit_quant(p, c, h_sb[par][:, 0:nn], hq_c[par][:, 0:nn],
                           qtmp, scq, rcq, nn)

                def dvh(cc, ti=ti, ngi=ngi, par=par):
                    return hqT.view(hqT.ap[:, ti, ngi * 4 + cc, :])
                pe_transpose_sb(p, c, hq_c[par], dvh, nn // 128)
        p.dma("S", wdq[0], dview[:, :, 0:256])
        for q4 in range(4):
            if q4 < 3:
                p.dma("S", wdq[(q4 + 1) % 2],
                      dview[:, :, (q4 + 1) * 256:(q4 + 2) * 256])
            wd = wdq[q4 % 2]
            for ti, t in enumerate(tiles):
                pw = pb[ti % 2]
                pwv = pw.view(pw.ap[:, 0:256])
                for cc in range(22):
                    p.op("T", "matmul", pwv, hqT[:, ti, cc, :], wd[:, cc, :],
                         start=(cc == 0), stop=(cc == 21), reads=[hqT, wd], writes=[pw])
                xsl = xr[:, t, q4 * 256:(q4 + 1) * 256]
                p.op("V", "tensor_tensor", xsl, xsl, pwv, op=ALU.add,
                     reads=[xr, pw], writes=[xr])
        for ti, t in enumerate(tiles):
            emit_ln2(p, c, xr[:, t, :], pb[4], pb[5], t % 2)



UNFUSED = False
_cache = {}


def _get_prog(blocks):
    key = tuple(blocks)
    if key not in _cache:
        _cache[key] = build_program(list(blocks))
    return _cache[key]


def kernel(**inputs):
    x = np.ascontiguousarray(np.asarray(inputs["L0_attn_hidden_states"], np.float32))
    B, T_, D_ = x.shape
    xc = x.reshape(N_CORES, N_ITEMS * T_, D_)
    stages = [["l0a"], ["l0f"], ["l1a"], ["l1f"]] if UNFUSED else [["l0a", "l0f", "l1a", "l1f"]]
    act = xc
    for blocks in stages:
        nc, names = _get_prog(tuple(blocks))
        in_maps = []
        for core in range(N_CORES):
            m = {"x": np.ascontiguousarray(act[core])}
            for nm in names:
                m[nm] = np.ascontiguousarray(np.asarray(inputs[nm], np.float32))
            in_maps.append(m)
        res = run_bass_kernel_spmd(nc, in_maps, core_ids=list(range(N_CORES)))
        act = np.stack([res.results[i]["y"] for i in range(N_CORES)], axis=0)
    return act.reshape(B, T_, D_)


# version 13
# speedup vs baseline: 1.0014x; 1.0014x over previous
import contextlib
import math

import numpy as np

import concourse.bass as bass
import concourse.mybir as mybir
from concourse.bass_utils import run_bass_kernel_spmd

F32 = mybir.dt.float32
BF16 = mybir.dt.bfloat16
AF = mybir.ActivationFunctionType
ALU = mybir.AluOpType
AX = mybir.AxisListType

MAGIC = 12582912.0
TOK = 2048
D = 1024
NT = TOK // 128
DC = D // 128
N_ITEMS = 4
N_CORES = 8

ENGS = ("T", "V", "S", "P", "Y")
ESEM = {e: "e" + e for e in ENGS}
DSEM = {"Y": "dY", "S": "dS", "P": "dG"}
ALLSEMS = [ESEM[e] for e in ENGS] + list(DSEM.values())

_gid = [0]


class Tok:
    __slots__ = ("sem", "gid", "need", "val", "amt")

    def __init__(self, sem, amt):
        self.sem = sem
        self.amt = amt
        self.need = False
        self.val = None
        _gid[0] += 1
        self.gid = _gid[0]


class Buf:
    __slots__ = ("w", "r", "name")

    def __init__(self, name=""):
        self.w = None
        self.r = {}
        self.name = name


class V:
    __slots__ = ("ap", "buf")

    def __init__(self, ap, buf):
        self.ap = ap
        self.buf = buf

    def __getitem__(self, idx):
        return V(self.ap[idx], self.buf)

    def rearrange(self, *a, **k):
        return V(self.ap.rearrange(*a, **k), self.buf)

    def view(self, ap):
        return V(ap, self.buf)


def carve(v, lo, hi, name):
    return V(v.ap[:, lo:hi], Buf(name))


class P:
    def __init__(self, nc):
        self.nc = nc
        self.es = contextlib.ExitStack()
        self.recs = {e: [] for e in ENGS}
        self.semh = {}
        self.last = {}
        self.qlast = {}

    def sb(self, name, shape, dt):
        h = self.es.enter_context(self.nc.sbuf_tensor(name, list(shape), dt))
        return V(h[:], Buf(name)), h

    def ps(self, name, shape, dt):
        h = self.es.enter_context(self.nc.psum_tensor(name, list(shape), dt))
        return V(h[:], Buf(name)), h

    def _deps(self, eng, reads, writes):
        deps = {}
        for v in reads:
            b = v.buf
            if b.w is not None:
                t = b.w
                if t.gid > deps.get(t.sem, (None, -1))[1]:
                    deps[t.sem] = (t, t.gid)
        for v in writes:
            b = v.buf
            cand = ([b.w] if b.w is not None else []) + list(b.r.values())
            for t in cand:
                if t.gid > deps.get(t.sem, (None, -1))[1]:
                    deps[t.sem] = (t, t.gid)
        out = []
        for sem, (t, _) in deps.items():
            if eng == "T" and sem == "eT":
                continue
            out.append(t)
        return out

    def _mark(self, eng, tok, reads, writes):
        for v in writes:
            v.buf.w = tok
            v.buf.r = {}
        for v in reads:
            prev = v.buf.r.get(tok.sem)
            if prev is None or prev.gid < tok.gid:
                v.buf.r[tok.sem] = tok

    def op(self, eng, meth, *args, reads=(), writes=(), **kw):
        for t in self._deps(eng, reads, writes):
            t.need = True
            self.recs[eng].append(("w", t))
        tok = Tok(ESEM[eng], 1)
        args = tuple(a.ap if isinstance(a, V) else a for a in args)
        kw = {k: (v.ap if isinstance(v, V) else v) for k, v in kw.items()}
        self.recs[eng].append(("op", meth, args, kw, tok))
        self.last[tok.sem] = tok
        self._mark(eng, tok, reads, writes)
        return tok

    def dma(self, eng, out, in_, reads=(), writes=(), meth="dma_start", **kw):
        reads = list(reads)
        writes = list(writes)
        if isinstance(out, V):
            writes.append(out)
        if isinstance(in_, V):
            reads.append(in_)
        for t in self._deps(eng, reads, writes):
            t.need = True
            self.recs[eng].append(("w", t))
        q = DSEM[eng]
        if q in self.qlast:
            self.recs[eng].append(("w", self.qlast[q]))
        tok = Tok(DSEM[eng], 16)
        self.qlast[q] = tok
        o = out.ap if isinstance(out, V) else out
        i = in_.ap if isinstance(in_, V) else in_
        self.recs[eng].append(("op", meth, (), {"out": o, "in_": i, **kw}, tok))
        tok.need = True
        self.last[tok.sem] = tok
        self._mark(eng, tok, reads, writes)
        return tok

    def wait_on(self, eng, toks):
        for t in toks:
            t.need = True
            self.recs[eng].append(("w", t))

    def barrier(self):
        toks = [t for t in self.last.values()]
        for e in ENGS:
            self.wait_on(e, toks)

    def finalize_and_replay(self, block):
        cnt = {s: 0 for s in ALLSEMS}
        for e in ENGS:
            for r in self.recs[e]:
                if r[0] == "op":
                    tok = r[-1]
                    if tok.need:
                        cnt[tok.sem] += tok.amt
                        tok.val = cnt[tok.sem]
        for s in ALLSEMS:
            self.semh[s] = self.es.enter_context(self.nc.semaphore(name="sem_" + s))
        secs = {"T": block.tensor, "V": block.vector, "S": block.scalar,
                "P": block.gpsimd, "Y": block.sync}
        for e in ENGS:
            recs = self.recs[e]
            if not recs:
                continue
            hwm = {}

            def run(eng_h, recs=recs, hwm=hwm):
                for r in recs:
                    if r[0] == "w":
                        t = r[1]
                        if t.val is None:
                            continue
                        if hwm.get(t.sem, 0) >= t.val:
                            continue
                        eng_h.wait_ge(self.semh[t.sem], t.val)
                        hwm[t.sem] = t.val
                    else:
                        _, meth, args, kw, tok = r
                        inst = getattr(eng_h, meth)(*args, **kw)
                        if tok.need:
                            inst.then_inc(self.semh[tok.sem], tok.amt)
            secs[e](run)


class Arena:

    def __init__(self, p, name, kb):
        self.p = p
        self.n = kb * 512
        v, h = p.sb(name, [128, self.n], BF16)
        self.t = h
        self.off = 0

    def _take(self, n_bf16, name):
        off = (self.off + 1) & ~1
        assert off + n_bf16 <= self.n, f"arena overflow at {name}: {off}+{n_bf16}>{self.n}"
        self.off = off + n_bf16
        return off

    @staticmethod
    def _shape(ap, shape):
        if len(shape) == 2:
            return ap.rearrange("p (a b) -> p a b", b=shape[-1])
        if len(shape) == 3:
            return ap.rearrange("p (a b c) -> p a b c", b=shape[1], c=shape[2])
        return ap

    def bf16(self, name, shape):
        n = int(np.prod(shape))
        off = self._take(n, name)
        return V(self._shape(self.t[:, off:off + n], shape), Buf(name))

    def f32(self, name, shape):
        n = int(np.prod(shape))
        off = self._take(2 * n, name)
        return V(self._shape(self.t[:, off:off + 2 * n].bitcast(F32), shape), Buf(name))

    def at_bf16(self, off, name, shape):
        n = int(np.prod(shape))
        assert off + n <= self.n
        return V(self._shape(self.t[:, off:off + n], shape), Buf(name))

    def at_f32(self, off, name, shape):
        n = int(np.prod(shape))
        assert off + 2 * n <= self.n
        return V(self._shape(self.t[:, off:off + 2 * n].bitcast(F32), shape), Buf(name))

    def mark(self):
        return self.off

    def reset(self, to=None):
        self.p.barrier()
        self.off = 0 if to is None else to


def ceil_div(a, b):
    return (a + b - 1) // b



WNAMES = {
    "l0a": ["L0_attn_Wq", "L0_attn_bq", "L0_attn_Wk", "L0_attn_bk", "L0_attn_Wv",
            "L0_attn_bv", "L0_attn_Wo", "L0_attn_bo", "L0_attn_Wtq", "L0_attn_btq",
            "L0_attn_Wfq", "L0_attn_bfq"],
    "l0f": ["L0_ffn_U1", "L0_ffn_V1", "L0_ffn_b1", "L0_ffn_U2", "L0_ffn_V2", "L0_ffn_b2"],
    "l1a": ["L1_attn_Wq", "L1_attn_Wk", "L1_attn_Wv"],
    "l1f": ["L1_ffn_w_gate", "L1_ffn_w_up", "L1_ffn_w_down"],
}

WSHAPES = {"L0_attn_Wq": [D, D], "L0_attn_Wk": [D, D], "L0_attn_Wv": [D, D],
           "L0_attn_Wo": [D, D], "L0_attn_Wtq": [2, D], "L0_attn_Wfq": [D, 2],
           "L0_attn_bq": [D], "L0_attn_bk": [D], "L0_attn_bv": [D],
           "L0_attn_bo": [D], "L0_attn_btq": [2], "L0_attn_bfq": [D],
           "L0_ffn_U1": [D, 256], "L0_ffn_V1": [256, 4096], "L0_ffn_b1": [4096],
           "L0_ffn_U2": [4096, 256], "L0_ffn_V2": [256, D], "L0_ffn_b2": [D],
           "L1_attn_Wq": [D, D], "L1_attn_Wk": [D, D], "L1_attn_Wv": [D, D],
           "L1_ffn_w_gate": [2816, D], "L1_ffn_w_up": [2816, D],
           "L1_ffn_w_down": [D, 2816]}


def build_program(blocks, n_items=N_ITEMS):
    nc = bass.Bass("TRN2", target_bir_lowering=False)
    p = P(nc)
    c = {}

    x_in = nc.dram_tensor("x", [n_items * TOK, D], F32, kind="ExternalInput").ap()
    y_out = nc.dram_tensor("y", [n_items * TOK, D], F32, kind="ExternalOutput").ap()
    names = []
    w = {}
    for b in blocks:
        names.extend(WNAMES[b])
    for nm in names:
        w[nm] = nc.dram_tensor(nm, WSHAPES[nm], F32, kind="ExternalInput").ap()
    c["w"] = w

    scr = {}
    if "l0a" in blocks:
        scr["g0"] = nc.dram_tensor("s_g0", [D, D], BF16, kind="Internal").ap()
        scr["wvT0"] = nc.dram_tensor("s_wvT0", [D, D], BF16, kind="Internal").ap()
        scr["woT0"] = nc.dram_tensor("s_woT0", [D, D], BF16, kind="Internal").ap()
    if "l1a" in blocks:
        scr["g1"] = nc.dram_tensor("s_g1", [D, D], BF16, kind="Internal").ap()
        scr["wvT1"] = nc.dram_tensor("s_wvT1", [D, D], BF16, kind="Internal").ap()
    if "l1f" in blocks:
        scr["wgT"] = nc.dram_tensor("s_wgT", [D, 2816], BF16, kind="Internal").ap()
        scr["wuT"] = nc.dram_tensor("s_wuT", [D, 2816], BF16, kind="Internal").ap()
        scr["wdT"] = nc.dram_tensor("s_wdT", [2816, D], BF16, kind="Internal").ap()
    c["scr"] = scr

    c["xr"], _ = p.sb("xr", [128, NT, D], F32)
    c["id_bf"], _ = p.sb("id_bf", [128, 128], BF16)
    c["id_f"], _ = p.sb("id_f", [128, 128], F32)
    c["stat"], _ = p.sb("stat", [128, 192], F32)
    c["qt"], _ = p.sb("qt", [128, 48], F32)
    c["crec"], _ = p.sb("crec", [128, NT], F32)
    c["s01"], _ = p.sb("s01", [128, NT], F32)
    c["consts"], _ = p.sb("consts", [128, 12], F32)
    c["smalls"], _ = p.sb("smalls", [128, 160], F32)
    c["ones128"], _ = p.sb("ones128", [128, 1], F32)
    c["b1_pp"], _ = p.sb("b1_pp", [128, 32], F32)
    c["wtqT"], _ = p.sb("wtqT", [128, 16], F32)
    c["wfq_pp"], _ = p.sb("wfq_pp", [128, 16], F32)
    c["bq_pp"], _ = p.sb("bq_pp", [128, 8], F32)
    c["w1_ppb"], _ = p.sb("w1_ppb", [128, 8], BF16)
    c["bv_ppb"], _ = p.sb("bv_ppb", [128, 8], BF16)
    c["stripeA"], _ = p.sb("stripeA", [128, 2048], F32)
    c["stripeC"], _ = p.sb("stripeC", [128, 1536], F32)
    c["rows_bf"], _ = p.sb("rows_bf", [128, 2176], BF16)
    c["bc_bc"], _ = p.sb("bc_bc", [128, 1024], F32)
    c["cmask"], _ = p.sb("cmask", [128, 4, 512], BF16)

    c["pb"] = []
    for i in range(7):
        v, _ = p.ps("pb%d" % i, [128, 512], F32)
        c["pb"].append(v)
    c["pstb"], _ = p.ps("pstb", [128, 1024], BF16)

    ar = Arena(p, "arena", 114)
    c["ar"] = ar
    c["p"] = p
    c["nc"] = nc

    emit_prep(p, c, blocks)
    ar.reset()

    for item in range(n_items):
        xs = x_in[item * TOK:(item + 1) * TOK, :].rearrange("(t p) d -> p t d", p=128)
        p.dma("Y", c["xr"], xs)
        for b in blocks:
            if b == "l0a":
                emit_l0_attn(p, c)
            elif b == "l0f":
                emit_l0_ffn(p, c)
            elif b == "l1a":
                emit_l1_attn(p, c)
            elif b == "l1f":
                emit_l1_ffn(p, c)
            ar.reset()
        ys = y_out[item * TOK:(item + 1) * TOK, :].rearrange("(t p) d -> p t d", p=128)
        p.dma("Y", ys, c["xr"], writes=())
        ar.reset()

    if DBG_STATICS:
        aap = nc.dram_tensor("dbg_arena", [128, ar.n], BF16, kind="ExternalOutput").ap()
        p.dma("Y", aap, V(ar.t[:], Buf("arena_dump")), writes=())
        for nm in ("stat", "smalls", "stripeA", "crec", "s01", "qt", "rows_bf", "bc_bc"):
            v = c[nm]
            shp = list(v.ap.shape)
            dt = v.ap.dtype
            dap = nc.dram_tensor("dbg_" + nm, shp, dt, kind="ExternalOutput").ap()
            p.dma("Y", dap, v, writes=())
    p.barrier()
    with nc.Block() as block:
        p.finalize_and_replay(block)
    p.es.close()
    return nc, names



def emit_prep(p, c, blocks):
    ar = c["ar"]
    w = c["w"]
    scr = c["scr"]
    pb = c["pb"]
    sA = c["stripeA"]
    sC = c["stripeC"]

    p.op("P", "memset", c["smalls"], 1.0, writes=[c["smalls"]])
    p.op("P", "affine_select", c["id_f"], c["smalls"][:, 0:128], pattern=[[1, 128]],
         compare_op=ALU.is_equal, fill=0.0, base=0, channel_multiplier=-1,
         reads=[c["smalls"]], writes=[c["id_f"]])
    p.op("V", "tensor_copy", c["id_bf"], c["id_f"], reads=[c["id_f"]], writes=[c["id_bf"]])
    p.op("V", "memset", c["ones128"], 1.0, writes=[c["ones128"]])
    for i, val in ((0, 1e-5), (1, -0.5), (2, 1.0 / 32.0), (3, 1.0),
                   (4, -128.0), (5, -32.0), (6, MAGIC), (7, -MAGIC * 0.5),
                   (8, -2.0 * MAGIC), (9, 0.5), (10, 2.0)):
        p.op("V", "memset", c["consts"][:, i:i + 1], val, writes=[c["consts"]])
    p.op("V", "memset", sC.view(sC.ap[0:1, 1024:1536]), 1.0, writes=[sC])
    p.op("V", "memset", c["rows_bf"].view(c["rows_bf"].ap[0:1, 2048:2176]), 1.0,
         writes=[c["rows_bf"]])
    p.op("V", "memset", c["cmask"], 1.0, writes=[c["cmask"]])
    for d in range(4):
        p.op("P", "affine_select", c["cmask"][:, d, :], c["cmask"][:, d, :],
             pattern=[[-1, 512]], compare_op=ALU.is_ge, fill=0.0, base=d * 128,
             channel_multiplier=1, reads=[c["cmask"]], writes=[c["cmask"]])

    stage_f = ar.f32("prep_stage_f", (2816,))
    stage_f2 = ar.f32("prep_stage_f2", (2816,))
    stage_b = [ar.bf16("prep_stage_b0", (2816,)), ar.bf16("prep_stage_b1", (2816,))]
    tstage = [ar.bf16("prep_tstage0", (22, 128)), ar.bf16("prep_tstage1", (22, 128))]
    qsets = []
    for qsi in range(2):
        qsets.append((ar.f32("prep_qb%d" % qsi, (3, 512)),
                      ar.f32("prep_sc%d" % qsi, (32,)),
                      ar.f32("prep_rc%d" % qsi, (32,))))
    qb, sc, rc = qsets[0]
    wqb = ar.bf16("prep_wqb", (DC, 1024))
    wkb = ar.bf16("prep_wkb", (DC, 1024))
    gstage = ar.bf16("prep_gstage", (1024,))

    def wT_prep(src_ap, dst_ap, rows, cols):
        nrc = rows // 128
        ncc = cols // 128
        dview = dst_ap.rearrange("(c p) o -> p c o", p=128)
        for rci in range(nrc):
            par = rci % 2
            st_ = stage_b[par].view(stage_b[par].ap[:, 0:cols])
            p.dma("P", st_, src_ap[rci * 128:(rci + 1) * 128, :])

            def dv(cc, par=par):
                return tstage[par].view(tstage[par].ap[:, cc, :])
            pe_transpose_sb(p, c, st_, dv, ncc)
            for t in range(ncc):
                p.dma("Y" if t % 2 == 0 else "P",
                      dview[:, t, rci * 128:(rci + 1) * 128],
                      tstage[par].view(tstage[par].ap[:, t, :]))

    def load_wqk(nmq, nmk):
        for rc_ in range(DC):
            p.dma("P", wqb[:, rc_, :], w[nmq][rc_ * 128:(rc_ + 1) * 128, :])
            p.dma("P", wkb[:, rc_, :], w[nmk][rc_ * 128:(rc_ + 1) * 128, :])

    def build_g(dst_ap):
        gview = dst_ap.rearrange("(c p) o -> p c o", p=128)
        for d1c in range(DC):
            for h in range(2):
                bank = pb[(d1c % 2) * 2 + h]
                for oc in range(DC):
                    p.op("T", "matmul", bank, wqb[:, oc, d1c * 128:(d1c + 1) * 128],
                         wkb[:, oc, h * 512:(h + 1) * 512],
                         start=(oc == 0), stop=(oc == DC - 1),
                         reads=[wqb, wkb], writes=[bank])
                p.op("V", "tensor_copy", gstage[:, h * 512:(h + 1) * 512], bank,
                     reads=[bank], writes=[gstage])
            p.dma("Y", gview[:, d1c, :], gstage)

    def row_to_pp(src_row, pp_f32, pp_bf):
        for oc in range(DC):
            p.op("T", "transpose", pb[6].view(pb[6].ap[:, oc:oc + 1]),
                 src_row.view(src_row.ap[0:1, oc * 128:(oc + 1) * 128]),
                 c["id_f"][0:1, 0:1], reads=[src_row, c["id_f"]], writes=[pb[6]])
        if pp_f32 is not None:
            p.op("V", "tensor_copy", pp_f32, pb[6].view(pb[6].ap[:, 0:8]),
                 reads=[pb[6]], writes=[pp_f32])
        p.op("V", "tensor_copy", pp_bf, pb[6].view(pb[6].ap[:, 0:8]),
             reads=[pb[6]], writes=[pp_bf])

    if "l0a" in blocks:
        wT_prep(w["L0_attn_Wv"], scr["wvT0"], D, D)
        wT_prep(w["L0_attn_Wo"], scr["woT0"], D, D)
        st_ = stage_f.view(stage_f.ap[0:8, 0:128])
        p.dma("Y", st_, w["L0_attn_bq"].rearrange("(c p) -> c p", p=128))
        p.op("T", "transpose", pb[6].view(pb[6].ap[:, 0:8]), st_, c["id_f"][0:8, 0:8],
             reads=[st_, c["id_f"]], writes=[pb[6]])
        p.op("V", "tensor_copy", c["bq_pp"], pb[6].view(pb[6].ap[:, 0:8]),
             reads=[pb[6]], writes=[c["bq_pp"]])
        p.dma("Y", st_, w["L0_attn_bv"].rearrange("(c p) -> c p", p=128))
        p.op("T", "transpose", pb[6].view(pb[6].ap[:, 0:8]), st_, c["id_f"][0:8, 0:8],
             reads=[st_, c["id_f"]], writes=[pb[6]])
        p.op("V", "tensor_copy", c["bv_ppb"], pb[6].view(pb[6].ap[:, 0:8]),
             reads=[pb[6]], writes=[c["bv_ppb"]])
        load_wqk("L0_attn_Wq", "L0_attn_Wk")
        build_g(scr["g0"])
        bqb = ar.bf16("prep_bqb", (8,))
        p.op("V", "tensor_copy", bqb, c["bq_pp"], reads=[c["bq_pp"]], writes=[bqb])
        for h in range(2):
            bank = pb[4 + h]
            for oc in range(DC):
                p.op("T", "matmul", bank.view(bank.ap[0:1, :]),
                     bqb[:, oc:oc + 1],
                     wkb[:, oc, h * 512:(h + 1) * 512],
                     start=(oc == 0), stop=(oc == DC - 1),
                     reads=[bqb, wkb], writes=[bank])
            p.op("V", "tensor_copy", sA.view(sA.ap[0:1, h * 512:(h + 1) * 512]),
                 bank.view(bank.ap[0:1, :]), reads=[bank], writes=[sA])
        row_to_pp(sA.view(sA.ap[0:1, 0:1024]), None, c["w1_ppb"])
        woTsl = ar.bf16("prep_woTsl", (DC, 1024))
        p.dma("Y", woTsl, scr["woT0"].rearrange("(c p) o -> p c o", p=128),
              writes=[tstage[0], tstage[1]])
        for h in range(2):
            bank = pb[4 + h]
            for oc in range(DC):
                p.op("T", "matmul", bank.view(bank.ap[0:1, :]),
                     c["bv_ppb"][:, oc:oc + 1], woTsl[:, oc, h * 512:(h + 1) * 512],
                     start=(oc == 0), stop=(oc == DC - 1),
                     reads=[c["bv_ppb"], woTsl], writes=[bank])
        p.dma("Y", sA.view(sA.ap[64:65, 0:1024]), w["L0_attn_bfq"].unsqueeze(0))
        p.dma("Y", sA.view(sA.ap[64:65, 1024:2048]), w["L0_attn_bo"].unsqueeze(0))
        p.op("V", "scalar_tensor_tensor", sA.view(sA.ap[0:1, 0:1024]),
             sA.view(sA.ap[64:65, 0:1024]), 0.1, sA.view(sA.ap[64:65, 1024:2048]),
             op0=ALU.mult, op1=ALU.add, reads=[sA], writes=[sA])
        for h in range(2):
            p.op("V", "tensor_tensor", sA.view(sA.ap[0:1, h * 512:(h + 1) * 512]),
                 sA.view(sA.ap[0:1, h * 512:(h + 1) * 512]),
                 pb[4 + h].view(pb[4 + h].ap[0:1, :]), op=ALU.add,
                 reads=[sA, pb[4 + h]], writes=[sA])
        for h in range(2):
            bank = pb[h]
            p.op("T", "matmul", bank, sC.view(sC.ap[0:1, 1024:1152]),
                 sA.view(sA.ap[0:1, h * 512:(h + 1) * 512]), start=True, stop=True,
                 reads=[sC, sA], writes=[bank])
            p.op("V", "tensor_copy", c["bc_bc"][:, h * 512:(h + 1) * 512], bank,
                 reads=[bank], writes=[c["bc_bc"]])
        st2 = stage_f.view(stage_f.ap[0:2, 0:1024])
        p.dma("Y", st2, w["L0_attn_Wtq"])
        for cc in range(DC):
            p.op("T", "transpose", pb[6].view(pb[6].ap[:, 2 * cc:2 * cc + 2]),
                 st2.view(st2.ap[:, cc * 128:(cc + 1) * 128]), c["id_f"][0:2, 0:2],
                 reads=[st2, c["id_f"]], writes=[pb[6]])
        p.op("V", "tensor_copy", c["wtqT"], pb[6].view(pb[6].ap[:, 0:16]),
             reads=[pb[6]], writes=[c["wtqT"]])
        p.dma("Y", c["wfq_pp"].rearrange("p (a b) -> p a b", b=2),
              w["L0_attn_Wfq"].rearrange("(c p) t -> p c t", p=128),
              allow_slow_non_contiguous=True)
        for cc in range(DC):
            p.op("T", "transpose", pb[6].view(pb[6].ap[0:2, 0:128]),
                 c["wfq_pp"].view(c["wfq_pp"].ap.rearrange("p (a b) -> p a b", b=2)[:, cc, :]),
                 c["id_f"], reads=[c["wfq_pp"], c["id_f"]], writes=[pb[6]])
            p.op("V", "tensor_copy", sC.view(sC.ap[0:2, cc * 128:(cc + 1) * 128]),
                 pb[6].view(pb[6].ap[0:2, 0:128]), reads=[pb[6]], writes=[sC])
        p.dma("Y", c["stat"].view(c["stat"].ap[0:1, 60:62]), w["L0_attn_btq"].unsqueeze(0))

    if "l0f" in blocks:
        st_ = stage_f.view(stage_f.ap[0:32, 0:128])
        p.dma("Y", st_, w["L0_ffn_b1"].rearrange("(c p) -> c p", p=128))
        p.op("T", "transpose", pb[6].view(pb[6].ap[:, 0:32]), st_, c["id_f"][0:32, 0:32],
             reads=[st_, c["id_f"]], writes=[pb[6]])
        p.op("V", "tensor_copy", c["b1_pp"], pb[6].view(pb[6].ap[:, 0:32]),
             reads=[pb[6]], writes=[c["b1_pp"]])

    if "l1a" in blocks:
        wT_prep(w["L1_attn_Wv"], scr["wvT1"], D, D)
        load_wqk("L1_attn_Wq", "L1_attn_Wk")
        build_g(scr["g1"])

    if "l1f" in blocks:
        qctr = [0]

        def quant_w(src_ap, dst_ap, rows, cols):
            nrc = rows // 128
            ncc = cols // 128
            dview = dst_ap.rearrange("(c p) o -> p c o", p=128)
            for rci in range(nrc):
                par = rci % 2
                sfb = stage_f if par == 0 else stage_f2
                sf = sfb.view(sfb.ap[:, 0:cols])
                p.dma("Y" if par == 0 else "P", sf, src_ap[rci * 128:(rci + 1) * 128, :])
                sb_ = stage_b[par].view(stage_b[par].ap[:, 0:cols])
                for q0 in range(0, cols, 512):
                    qn = min(512, cols - q0)
                    qb_, sc_, rc_ = qsets[qctr[0] % 2]
                    qctr[0] += 1
                    emit_quant(p, c, sf[:, q0:q0 + qn], sb_[:, q0:q0 + qn],
                               qb_, sc_, rc_, qn)

                def dv(t, par=par):
                    return tstage[par].view(tstage[par].ap[:, t, :])
                pe_transpose_sb(p, c, sb_, dv, ncc)
                for t in range(ncc):
                    p.dma("Y" if t % 2 == 0 else "P",
                          dview[:, t, rci * 128:(rci + 1) * 128],
                          tstage[par].view(tstage[par].ap[:, t, :]))
        quant_w(w["L1_ffn_w_gate"], scr["wgT"], 2816, D)
        quant_w(w["L1_ffn_w_up"], scr["wuT"], 2816, D)
        quant_w(w["L1_ffn_w_down"], scr["wdT"], D, 2816)



def emit_quant(p, c, src, out, qb, sc, rc, F):
    nb = F // 16
    A = qb[:, 0, 0:F]
    B1 = qb[:, 1, 0:F]
    B2 = qb[:, 2, 0:F]
    scv = sc[:, 0:nb]
    rcv = rc[:, 0:nb]
    s3 = src.rearrange("p (b k) -> p b k", k=16)
    bc = lambda vv: vv.view(vv.ap.unsqueeze(2).broadcast_to([128, nb, 16]))
    p.op("V", "tensor_reduce", scv, s3, axis=AX.X, op=ALU.max,
         apply_absolute_value=True, reads=[src], writes=[sc])
    p.op("V", "tensor_scalar", scv, scv, 1.0 / 6.0, 1e-12, ALU.mult, ALU.max,
         reads=[sc], writes=[sc])
    p.op("V", "reciprocal", rcv, scv, reads=[sc], writes=[rc])
    p.op("S", "activation", A, src, AF.Abs, reads=[src], writes=[qb])
    p.op("V", "tensor_tensor", B1.rearrange("p (b k) -> p b k", k=16),
         A.rearrange("p (b k) -> p b k", k=16), bc(rcv), op=ALU.mult,
         reads=[qb, rc], writes=[qb])
    cst = c["consts"]
    p.op("V", "tensor_scalar", B2, B1, 2.0, 2.0, ALU.min, ALU.mult, reads=[qb], writes=[qb])
    p.op("S", "activation", B2, B2, AF.Identity, bias=cst[:, 6:7],
         reads=[qb, cst], writes=[qb])
    p.op("S", "activation", B2, B2, AF.Identity, scale=cst[:, 9:10], bias=cst[:, 7:8],
         reads=[qb, cst], writes=[qb])
    p.op("V", "tensor_scalar", A, B1, 2.0, 0.0, ALU.subtract, ALU.max, reads=[qb], writes=[qb])
    p.op("S", "activation", A, A, AF.Identity, bias=cst[:, 6:7],
         reads=[qb, cst], writes=[qb])
    p.op("V", "tensor_scalar", A, A, MAGIC, 2.0, ALU.subtract, ALU.min, reads=[qb], writes=[qb])
    p.op("V", "tensor_tensor", B2, B2, A, op=ALU.add, reads=[qb], writes=[qb])
    p.op("V", "tensor_scalar", A, B1, 4.5, 0.0, ALU.subtract, ALU.max, reads=[qb], writes=[qb])
    p.op("V", "tensor_scalar", A, A, 1.2, MAGIC, ALU.min, ALU.add, reads=[qb], writes=[qb])
    p.op("S", "activation", A, A, AF.Identity, scale=cst[:, 10:11], bias=cst[:, 8:9],
         reads=[qb, cst], writes=[qb])
    p.op("V", "tensor_tensor", B2, B2, A, op=ALU.add, reads=[qb], writes=[qb])
    p.op("S", "activation", A, src, AF.Sign, reads=[src], writes=[qb])
    p.op("V", "tensor_tensor", B1.rearrange("p (b k) -> p b k", k=16),
         A.rearrange("p (b k) -> p b k", k=16), bc(scv), op=ALU.mult,
         reads=[qb, sc], writes=[qb])
    p.op("V", "tensor_tensor", out, B2, B1, op=ALU.mult, reads=[qb], writes=[out])


def pe_transpose_sb(p, c, src_v, dst_view1, ncc, par=0):
    pstb = c["pstb"]
    for base in range(0, ncc, 8):
        nb = min(8, ncc - base)
        for b in range(nb):
            p.op("T", "transpose", pstb.view(pstb.ap[:, b * 128:(b + 1) * 128]),
                 src_v.view(src_v.ap[:, (base + b) * 128:(base + b + 1) * 128]), c["id_bf"],
                 reads=[src_v, c["id_bf"]], writes=[pstb])
        for b in range(nb):
            p.op("S", "copy", dst_view1(base + b),
                 pstb.view(pstb.ap[:, b * 128:(b + 1) * 128]),
                 reads=[pstb], writes=[dst_view1(base + b)])


def emit_xT(p, c, xT, src=None, bf_src=None):
    ar = c["ar"]
    mk = ar.mark()
    xbt = ar.bf16("xbt", (1024,))
    src = src if src is not None else c["xr"]
    for t in range(NT):
        if bf_src is None:
            s = xbt
            p.op("V", "tensor_copy", s, src[:, t, :], reads=[src], writes=[xbt])
        else:
            s = bf_src[:, t, :]

        def dv(cc, t=t):
            return xT.view(xT.ap[:, cc, t * 128:(t + 1) * 128])
        pe_transpose_sb(p, c, s, dv, DC)
    ar.off = mk


def emit_ln2(p, c, xsl, bA, bB, par):
    st = c["stat"]
    base = 16 + par * 8
    mu = st[:, base:base + 1]
    msq = st[:, base + 1:base + 2]
    t1 = st[:, base + 2:base + 3]
    var = st[:, base + 3:base + 4]
    rstd = st[:, base + 4:base + 5]
    nb = st[:, base + 5:base + 6]
    t2 = st[:, base + 6:base + 7]
    m2 = st[:, base + 7:base + 8]
    x0 = xsl.view(xsl.ap[:, 0:512])
    x1 = xsl.view(xsl.ap[:, 512:1024])
    p.op("S", "activation", bA, x0, AF.Square, accum_out=msq.ap,
         reads=[xsl], writes=[bA, st])
    p.op("S", "activation", bB, x1, AF.Square, accum_out=t2.ap,
         reads=[xsl], writes=[bB, st])
    p.op("S", "activation", bA, x0, AF.Identity, accum_out=mu.ap,
         reads=[xsl], writes=[bA, st])
    p.op("S", "activation", bB, x1, AF.Identity, accum_out=m2.ap,
         reads=[xsl], writes=[bB, st])
    p.op("V", "tensor_tensor", mu, mu, m2, op=ALU.add, reads=[st], writes=[st])
    p.op("V", "tensor_scalar", mu, mu, 1.0 / D, None, ALU.mult, reads=[st], writes=[st])
    p.op("V", "tensor_tensor", msq, msq, t2, op=ALU.add, reads=[st], writes=[st])
    p.op("V", "tensor_scalar", msq, msq, 1.0 / D, None, ALU.mult, reads=[st], writes=[st])
    p.op("V", "tensor_tensor", t1, mu, mu, op=ALU.mult, reads=[st], writes=[st])
    p.op("V", "tensor_tensor", var, msq, t1, op=ALU.subtract, reads=[st], writes=[st])
    p.op("S", "activation", t1, var, AF.Ln, bias=c["consts"][:, 0:1],
         reads=[st, c["consts"]], writes=[st])
    p.op("S", "activation", rstd, t1, AF.Exp, scale=c["consts"][:, 1:2],
         reads=[st, c["consts"]], writes=[st])
    p.op("V", "tensor_tensor", t1, mu, rstd, op=ALU.mult, reads=[st], writes=[st])
    p.op("V", "tensor_scalar", nb, t1, -1.0, None, ALU.mult, reads=[st], writes=[st])
    p.op("V", "tensor_scalar", xsl, xsl, rstd, nb, ALU.mult, ALU.add,
         reads=[xsl, st], writes=[xsl])



L0A_NT = NT
L0A_BAR = 0
L1F_BAR = 0
DBG_STATICS = False


def emit_l0_attn(p, c):
    ar = c["ar"]
    w = c["w"]
    scr = c["scr"]
    pb = c["pb"]
    pstb = c["pstb"]
    xr = c["xr"]
    st = c["stat"]
    sm = c["smalls"]
    sA = c["stripeA"]
    sC = c["stripeC"]
    rbf = c["rows_bf"]

    qp2 = sA.view(sA.ap[0:2, 0:2048])
    qp3 = sA.view(sA.ap[0:3, 0:2048])
    onesf = sC.view(sC.ap[0:1, 1024:1536])
    r1b = rbf.view(rbf.ap[0:1, 0:2048])
    onesb = rbf.view(rbf.ap[0:1, 2048:2176])
    wfqT = sC.view(sC.ap[0:2, 0:1024])
    btq = st[0:1, 60:62]

    xT = ar.bf16("xT", (DC, TOK))
    v_sb = ar.bf16("v_sb", (NT, D))
    g0s = ar.bf16("g0s", (DC, 1024))
    woTs = ar.bf16("woTs", (DC, 1024))
    off_wvT = (ar.mark() + 1) & ~1
    wvTs = ar.bf16("wvTs", (DC, 1024))
    xt32 = ar.at_f32(off_wvT, "xt32", (DC, 512))

    p.dma("Y", g0s, scr["g0"].rearrange("(c p) o -> p c o", p=128))
    p.dma("S", woTs, scr["woT0"].rearrange("(c p) o -> p c o", p=128))

    for g in range(4):
        for tt in range(4):
            tl = g * 4 + tt
            for cc in range(DC):
                bank = pb[3 + cc // 4]
                p.op("T", "transpose", bank.view(bank.ap[:, (cc % 4) * 128:(cc % 4 + 1) * 128]),
                     xr[:, tl, cc * 128:(cc + 1) * 128], c["id_f"],
                     reads=[xr, c["id_f"]], writes=[bank])
            for cc in range(DC):
                bank = pb[3 + cc // 4]
                p.op("V", "tensor_copy", xt32[:, cc, tt * 128:(tt + 1) * 128],
                     bank.view(bank.ap[:, (cc % 4) * 128:(cc % 4 + 1) * 128]),
                     reads=[bank], writes=[xt32])
        p.op("T", "matmul", pb[2].view(pb[2].ap[0:2, :]), btq, onesf,
             start=True, stop=False, reads=[st, sC], writes=[pb[2]])
        for cc in range(DC):
            p.op("T", "matmul", pb[2].view(pb[2].ap[0:2, :]),
                 c["wtqT"].view(c["wtqT"].ap.rearrange("p (a b) -> p a b", b=2)[:, cc, :]),
                 xt32[:, cc, :], start=False, stop=(cc == DC - 1),
                 reads=[c["wtqT"], xt32], writes=[pb[2]])
        p.op("V", "tensor_copy", qp2.view(qp2.ap[:, g * 512:(g + 1) * 512]),
             pb[2].view(pb[2].ap[0:2, :]), reads=[pb[2]], writes=[sA])

    qtv = c["qt"]
    qt32 = qtv.ap[:, 0:32].rearrange("p (t two) -> p t two", two=2)
    for t in range(NT):
        p.op("T", "transpose", pb[0].view(pb[0].ap[:, 2 * t:2 * t + 2]),
             qp2.view(qp2.ap[:, t * 128:(t + 1) * 128]), c["id_f"][0:2, 0:2],
             reads=[sA, c["id_f"]], writes=[pb[0]])
    p.op("V", "tensor_copy", qtv[:, 0:32], pb[0].view(pb[0].ap[:, 0:32]),
         reads=[pb[0]], writes=[qtv])
    a_t = qtv.view(qt32[:, :, 0])
    b_t = qtv.view(qt32[:, :, 1])
    r_t = qtv[:, 32:48]
    sq2 = st.view(st.ap[:, 16:48].rearrange("p (t two) -> p t two", two=2))
    p.op("V", "tensor_tensor", sq2, qtv.view(qt32), qtv.view(qt32), op=ALU.mult,
         reads=[qtv], writes=[st])
    p.op("V", "tensor_tensor", r_t, sq2.view(sq2.ap[:, :, 0]), sq2.view(sq2.ap[:, :, 1]),
         op=ALU.add, reads=[st], writes=[qtv])
    rstage = ar.at_f32(off_wvT, "rstage", (2048,))
    for t in range(NT):
        p.op("T", "transpose", pb[2].view(pb[2].ap[0:1, 0:128]), r_t[:, t:t + 1],
             c["id_f"], reads=[qtv, c["id_f"]], writes=[pb[2]])
        p.op("V", "tensor_copy", rstage.view(rstage.ap[0:1, t * 128:(t + 1) * 128]),
             pb[2].view(pb[2].ap[0:1, 0:128]), reads=[pb[2]], writes=[rstage])
    for hh in range(2):
        p.dma("Y", sA.view(sA.ap[2:3, hh * 1024:(hh + 1) * 1024]),
              rstage.view(rstage.ap[0:1, hh * 1024:(hh + 1) * 1024]))

    stc = st[:, 48:57]
    tmp16 = st[:, 16:32]
    p.op("V", "tensor_reduce", stc[:, 0:1], a_t, axis=AX.X, op=ALU.add, reads=[qtv], writes=[st])
    p.op("V", "tensor_reduce", stc[:, 1:2], b_t, axis=AX.X, op=ALU.add, reads=[qtv], writes=[st])
    p.op("V", "tensor_reduce", stc[:, 2:3], r_t, axis=AX.X, op=ALU.add, reads=[qtv], writes=[st])
    p.op("V", "tensor_tensor", tmp16, a_t, a_t, op=ALU.mult, reads=[qtv], writes=[st])
    p.op("V", "tensor_reduce", stc[:, 3:4], tmp16, axis=AX.X, op=ALU.add, reads=[st], writes=[st])
    p.op("V", "tensor_tensor", tmp16, b_t, b_t, op=ALU.mult, reads=[qtv], writes=[st])
    p.op("V", "tensor_reduce", stc[:, 4:5], tmp16, axis=AX.X, op=ALU.add, reads=[st], writes=[st])
    p.op("V", "tensor_tensor", tmp16, a_t, b_t, op=ALU.mult, reads=[qtv], writes=[st])
    p.op("V", "tensor_reduce", stc[:, 5:6], tmp16, axis=AX.X, op=ALU.add, reads=[st], writes=[st])
    p.op("V", "tensor_tensor", tmp16, r_t, r_t, op=ALU.mult, reads=[qtv], writes=[st])
    p.op("V", "tensor_reduce", stc[:, 6:7], tmp16, axis=AX.X, op=ALU.add, reads=[st], writes=[st])
    p.op("V", "tensor_tensor", tmp16, r_t, a_t, op=ALU.mult, reads=[qtv], writes=[st])
    p.op("V", "tensor_reduce", stc[:, 7:8], tmp16, axis=AX.X, op=ALU.add, reads=[st], writes=[st])
    p.op("V", "tensor_tensor", tmp16, r_t, b_t, op=ALU.mult, reads=[qtv], writes=[st])
    p.op("V", "tensor_reduce", stc[:, 8:9], tmp16, axis=AX.X, op=ALU.add, reads=[st], writes=[st])
    p.op("T", "matmul", pb[2].view(pb[2].ap[0:1, 0:9]), c["ones128"], stc,
         start=True, stop=True, reads=[c["ones128"], st], writes=[pb[2]])
    Srow = sm[32:33, 0:9]
    p.op("V", "tensor_copy", Srow, pb[2].view(pb[2].ap[0:1, 0:9]), reads=[pb[2]], writes=[sm])
    N = float(TOK)
    M2 = N * N

    def s_(i):
        return sm[32:33, 16 + i:17 + i]

    def S_(i):
        return sm[32:33, i:i + 1]
    for i in range(9):
        p.op("V", "tensor_scalar", S_(i), S_(i), 1.0 / N, None, ALU.mult, reads=[sm], writes=[sm])
    p.op("V", "tensor_tensor", s_(0), S_(0), S_(0), op=ALU.mult, reads=[sm], writes=[sm])
    p.op("V", "tensor_tensor", s_(1), S_(1), S_(1), op=ALU.mult, reads=[sm], writes=[sm])
    p.op("V", "tensor_tensor", s_(2), s_(0), s_(1), op=ALU.add, reads=[sm], writes=[sm])
    p.op("V", "tensor_scalar", s_(3), S_(2), 2.0, None, ALU.mult, reads=[sm], writes=[sm])
    p.op("V", "scalar_tensor_tensor", s_(4), s_(2), -2.0, s_(3), op0=ALU.mult, op1=ALU.add,
         reads=[sm], writes=[sm])
    p.op("V", "tensor_tensor", s_(5), S_(3), S_(3), op=ALU.mult, reads=[sm], writes=[sm])
    p.op("V", "tensor_tensor", s_(6), S_(4), S_(4), op=ALU.mult, reads=[sm], writes=[sm])
    p.op("V", "tensor_tensor", s_(7), S_(5), S_(5), op=ALU.mult, reads=[sm], writes=[sm])
    p.op("V", "tensor_tensor", s_(8), s_(5), s_(6), op=ALU.add, reads=[sm], writes=[sm])
    p.op("V", "scalar_tensor_tensor", s_(8), s_(7), 2.0, s_(8), op0=ALU.mult, op1=ALU.add,
         reads=[sm], writes=[sm])
    p.op("V", "tensor_tensor", s_(9), S_(0), S_(7), op=ALU.mult, reads=[sm], writes=[sm])
    p.op("V", "tensor_tensor", s_(10), S_(1), S_(8), op=ALU.mult, reads=[sm], writes=[sm])
    p.op("V", "tensor_tensor", s_(9), s_(9), s_(10), op=ALU.add, reads=[sm], writes=[sm])
    p.op("V", "tensor_tensor", s_(11), S_(2), S_(2), op=ALU.mult, reads=[sm], writes=[sm])
    p.op("V", "tensor_scalar", s_(12), S_(6), 2.0, None, ALU.mult, reads=[sm], writes=[sm])
    p.op("V", "scalar_tensor_tensor", s_(12), s_(11), 2.0, s_(12), op0=ALU.mult, op1=ALU.add,
         reads=[sm], writes=[sm])
    p.op("V", "scalar_tensor_tensor", s_(12), s_(8), 4.0, s_(12), op0=ALU.mult, op1=ALU.add,
         reads=[sm], writes=[sm])
    p.op("V", "scalar_tensor_tensor", s_(12), s_(9), -8.0, s_(12), op0=ALU.mult, op1=ALU.add,
         reads=[sm], writes=[sm])
    p.op("V", "tensor_tensor", s_(13), s_(4), s_(4), op=ALU.mult, reads=[sm], writes=[sm])
    p.op("V", "tensor_tensor", s_(13), s_(12), s_(13), op=ALU.subtract, reads=[sm], writes=[sm])
    p.op("V", "tensor_scalar", s_(13), s_(13), M2 / (M2 - 1.0), 1e-30, ALU.mult, ALU.max,
         reads=[sm], writes=[sm])
    p.op("S", "activation", s_(14), s_(13), AF.Sqrt, reads=[sm], writes=[sm])
    p.op("V", "tensor_scalar", s_(10), s_(14), 1e-20, None, ALU.max, reads=[sm], writes=[sm])
    p.op("V", "reciprocal", s_(10), s_(10), reads=[sm], writes=[sm])
    p.op("V", "tensor_tensor", s_(10), s_(13), s_(10), op=ALU.mult, reads=[sm], writes=[sm])
    p.op("V", "tensor_tensor", s_(14), s_(14), s_(10), op=ALU.add, reads=[sm], writes=[sm])
    p.op("V", "tensor_scalar", s_(14), s_(14), 0.5, None, ALU.mult, reads=[sm], writes=[sm])
    thr = sm[0:1, 140:141]
    p.op("V", "scalar_tensor_tensor", s_(15), s_(14), 1.25, s_(4),
         op0=ALU.mult, op1=ALU.add, reads=[sm], writes=[sm])
    p.op("V", "tensor_copy", thr, s_(15), reads=[sm], writes=[sm])

    p.op("V", "tensor_scalar", tmp16, r_t, 1e-24, None, ALU.max, reads=[qtv], writes=[st])
    p.op("S", "activation", tmp16, tmp16, AF.Ln, reads=[st], writes=[st])
    p.op("S", "activation", tmp16, tmp16, AF.Exp, scale=c["consts"][:, 1:2],
         reads=[st, c["consts"]], writes=[st])
    p.op("V", "tensor_scalar", c["s01"], tmp16, 0.1, None, ALU.mult,
         reads=[st], writes=[c["s01"]])
    p.op("T", "matmul", pb[2].view(pb[2].ap[:, 0:1]), sC.view(sC.ap[0:1, 1024:1152]),
         sm[0:1, 140:141], start=True, stop=True, reads=[sC, sm], writes=[pb[2]])
    p.op("V", "tensor_copy", st[:, 58:59], pb[2].view(pb[2].ap[:, 0:1]),
         reads=[pb[2]], writes=[st])

    emit_xT(p, c, xT)
    p.dma("Y", wvTs, scr["wvT0"].rearrange("(c p) o -> p c o", p=128),
          writes=[xt32, rstage])

    for t in range(NT):
        for h in range(2):
            bank = pb[5 + h]
            for cc in range(DC):
                p.op("T", "matmul", bank, xT[:, cc, t * 128:(t + 1) * 128],
                     wvTs[:, cc, h * 512:(h + 1) * 512],
                     start=(cc == 0), stop=(cc == DC - 1),
                     reads=[xT, wvTs], writes=[bank])
            p.op("S", "copy", v_sb[:, t, h * 512:(h + 1) * 512], bank,
                 reads=[bank], writes=[v_sb])

    for g in range(4):
        for cc in range(DC):
            p.op("T", "matmul", pb[2].view(pb[2].ap[0:1, :]),
                 c["w1_ppb"][:, cc:cc + 1], xT[:, cc, g * 512:(g + 1) * 512],
                 start=(cc == 0), stop=(cc == DC - 1),
                 reads=[c["w1_ppb"], xT], writes=[pb[2]])
        p.op("V", "tensor_copy", r1b.view(r1b.ap[:, g * 512:(g + 1) * 512]),
             pb[2].view(pb[2].ap[0:1, :]), reads=[pb[2]], writes=[rbf])

    p.barrier()

    o = off_wvT
    PT = ar.at_bf16(o, "PT", (NT, 128)); o += 2048
    q1 = ar.at_bf16(o, "q1", (DC, 128)); o += 1024
    ctx_sb = ar.at_bf16(o, "ctx_sb", (1024,)); o += 1024
    ctxT = ar.at_bf16(o, "ctxT", (DC, 128)); o += 1024
    Pp = [ar.at_bf16(o, "P0", (512,)), ar.at_bf16(o + 512, "P1", (512,))]; o += 1024
    t_sb = ar.at_f32(o, "t_sb", (512,)); o += 1024
    e_sb = ar.at_f32(o, "e_sb", (512,)); o += 1024
    assert o <= off_wvT + 8192

    qpm2t = st.view(st.ap[0:2, 64:192])
    qpm3t = st.view(st.ap[0:3, 64:192])
    r_t16 = c["qt"][:, 32:48]
    p.op("V", "memset", qpm3t, 1.0, writes=[st])

    for qi in range(L0A_NT):
        if L0A_BAR >= 1:
            p.barrier()
        ng = ceil_div(128 * (qi + 1), 512)
        par = qi % 2
        p.op("V", "tensor_scalar", qpm2t, qp2.view(qp2.ap[:, qi * 128:(qi + 1) * 128]),
             -2.0, None, ALU.mult, reads=[sA], writes=[st])
        for d2c in range(DC):
            bank = pb[3 + d2c // 4]
            for d1c in range(DC):
                p.op("T", "matmul", bank.view(bank.ap[:, (d2c % 4) * 128:(d2c % 4 + 1) * 128]),
                     g0s[:, d1c, d2c * 128:(d2c + 1) * 128],
                     xT[:, d1c, qi * 128:(qi + 1) * 128],
                     start=(d2c % 4 == 0 and d1c == 0),
                     stop=(d2c % 4 == 3 and d1c == DC - 1),
                     reads=[g0s, xT], writes=[bank])
        q1f = q1.view(q1.ap.rearrange("p a b -> p (a b)"))
        p.op("S", "copy", q1f.view(q1f.ap[:, 0:512]), pb[3], reads=[pb[3]], writes=[q1])
        p.op("S", "copy", q1f.view(q1f.ap[:, 512:1024]), pb[4], reads=[pb[4]], writes=[q1])

        for kg in range(ng):
            if L0A_BAR >= 2:
                p.barrier()
            sl = slice(kg * 512, (kg + 1) * 512)
            sbank = pb[kg % 2]
            p.op("T", "matmul", pb[2], qpm3t, qp3.view(qp3.ap[:, sl]),
                 start=True, stop=True, reads=[st, sA], writes=[pb[2]])
            for cc in range(DC):
                p.op("T", "matmul", sbank, q1[:, cc, :], xT[:, cc, sl],
                     start=(cc == 0), stop=False, reads=[q1, xT], writes=[sbank])
            p.op("T", "matmul", sbank, onesb.view(onesb.ap[:, 0:128]),
                 r1b.view(r1b.ap[:, sl]), start=False, stop=True,
                 reads=[rbf], writes=[sbank])
            p.op("V", "tensor_scalar", t_sb, pb[2], r_t16[:, qi:qi + 1].ap, st[:, 58:59].ap,
                 ALU.add, ALU.is_le, reads=[pb[2], c["qt"], st], writes=[t_sb])
            p.op("V", "scalar_tensor_tensor", pb[2], t_sb, 4096.0, sbank,
                 op0=ALU.mult, op1=ALU.add, reads=[t_sb, sbank], writes=[pb[2]])
            p.op("S", "activation", Pp[kg % 2], pb[2], AF.Exp,
                 bias=c["consts"][:, 4:5], scale=c["consts"][:, 2:3],
                 reads=[pb[2], c["consts"]], writes=[Pp[kg % 2]])
            if kg == ng - 1:
                dlt = qi - 4 * kg
                p.op("V", "tensor_tensor", Pp[kg % 2], Pp[kg % 2],
                     c["cmask"][:, dlt, :], op=ALU.mult,
                     reads=[Pp[kg % 2], c["cmask"]], writes=[Pp[kg % 2]])
            acc = st[:, par * 2:par * 2 + 1]
            tmp = st[:, par * 2 + 1:par * 2 + 2]
            dst = acc if kg == 0 else tmp
            p.op("V", "tensor_reduce", dst, Pp[kg % 2], axis=AX.X, op=ALU.add,
                 reads=[Pp[kg % 2]], writes=[st])
            if kg > 0:
                p.op("V", "tensor_tensor", acc, acc, tmp, op=ALU.add, reads=[st], writes=[st])
            nch = min(4, qi + 1 - 4 * kg)
            for j in range(nch):
                p.op("T", "transpose", pstb.view(pstb.ap[:, j * 128:(j + 1) * 128]),
                     Pp[kg % 2].view(Pp[kg % 2].ap[:, j * 128:(j + 1) * 128]), c["id_bf"],
                     reads=[Pp[kg % 2], c["id_bf"]], writes=[pstb])
            for j in range(nch):
                p.op("V", "tensor_copy", PT[:, 4 * kg + j, :],
                     pstb.view(pstb.ap[:, j * 128:(j + 1) * 128]),
                     reads=[pstb], writes=[PT])
        p.op("V", "reciprocal", c["crec"][:, qi:qi + 1], st[:, par * 2:par * 2 + 1],
             reads=[st], writes=[c["crec"]])
        for h in range(2):
            bank = pb[3 + h]
            for b in range(qi + 1):
                p.op("T", "matmul", bank, PT[:, b, :], v_sb[:, b, h * 512:(h + 1) * 512],
                     start=(b == 0), stop=(b == qi), reads=[PT, v_sb], writes=[bank])
        p.op("S", "copy", ctx_sb.view(ctx_sb.ap[:, 0:512]), pb[3],
             reads=[pb[3]], writes=[ctx_sb])
        p.op("S", "copy", ctx_sb.view(ctx_sb.ap[:, 512:1024]), pb[4],
             reads=[pb[4]], writes=[ctx_sb])
        for h in range(2):
            p.op("T", "matmul", pb[3 + h], qp2.view(qp2.ap[:, qi * 128:(qi + 1) * 128]),
                 wfqT.view(wfqT.ap[:, h * 512:(h + 1) * 512]), start=True, stop=True,
                 reads=[sA, sC], writes=[pb[3 + h]])
        def dvc(cc):
            return ctxT.view(ctxT.ap[:, cc, :])
        pe_transpose_sb(p, c, ctx_sb, dvc, DC)
        for h in range(2):
            bank = pb[5 + h]
            for cc in range(DC):
                p.op("T", "matmul", bank, ctxT[:, cc, :], woTs[:, cc, h * 512:(h + 1) * 512],
                     start=(cc == 0), stop=(cc == DC - 1),
                     reads=[ctxT, woTs], writes=[bank])
        for h in range(2):
            hsl = slice(h * 512, (h + 1) * 512)
            p.op("V", "scalar_tensor_tensor", e_sb, pb[3 + h], c["s01"][:, qi:qi + 1],
                 c["bc_bc"][:, hsl], op0=ALU.mult, op1=ALU.add,
                 reads=[pb[3 + h], c["s01"], c["bc_bc"]], writes=[e_sb])
            p.op("V", "scalar_tensor_tensor", pb[2], pb[5 + h], c["crec"][:, qi:qi + 1],
                 e_sb, op0=ALU.mult, op1=ALU.add,
                 reads=[pb[5 + h], c["crec"], e_sb], writes=[pb[2]])
            xsl = xr[:, qi, hsl]
            p.op("V", "scalar_tensor_tensor", xsl, xsl, 2.0, pb[2],
                 op0=ALU.mult, op1=ALU.add, reads=[xr, pb[2]], writes=[xr])
        emit_ln2(p, c, xr[:, qi, :], pb[5], pb[6], par)



def emit_l0_ffn(p, c):
    ar = c["ar"]
    w = c["w"]
    pb = c["pb"]
    xr = c["xr"]

    xT = ar.bf16("xT", (DC, TOK))
    u1 = ar.bf16("u1", (DC, 256))
    v1 = ar.bf16("v1", (2, 4096))
    u2 = ar.bf16("u2", (32, 256))
    v2 = ar.bf16("v2", (2, 1024))
    b2r = ar.f32("b2r", (1024,))
    b2_row = b2r[0:1, 0:1024]
    ones1 = c["stripeC"].view(c["stripeC"].ap[0:1, 1024:1152])
    PT1 = ar.bf16("PT1", (2, 512))
    H = ar.bf16("H", (32, 512))
    QT = ar.bf16("QT", (2, 512))

    p.dma("P", u1, w["L0_ffn_U1"].rearrange("(c p) r -> p c r", p=128))
    for rh in range(2):
        p.dma("P", v1[:, rh, :], w["L0_ffn_V1"][rh * 128:(rh + 1) * 128, :])
    u2v = w["L0_ffn_U2"].rearrange("(c p) r -> p c r", p=128)
    for gg in range(4):
        p.dma("P", u2[:, gg * 8:(gg + 1) * 8, :], u2v[:, gg * 8:(gg + 1) * 8, :])
    for rh in range(2):
        p.dma("P", v2[:, rh, :], w["L0_ffn_V2"][rh * 128:(rh + 1) * 128, :])
    p.dma("Y", b2_row, w["L0_ffn_b2"].unsqueeze(0))
    emit_xT(p, c, xT)

    for g in range(4):
        gsl = slice(g * 512, (g + 1) * 512)
        for rh in range(2):
            pw = pb[4 + rh]
            for cc in range(DC):
                p.op("T", "matmul", pw, u1[:, cc, rh * 128:(rh + 1) * 128],
                     xT[:, cc, gsl], start=(cc == 0), stop=(cc == DC - 1),
                     reads=[u1, xT], writes=[pw])
            p.op("S", "copy", PT1[:, rh, :], pw, reads=[pw], writes=[PT1])
        for fc in range(32):
            pw = pb[fc % 4]
            for rh in range(2):
                p.op("T", "matmul", pw, v1[:, rh, fc * 128:(fc + 1) * 128],
                     PT1[:, rh, :], start=(rh == 0), stop=(rh == 1),
                     reads=[v1, PT1], writes=[pw])
            p.op("S", "activation", H[:, fc, :], pw, AF.Gelu_apprx_tanh,
                 bias=c["b1_pp"][:, fc:fc + 1], reads=[pw, c["b1_pp"]], writes=[H])
        for rh in range(2):
            pw = pb[4 + rh]
            for fc in range(32):
                p.op("T", "matmul", pw, u2[:, fc, rh * 128:(rh + 1) * 128],
                     H[:, fc, :], start=(fc == 0), stop=(fc == 31),
                     reads=[u2, H], writes=[pw])
            p.op("S", "copy", QT[:, rh, :], pw, reads=[pw], writes=[QT])
        for tt in range(4):
            t = g * 4 + tt
            for h in range(2):
                pw = pb[(tt % 2) * 2 + h]
                p.op("T", "matmul", pw, ones1, b2_row.view(b2_row.ap[:, h * 512:(h + 1) * 512]),
                     start=True, stop=False, reads=[b2r, c["stripeC"]], writes=[pw])
                for rh in range(2):
                    p.op("T", "matmul", pw, QT[:, rh, tt * 128:(tt + 1) * 128],
                         v2[:, rh, h * 512:(h + 1) * 512], start=False, stop=(rh == 1),
                         reads=[QT, v2], writes=[pw])
                xsl = xr[:, t, h * 512:(h + 1) * 512]
                p.op("V", "tensor_tensor", xsl, xsl, pw, op=ALU.add,
                     reads=[xr, pw], writes=[xr])
            emit_ln2(p, c, xr[:, t, :], pb[4], pb[5], t % 2)



def emit_l1_attn(p, c):
    ar = c["ar"]
    scr = c["scr"]
    pb = c["pb"]
    pstb = c["pstb"]
    xr = c["xr"]
    st = c["stat"]

    xT = ar.bf16("xT", (DC, TOK))
    v_sb = ar.bf16("v_sb", (NT, D))
    g1s = ar.bf16("g1s", (DC, 1024))
    off_wvT = (ar.mark() + 1) & ~1
    wvTs = ar.bf16("wvT1s", (DC, 1024))

    p.dma("Y", g1s, scr["g1"].rearrange("(c p) o -> p c o", p=128))
    p.dma("S", wvTs, scr["wvT1"].rearrange("(c p) o -> p c o", p=128))

    emit_xT(p, c, xT)
    for t in range(NT):
        for h in range(2):
            bank = pb[5 + h]
            for cc in range(DC):
                p.op("T", "matmul", bank, xT[:, cc, t * 128:(t + 1) * 128],
                     wvTs[:, cc, h * 512:(h + 1) * 512],
                     start=(cc == 0), stop=(cc == DC - 1),
                     reads=[xT, wvTs], writes=[bank])
            p.op("S", "copy", v_sb[:, t, h * 512:(h + 1) * 512], bank,
                 reads=[bank], writes=[v_sb])

    p.barrier()
    o = off_wvT
    PT = ar.at_bf16(o, "PT1a", (NT, 128)); o += 2048
    q1 = ar.at_bf16(o, "q1a", (DC, 128)); o += 1024
    Pp = [ar.at_bf16(o, "P0a", (512,)), ar.at_bf16(o + 512, "P1a", (512,))]; o += 1024
    e_sb = ar.at_f32(o, "e1a", (512,)); o += 1024
    assert o <= off_wvT + 8192

    for qi in range(NT):
        par = qi % 2
        for d2c in range(DC):
            bank = pb[3 + d2c // 4]
            for d1c in range(DC):
                p.op("T", "matmul", bank.view(bank.ap[:, (d2c % 4) * 128:(d2c % 4 + 1) * 128]),
                     g1s[:, d1c, d2c * 128:(d2c + 1) * 128],
                     xT[:, d1c, qi * 128:(qi + 1) * 128],
                     start=(d2c % 4 == 0 and d1c == 0),
                     stop=(d2c % 4 == 3 and d1c == DC - 1),
                     reads=[g1s, xT], writes=[bank])
        q1f = q1.view(q1.ap.rearrange("p a b -> p (a b)"))
        p.op("S", "copy", q1f.view(q1f.ap[:, 0:512]), pb[3], reads=[pb[3]], writes=[q1])
        p.op("S", "copy", q1f.view(q1f.ap[:, 512:1024]), pb[4], reads=[pb[4]], writes=[q1])
        for kg in range(4):
            sl = slice(kg * 512, (kg + 1) * 512)
            sbank = pb[kg % 2]
            for cc in range(DC):
                p.op("T", "matmul", sbank, q1[:, cc, :], xT[:, cc, sl],
                     start=(cc == 0), stop=(cc == DC - 1), reads=[q1, xT], writes=[sbank])
            p.op("S", "activation", Pp[kg % 2], sbank, AF.Exp,
                 bias=c["consts"][:, 5:6], scale=c["consts"][:, 2:3],
                 reads=[sbank, c["consts"]], writes=[Pp[kg % 2]])
            acc = st[:, par * 2:par * 2 + 1]
            tmp = st[:, par * 2 + 1:par * 2 + 2]
            dst = acc if kg == 0 else tmp
            p.op("V", "tensor_reduce", dst, Pp[kg % 2], axis=AX.X, op=ALU.add,
                 reads=[Pp[kg % 2]], writes=[st])
            if kg > 0:
                p.op("V", "tensor_tensor", acc, acc, tmp, op=ALU.add, reads=[st], writes=[st])
            for j in range(4):
                p.op("T", "transpose", pstb.view(pstb.ap[:, j * 128:(j + 1) * 128]),
                     Pp[kg % 2].view(Pp[kg % 2].ap[:, j * 128:(j + 1) * 128]), c["id_bf"],
                     reads=[Pp[kg % 2], c["id_bf"]], writes=[pstb])
            for j in range(4):
                p.op("V", "tensor_copy", PT[:, 4 * kg + j, :],
                     pstb.view(pstb.ap[:, j * 128:(j + 1) * 128]),
                     reads=[pstb], writes=[PT])
        p.op("V", "reciprocal", c["crec"][:, qi:qi + 1], st[:, par * 2:par * 2 + 1],
             reads=[st], writes=[c["crec"]])
        for h in range(2):
            bank = pb[5 + h]
            for b in range(NT):
                p.op("T", "matmul", bank, PT[:, b, :], v_sb[:, b, h * 512:(h + 1) * 512],
                     start=(b == 0), stop=(b == NT - 1), reads=[PT, v_sb], writes=[bank])
            p.op("V", "tensor_scalar", e_sb, bank, c["crec"][:, qi:qi + 1], None,
                 ALU.mult, reads=[bank, c["crec"]], writes=[e_sb])
            xsl = xr[:, qi, h * 512:(h + 1) * 512]
            p.op("V", "tensor_tensor", xsl, xsl, e_sb, op=ALU.add, reads=[xr, e_sb], writes=[xr])
        emit_ln2(p, c, xr[:, qi, :], pb[5], pb[6], par)



def emit_l1_ffn(p, c):
    ar = c["ar"]
    scr = c["scr"]
    pb = c["pb"]
    xr = c["xr"]
    NI = 2816
    NCH = [512, 512, 512, 512, 512, 256]

    xqT = ar.bf16("xqT", (DC, TOK))
    hqT = ar.bf16("hqT", (4, 22, 128))
    wg = ar.bf16("wg", (DC, 512))
    wu = ar.bf16("wu", (DC, 512))
    wdq = [ar.bf16("wd0", (22, 256)), ar.bf16("wd1", (22, 256))]
    qtmp = ar.f32("qtmp", (3, 512))
    scq = ar.f32("scq", (32,))
    rcq = ar.f32("rcq", (32,))
    qtmp2 = ar.f32("qtmp2", (3, 512))
    scq2 = ar.f32("scq2", (32,))
    rcq2 = ar.f32("rcq2", (32,))
    qsets = [(qtmp, scq, rcq), (qtmp2, scq2, rcq2)]
    xq_t = ar.bf16("xq_t", (1024,))
    h_sb = [ar.f32("h0", (512,)), ar.f32("h1", (512,))]
    hq_c = [ar.bf16("hq0", (512,)), ar.bf16("hq1", (512,))]

    gview = scr["wgT"].rearrange("(c p) n -> p c n", p=128)
    uview = scr["wuT"].rearrange("(c p) n -> p c n", p=128)
    dview = scr["wdT"].rearrange("(c p) n -> p c n", p=128)

    for t in range(NT):
        for hh in range(2):
            qb_, sc_, rc_ = qsets[hh]
            emit_quant(p, c, xr[:, t, hh * 512:(hh + 1) * 512],
                       xq_t[:, hh * 512:(hh + 1) * 512], qb_, sc_, rc_, 512)

        def dv(cc, t=t):
            return xqT.view(xqT.ap[:, cc, t * 128:(t + 1) * 128])
        pe_transpose_sb(p, c, xq_t, dv, DC)

    if L1F_BAR >= 1:
        p.barrier()
    for Q in range(4):
        if L1F_BAR >= 2:
            p.barrier()
        tiles = [Q * 4 + i for i in range(4)]
        for ngi in range(6):
            if L1F_BAR >= 3:
                p.barrier()
            nn = NCH[ngi]
            n0 = ngi * 512
            for cc in range(DC):
                p.dma("Y", wg[:, cc, 0:nn], gview[:, cc, n0:n0 + nn])
                p.dma("P", wu[:, cc, 0:nn], uview[:, cc, n0:n0 + nn])
            for ti, t in enumerate(tiles):
                pg = pb[4] if ti % 2 == 0 else pb[2]
                pu = pb[5] if ti % 2 == 0 else pb[3]
                pgv = pg.view(pg.ap[:, 0:nn])
                puv = pu.view(pu.ap[:, 0:nn])
                for cc in range(DC):
                    p.op("T", "matmul", pgv, xqT[:, cc, t * 128:(t + 1) * 128],
                         wg[:, cc, 0:nn], start=(cc == 0), stop=(cc == DC - 1),
                         reads=[xqT, wg], writes=[pg])
                for cc in range(DC):
                    p.op("T", "matmul", puv, xqT[:, cc, t * 128:(t + 1) * 128],
                         wu[:, cc, 0:nn], start=(cc == 0), stop=(cc == DC - 1),
                         reads=[xqT, wu], writes=[pu])
                par = ti % 2
                p.op("S", "activation", h_sb[par][:, 0:nn], pgv, AF.Silu,
                     reads=[pg], writes=[h_sb[par]])
                p.op("V", "tensor_tensor", h_sb[par][:, 0:nn], h_sb[par][:, 0:nn], puv,
                     op=ALU.mult, reads=[h_sb[par], pu], writes=[h_sb[par]])
                qb_, sc_, rc_ = qsets[par]
                emit_quant(p, c, h_sb[par][:, 0:nn], hq_c[par][:, 0:nn],
                           qb_, sc_, rc_, nn)

                def dvh(cc, ti=ti, ngi=ngi, par=par):
                    return hqT.view(hqT.ap[:, ti, ngi * 4 + cc, :])
                pe_transpose_sb(p, c, hq_c[par], dvh, nn // 128)
        p.dma("S", wdq[0], dview[:, :, 0:256])
        for q4 in range(4):
            if q4 < 3:
                p.dma("S", wdq[(q4 + 1) % 2],
                      dview[:, :, (q4 + 1) * 256:(q4 + 2) * 256])
            wd = wdq[q4 % 2]
            for ti, t in enumerate(tiles):
                pw = pb[ti % 2]
                pwv = pw.view(pw.ap[:, 0:256])
                for cc in range(22):
                    p.op("T", "matmul", pwv, hqT[:, ti, cc, :], wd[:, cc, :],
                         start=(cc == 0), stop=(cc == 21), reads=[hqT, wd], writes=[pw])
                xsl = xr[:, t, q4 * 256:(q4 + 1) * 256]
                p.op("V", "tensor_tensor", xsl, xsl, pwv, op=ALU.add,
                     reads=[xr, pw], writes=[xr])
        for ti, t in enumerate(tiles):
            emit_ln2(p, c, xr[:, t, :], pb[4], pb[5], t % 2)



UNFUSED = False
_cache = {}


def _get_prog(blocks):
    key = tuple(blocks)
    if key not in _cache:
        _cache[key] = build_program(list(blocks))
    return _cache[key]


def kernel(**inputs):
    x = np.ascontiguousarray(np.asarray(inputs["L0_attn_hidden_states"], np.float32))
    B, T_, D_ = x.shape
    xc = x.reshape(N_CORES, N_ITEMS * T_, D_)
    stages = [["l0a"], ["l0f"], ["l1a"], ["l1f"]] if UNFUSED else [["l0a", "l0f", "l1a", "l1f"]]
    act = xc
    for blocks in stages:
        nc, names = _get_prog(tuple(blocks))
        in_maps = []
        for core in range(N_CORES):
            m = {"x": np.ascontiguousarray(act[core])}
            for nm in names:
                m[nm] = np.ascontiguousarray(np.asarray(inputs[nm], np.float32))
            in_maps.append(m)
        res = run_bass_kernel_spmd(nc, in_maps, core_ids=list(range(N_CORES)))
        act = np.stack([res.results[i]["y"] for i in range(N_CORES)], axis=0)
    return act.reshape(B, T_, D_)


# version 16
# speedup vs baseline: 1.0397x; 1.0382x over previous
import contextlib
import math

import numpy as np

import concourse.bass as bass
import concourse.mybir as mybir
from concourse.bass_utils import run_bass_kernel_spmd

F32 = mybir.dt.float32
BF16 = mybir.dt.bfloat16
AF = mybir.ActivationFunctionType
ALU = mybir.AluOpType
AX = mybir.AxisListType

MAGIC = 12582912.0
TOK = 2048
D = 1024
NT = TOK // 128
DC = D // 128
N_ITEMS = 4
N_CORES = 8

ENGS = ("T", "V", "S", "P", "Y")
ESEM = {e: "e" + e for e in ENGS}
DSEM = {"Y": "dY", "S": "dS", "P": "dG"}
ALLSEMS = [ESEM[e] for e in ENGS] + list(DSEM.values())

_gid = [0]


class Tok:
    __slots__ = ("sem", "gid", "need", "val", "amt")

    def __init__(self, sem, amt):
        self.sem = sem
        self.amt = amt
        self.need = False
        self.val = None
        _gid[0] += 1
        self.gid = _gid[0]


class Buf:
    __slots__ = ("w", "r", "name")

    def __init__(self, name=""):
        self.w = None
        self.r = {}
        self.name = name


class V:
    __slots__ = ("ap", "buf")

    def __init__(self, ap, buf):
        self.ap = ap
        self.buf = buf

    def __getitem__(self, idx):
        return V(self.ap[idx], self.buf)

    def rearrange(self, *a, **k):
        return V(self.ap.rearrange(*a, **k), self.buf)

    def view(self, ap):
        return V(ap, self.buf)


def carve(v, lo, hi, name):
    return V(v.ap[:, lo:hi], Buf(name))


class P:
    def __init__(self, nc):
        self.nc = nc
        self.es = contextlib.ExitStack()
        self.recs = {e: [] for e in ENGS}
        self.semh = {}
        self.last = {}
        self.qlast = {}

    def sb(self, name, shape, dt):
        h = self.es.enter_context(self.nc.sbuf_tensor(name, list(shape), dt))
        return V(h[:], Buf(name)), h

    def ps(self, name, shape, dt):
        h = self.es.enter_context(self.nc.psum_tensor(name, list(shape), dt))
        return V(h[:], Buf(name)), h

    def _deps(self, eng, reads, writes):
        deps = {}
        for v in reads:
            b = v.buf
            if b.w is not None:
                t = b.w
                if t.gid > deps.get(t.sem, (None, -1))[1]:
                    deps[t.sem] = (t, t.gid)
        for v in writes:
            b = v.buf
            cand = ([b.w] if b.w is not None else []) + list(b.r.values())
            for t in cand:
                if t.gid > deps.get(t.sem, (None, -1))[1]:
                    deps[t.sem] = (t, t.gid)
        out = []
        for sem, (t, _) in deps.items():
            if eng == "T" and sem == "eT":
                continue
            out.append(t)
        return out

    def _mark(self, eng, tok, reads, writes):
        for v in writes:
            v.buf.w = tok
            v.buf.r = {}
        for v in reads:
            prev = v.buf.r.get(tok.sem)
            if prev is None or prev.gid < tok.gid:
                v.buf.r[tok.sem] = tok

    def op(self, eng, meth, *args, reads=(), writes=(), **kw):
        for t in self._deps(eng, reads, writes):
            t.need = True
            self.recs[eng].append(("w", t))
        tok = Tok(ESEM[eng], 1)
        args = tuple(a.ap if isinstance(a, V) else a for a in args)
        kw = {k: (v.ap if isinstance(v, V) else v) for k, v in kw.items()}
        self.recs[eng].append(("op", meth, args, kw, tok))
        self.last[tok.sem] = tok
        self._mark(eng, tok, reads, writes)
        return tok

    def dma(self, eng, out, in_, reads=(), writes=(), meth="dma_start", **kw):
        reads = list(reads)
        writes = list(writes)
        if isinstance(out, V):
            writes.append(out)
        if isinstance(in_, V):
            reads.append(in_)
        for t in self._deps(eng, reads, writes):
            t.need = True
            self.recs[eng].append(("w", t))
        q = DSEM[eng]
        if q in self.qlast:
            self.recs[eng].append(("w", self.qlast[q]))
        tok = Tok(DSEM[eng], 16)
        self.qlast[q] = tok
        o = out.ap if isinstance(out, V) else out
        i = in_.ap if isinstance(in_, V) else in_
        self.recs[eng].append(("op", meth, (), {"out": o, "in_": i, **kw}, tok))
        tok.need = True
        self.last[tok.sem] = tok
        self._mark(eng, tok, reads, writes)
        return tok

    def wait_on(self, eng, toks):
        for t in toks:
            t.need = True
            self.recs[eng].append(("w", t))

    def barrier(self):
        toks = [t for t in self.last.values()]
        for e in ENGS:
            self.wait_on(e, toks)

    def finalize_and_replay(self, block):
        cnt = {s: 0 for s in ALLSEMS}
        for e in ENGS:
            for r in self.recs[e]:
                if r[0] == "op":
                    tok = r[-1]
                    if tok.need:
                        cnt[tok.sem] += tok.amt
                        tok.val = cnt[tok.sem]
        for s in ALLSEMS:
            self.semh[s] = self.es.enter_context(self.nc.semaphore(name="sem_" + s))
        secs = {"T": block.tensor, "V": block.vector, "S": block.scalar,
                "P": block.gpsimd, "Y": block.sync}
        for e in ENGS:
            recs = self.recs[e]
            if not recs:
                continue
            hwm = {}

            def run(eng_h, recs=recs, hwm=hwm):
                for r in recs:
                    if r[0] == "w":
                        t = r[1]
                        if t.val is None:
                            continue
                        if hwm.get(t.sem, 0) >= t.val:
                            continue
                        eng_h.wait_ge(self.semh[t.sem], t.val)
                        hwm[t.sem] = t.val
                    else:
                        _, meth, args, kw, tok = r
                        inst = getattr(eng_h, meth)(*args, **kw)
                        if tok.need:
                            inst.then_inc(self.semh[tok.sem], tok.amt)
            secs[e](run)


class Arena:

    def __init__(self, p, name, kb):
        self.p = p
        self.n = kb * 512
        v, h = p.sb(name, [128, self.n], BF16)
        self.t = h
        self.off = 0

    def _take(self, n_bf16, name):
        off = (self.off + 1) & ~1
        assert off + n_bf16 <= self.n, f"arena overflow at {name}: {off}+{n_bf16}>{self.n}"
        self.off = off + n_bf16
        return off

    @staticmethod
    def _shape(ap, shape):
        if len(shape) == 2:
            return ap.rearrange("p (a b) -> p a b", b=shape[-1])
        if len(shape) == 3:
            return ap.rearrange("p (a b c) -> p a b c", b=shape[1], c=shape[2])
        return ap

    def bf16(self, name, shape):
        n = int(np.prod(shape))
        off = self._take(n, name)
        return V(self._shape(self.t[:, off:off + n], shape), Buf(name))

    def f32(self, name, shape):
        n = int(np.prod(shape))
        off = self._take(2 * n, name)
        return V(self._shape(self.t[:, off:off + 2 * n].bitcast(F32), shape), Buf(name))

    def at_bf16(self, off, name, shape):
        n = int(np.prod(shape))
        assert off + n <= self.n
        return V(self._shape(self.t[:, off:off + n], shape), Buf(name))

    def at_f32(self, off, name, shape):
        n = int(np.prod(shape))
        assert off + 2 * n <= self.n
        return V(self._shape(self.t[:, off:off + 2 * n].bitcast(F32), shape), Buf(name))

    def mark(self):
        return self.off

    def reset(self, to=None):
        self.p.barrier()
        self.off = 0 if to is None else to


def ceil_div(a, b):
    return (a + b - 1) // b



WNAMES = {
    "l0a": ["L0_attn_Wq", "L0_attn_bq", "L0_attn_Wk", "L0_attn_bk", "L0_attn_Wv",
            "L0_attn_bv", "L0_attn_Wo", "L0_attn_bo", "L0_attn_Wtq", "L0_attn_btq",
            "L0_attn_Wfq", "L0_attn_bfq"],
    "l0f": ["L0_ffn_U1", "L0_ffn_V1", "L0_ffn_b1", "L0_ffn_U2", "L0_ffn_V2", "L0_ffn_b2"],
    "l1a": ["L1_attn_Wq", "L1_attn_Wk", "L1_attn_Wv"],
    "l1f": ["L1_ffn_w_gate", "L1_ffn_w_up", "L1_ffn_w_down"],
}

WSHAPES = {"L0_attn_Wq": [D, D], "L0_attn_Wk": [D, D], "L0_attn_Wv": [D, D],
           "L0_attn_Wo": [D, D], "L0_attn_Wtq": [2, D], "L0_attn_Wfq": [D, 2],
           "L0_attn_bq": [D], "L0_attn_bk": [D], "L0_attn_bv": [D],
           "L0_attn_bo": [D], "L0_attn_btq": [2], "L0_attn_bfq": [D],
           "L0_ffn_U1": [D, 256], "L0_ffn_V1": [256, 4096], "L0_ffn_b1": [4096],
           "L0_ffn_U2": [4096, 256], "L0_ffn_V2": [256, D], "L0_ffn_b2": [D],
           "L1_attn_Wq": [D, D], "L1_attn_Wk": [D, D], "L1_attn_Wv": [D, D],
           "L1_ffn_w_gate": [2816, D], "L1_ffn_w_up": [2816, D],
           "L1_ffn_w_down": [D, 2816]}


def build_program(blocks, n_items=N_ITEMS):
    nc = bass.Bass("TRN2", target_bir_lowering=False)
    p = P(nc)
    c = {}

    x_in = nc.dram_tensor("x", [n_items * TOK, D], F32, kind="ExternalInput").ap()
    y_out = nc.dram_tensor("y", [n_items * TOK, D], F32, kind="ExternalOutput").ap()
    names = []
    w = {}
    for b in blocks:
        names.extend(WNAMES[b])
    for nm in names:
        w[nm] = nc.dram_tensor(nm, WSHAPES[nm], F32, kind="ExternalInput").ap()
    c["w"] = w

    scr = {}
    if "l0a" in blocks:
        scr["g0"] = nc.dram_tensor("s_g0", [D, D], BF16, kind="Internal").ap()
        scr["wvT0"] = nc.dram_tensor("s_wvT0", [D, D], BF16, kind="Internal").ap()
        scr["woT0"] = nc.dram_tensor("s_woT0", [D, D], BF16, kind="Internal").ap()
    if "l1a" in blocks:
        scr["g1"] = nc.dram_tensor("s_g1", [D, D], BF16, kind="Internal").ap()
        scr["wvT1"] = nc.dram_tensor("s_wvT1", [D, D], BF16, kind="Internal").ap()
    if "l1f" in blocks:
        scr["wgT"] = nc.dram_tensor("s_wgT", [D, 2816], BF16, kind="Internal").ap()
        scr["wuT"] = nc.dram_tensor("s_wuT", [D, 2816], BF16, kind="Internal").ap()
        scr["wdT"] = nc.dram_tensor("s_wdT", [2816, D], BF16, kind="Internal").ap()
    c["scr"] = scr

    c["xr"], _ = p.sb("xr", [128, NT, D], F32)
    c["id_bf"], _ = p.sb("id_bf", [128, 128], BF16)
    c["id_f"], _ = p.sb("id_f", [128, 128], F32)
    c["stat"], _ = p.sb("stat", [128, 192], F32)
    c["qt"], _ = p.sb("qt", [128, 48], F32)
    c["crec"], _ = p.sb("crec", [128, NT], F32)
    c["s01"], _ = p.sb("s01", [128, NT], F32)
    c["consts"], _ = p.sb("consts", [128, 12], F32)
    c["smalls"], _ = p.sb("smalls", [128, 160], F32)
    c["ones128"], _ = p.sb("ones128", [128, 1], F32)
    c["b1_pp"], _ = p.sb("b1_pp", [128, 32], F32)
    c["wtqT"], _ = p.sb("wtqT", [128, 16], F32)
    c["wfq_pp"], _ = p.sb("wfq_pp", [128, 16], F32)
    c["bq_pp"], _ = p.sb("bq_pp", [128, 8], F32)
    c["w1_ppb"], _ = p.sb("w1_ppb", [128, 8], BF16)
    c["bv_ppb"], _ = p.sb("bv_ppb", [128, 8], BF16)
    c["stripeA"], _ = p.sb("stripeA", [128, 2048], F32)
    c["stripeC"], _ = p.sb("stripeC", [128, 1536], F32)
    c["rows_bf"], _ = p.sb("rows_bf", [128, 2176], BF16)
    c["bc_bc"], _ = p.sb("bc_bc", [128, 1024], F32)
    c["cmask"], _ = p.sb("cmask", [128, 4, 512], BF16)

    c["pb"] = []
    for i in range(7):
        v, _ = p.ps("pb%d" % i, [128, 512], F32)
        c["pb"].append(v)
    c["pstb"], _ = p.ps("pstb", [128, 1024], BF16)

    ar = Arena(p, "arena", 114)
    c["ar"] = ar
    c["p"] = p
    c["nc"] = nc

    emit_prep(p, c, blocks)
    ar.reset()

    for item in range(n_items):
        xs = x_in[item * TOK:(item + 1) * TOK, :].rearrange("(t p) d -> p t d", p=128)
        p.dma("Y", c["xr"], xs)
        for b in blocks:
            if b == "l0a":
                emit_l0_attn(p, c)
            elif b == "l0f":
                emit_l0_ffn(p, c)
            elif b == "l1a":
                emit_l1_attn(p, c)
            elif b == "l1f":
                emit_l1_ffn(p, c)
            ar.reset()
        ys = y_out[item * TOK:(item + 1) * TOK, :].rearrange("(t p) d -> p t d", p=128)
        p.dma("Y", ys, c["xr"], writes=())
        ar.reset()

    if DBG_STATICS:
        aap = nc.dram_tensor("dbg_arena", [128, ar.n], BF16, kind="ExternalOutput").ap()
        p.dma("Y", aap, V(ar.t[:], Buf("arena_dump")), writes=())
        for nm in ("stat", "smalls", "stripeA", "crec", "s01", "qt", "rows_bf", "bc_bc"):
            v = c[nm]
            shp = list(v.ap.shape)
            dt = v.ap.dtype
            dap = nc.dram_tensor("dbg_" + nm, shp, dt, kind="ExternalOutput").ap()
            p.dma("Y", dap, v, writes=())
    p.barrier()
    with nc.Block() as block:
        p.finalize_and_replay(block)
    p.es.close()
    return nc, names



def emit_prep(p, c, blocks):
    ar = c["ar"]
    w = c["w"]
    scr = c["scr"]
    pb = c["pb"]
    sA = c["stripeA"]
    sC = c["stripeC"]

    p.op("P", "memset", c["smalls"], 1.0, writes=[c["smalls"]])
    p.op("P", "affine_select", c["id_f"], c["smalls"][:, 0:128], pattern=[[1, 128]],
         compare_op=ALU.is_equal, fill=0.0, base=0, channel_multiplier=-1,
         reads=[c["smalls"]], writes=[c["id_f"]])
    p.op("V", "tensor_copy", c["id_bf"], c["id_f"], reads=[c["id_f"]], writes=[c["id_bf"]])
    p.op("V", "memset", c["ones128"], 1.0, writes=[c["ones128"]])
    for i, val in ((0, 1e-5), (1, -0.5), (2, 1.0 / 32.0), (3, 1.0),
                   (4, -128.0), (5, -32.0), (6, MAGIC), (7, -MAGIC * 0.5),
                   (8, -2.0 * MAGIC), (9, 0.5), (10, 2.0)):
        p.op("V", "memset", c["consts"][:, i:i + 1], val, writes=[c["consts"]])
    p.op("V", "memset", sC.view(sC.ap[0:1, 1024:1536]), 1.0, writes=[sC])
    p.op("V", "memset", c["rows_bf"].view(c["rows_bf"].ap[0:1, 2048:2176]), 1.0,
         writes=[c["rows_bf"]])
    p.op("V", "memset", c["cmask"], 1.0, writes=[c["cmask"]])
    for d in range(4):
        p.op("P", "affine_select", c["cmask"][:, d, :], c["cmask"][:, d, :],
             pattern=[[-1, 512]], compare_op=ALU.is_ge, fill=0.0, base=d * 128,
             channel_multiplier=1, reads=[c["cmask"]], writes=[c["cmask"]])

    stage_f = ar.f32("prep_stage_f", (2816,))
    stage_f2 = ar.f32("prep_stage_f2", (2816,))
    stage_b = [ar.bf16("prep_stage_b0", (2816,)), ar.bf16("prep_stage_b1", (2816,))]
    tstage = [ar.bf16("prep_tstage0", (22, 128)), ar.bf16("prep_tstage1", (22, 128))]
    qsets = []
    for qsi in range(2):
        qsets.append((ar.f32("prep_qb%d" % qsi, (3, 512)),
                      ar.f32("prep_sc%d" % qsi, (32,)),
                      ar.f32("prep_rc%d" % qsi, (32,))))
    qb, sc, rc = qsets[0]
    wqb = ar.bf16("prep_wqb", (DC, 1024))
    wkb = ar.bf16("prep_wkb", (DC, 1024))
    gstage = ar.bf16("prep_gstage", (1024,))

    def wT_prep(src_ap, dst_ap, rows, cols):
        nrc = rows // 128
        ncc = cols // 128
        dview = dst_ap.rearrange("(c p) o -> p c o", p=128)
        for rci in range(nrc):
            par = rci % 2
            st_ = stage_b[par].view(stage_b[par].ap[:, 0:cols])
            p.dma("P", st_, src_ap[rci * 128:(rci + 1) * 128, :])

            def dv(cc, par=par):
                return tstage[par].view(tstage[par].ap[:, cc, :])
            pe_transpose_sb(p, c, st_, dv, ncc)
            p.dma("Y", dview[:, 0:ncc, rci * 128:(rci + 1) * 128],
                  tstage[par].view(tstage[par].ap[:, 0:ncc, :]))

    def load_wqk(nmq, nmk):
        for rc_ in range(DC):
            p.dma("P", wqb[:, rc_, :], w[nmq][rc_ * 128:(rc_ + 1) * 128, :])
            p.dma("P", wkb[:, rc_, :], w[nmk][rc_ * 128:(rc_ + 1) * 128, :])

    def build_g(dst_ap):
        gview = dst_ap.rearrange("(c p) o -> p c o", p=128)
        for d1c in range(DC):
            for h in range(2):
                bank = pb[(d1c % 2) * 2 + h]
                for oc in range(DC):
                    p.op("T", "matmul", bank, wqb[:, oc, d1c * 128:(d1c + 1) * 128],
                         wkb[:, oc, h * 512:(h + 1) * 512],
                         start=(oc == 0), stop=(oc == DC - 1),
                         reads=[wqb, wkb], writes=[bank])
                p.op("V", "tensor_copy", gstage[:, h * 512:(h + 1) * 512], bank,
                     reads=[bank], writes=[gstage])
            p.dma("Y", gview[:, d1c, :], gstage)

    def row_to_pp(src_row, pp_f32, pp_bf):
        for oc in range(DC):
            p.op("T", "transpose", pb[6].view(pb[6].ap[:, oc:oc + 1]),
                 src_row.view(src_row.ap[0:1, oc * 128:(oc + 1) * 128]),
                 c["id_f"][0:1, 0:1], reads=[src_row, c["id_f"]], writes=[pb[6]])
        if pp_f32 is not None:
            p.op("V", "tensor_copy", pp_f32, pb[6].view(pb[6].ap[:, 0:8]),
                 reads=[pb[6]], writes=[pp_f32])
        p.op("V", "tensor_copy", pp_bf, pb[6].view(pb[6].ap[:, 0:8]),
             reads=[pb[6]], writes=[pp_bf])

    if "l0a" in blocks:
        wT_prep(w["L0_attn_Wv"], scr["wvT0"], D, D)
        wT_prep(w["L0_attn_Wo"], scr["woT0"], D, D)
        st_ = stage_f.view(stage_f.ap[0:8, 0:128])
        p.dma("Y", st_, w["L0_attn_bq"].rearrange("(c p) -> c p", p=128))
        p.op("T", "transpose", pb[6].view(pb[6].ap[:, 0:8]), st_, c["id_f"][0:8, 0:8],
             reads=[st_, c["id_f"]], writes=[pb[6]])
        p.op("V", "tensor_copy", c["bq_pp"], pb[6].view(pb[6].ap[:, 0:8]),
             reads=[pb[6]], writes=[c["bq_pp"]])
        p.dma("Y", st_, w["L0_attn_bv"].rearrange("(c p) -> c p", p=128))
        p.op("T", "transpose", pb[6].view(pb[6].ap[:, 0:8]), st_, c["id_f"][0:8, 0:8],
             reads=[st_, c["id_f"]], writes=[pb[6]])
        p.op("V", "tensor_copy", c["bv_ppb"], pb[6].view(pb[6].ap[:, 0:8]),
             reads=[pb[6]], writes=[c["bv_ppb"]])
        load_wqk("L0_attn_Wq", "L0_attn_Wk")
        build_g(scr["g0"])
        bqb = ar.bf16("prep_bqb", (8,))
        p.op("V", "tensor_copy", bqb, c["bq_pp"], reads=[c["bq_pp"]], writes=[bqb])
        for h in range(2):
            bank = pb[4 + h]
            for oc in range(DC):
                p.op("T", "matmul", bank.view(bank.ap[0:1, :]),
                     bqb[:, oc:oc + 1],
                     wkb[:, oc, h * 512:(h + 1) * 512],
                     start=(oc == 0), stop=(oc == DC - 1),
                     reads=[bqb, wkb], writes=[bank])
            p.op("V", "tensor_copy", sA.view(sA.ap[0:1, h * 512:(h + 1) * 512]),
                 bank.view(bank.ap[0:1, :]), reads=[bank], writes=[sA])
        row_to_pp(sA.view(sA.ap[0:1, 0:1024]), None, c["w1_ppb"])
        woTsl = ar.bf16("prep_woTsl", (DC, 1024))
        p.dma("Y", woTsl, scr["woT0"].rearrange("(c p) o -> p c o", p=128),
              writes=[tstage[0], tstage[1]])
        for h in range(2):
            bank = pb[4 + h]
            for oc in range(DC):
                p.op("T", "matmul", bank.view(bank.ap[0:1, :]),
                     c["bv_ppb"][:, oc:oc + 1], woTsl[:, oc, h * 512:(h + 1) * 512],
                     start=(oc == 0), stop=(oc == DC - 1),
                     reads=[c["bv_ppb"], woTsl], writes=[bank])
        p.dma("Y", sA.view(sA.ap[64:65, 0:1024]), w["L0_attn_bfq"].unsqueeze(0))
        p.dma("Y", sA.view(sA.ap[64:65, 1024:2048]), w["L0_attn_bo"].unsqueeze(0))
        p.op("V", "scalar_tensor_tensor", sA.view(sA.ap[0:1, 0:1024]),
             sA.view(sA.ap[64:65, 0:1024]), 0.1, sA.view(sA.ap[64:65, 1024:2048]),
             op0=ALU.mult, op1=ALU.add, reads=[sA], writes=[sA])
        for h in range(2):
            p.op("V", "tensor_tensor", sA.view(sA.ap[0:1, h * 512:(h + 1) * 512]),
                 sA.view(sA.ap[0:1, h * 512:(h + 1) * 512]),
                 pb[4 + h].view(pb[4 + h].ap[0:1, :]), op=ALU.add,
                 reads=[sA, pb[4 + h]], writes=[sA])
        for h in range(2):
            bank = pb[h]
            p.op("T", "matmul", bank, sC.view(sC.ap[0:1, 1024:1152]),
                 sA.view(sA.ap[0:1, h * 512:(h + 1) * 512]), start=True, stop=True,
                 reads=[sC, sA], writes=[bank])
            p.op("V", "tensor_copy", c["bc_bc"][:, h * 512:(h + 1) * 512], bank,
                 reads=[bank], writes=[c["bc_bc"]])
        st2 = stage_f.view(stage_f.ap[0:2, 0:1024])
        p.dma("Y", st2, w["L0_attn_Wtq"])
        for cc in range(DC):
            p.op("T", "transpose", pb[6].view(pb[6].ap[:, 2 * cc:2 * cc + 2]),
                 st2.view(st2.ap[:, cc * 128:(cc + 1) * 128]), c["id_f"][0:2, 0:2],
                 reads=[st2, c["id_f"]], writes=[pb[6]])
        p.op("V", "tensor_copy", c["wtqT"], pb[6].view(pb[6].ap[:, 0:16]),
             reads=[pb[6]], writes=[c["wtqT"]])
        p.dma("Y", c["wfq_pp"].rearrange("p (a b) -> p a b", b=2),
              w["L0_attn_Wfq"].rearrange("(c p) t -> p c t", p=128),
              allow_slow_non_contiguous=True)
        for cc in range(DC):
            p.op("T", "transpose", pb[6].view(pb[6].ap[0:2, 0:128]),
                 c["wfq_pp"].view(c["wfq_pp"].ap.rearrange("p (a b) -> p a b", b=2)[:, cc, :]),
                 c["id_f"], reads=[c["wfq_pp"], c["id_f"]], writes=[pb[6]])
            p.op("V", "tensor_copy", sC.view(sC.ap[0:2, cc * 128:(cc + 1) * 128]),
                 pb[6].view(pb[6].ap[0:2, 0:128]), reads=[pb[6]], writes=[sC])
        p.dma("Y", c["stat"].view(c["stat"].ap[0:1, 60:62]), w["L0_attn_btq"].unsqueeze(0))

    if "l0f" in blocks:
        st_ = stage_f.view(stage_f.ap[0:32, 0:128])
        p.dma("Y", st_, w["L0_ffn_b1"].rearrange("(c p) -> c p", p=128))
        p.op("T", "transpose", pb[6].view(pb[6].ap[:, 0:32]), st_, c["id_f"][0:32, 0:32],
             reads=[st_, c["id_f"]], writes=[pb[6]])
        p.op("V", "tensor_copy", c["b1_pp"], pb[6].view(pb[6].ap[:, 0:32]),
             reads=[pb[6]], writes=[c["b1_pp"]])

    if "l1a" in blocks:
        wT_prep(w["L1_attn_Wv"], scr["wvT1"], D, D)
        load_wqk("L1_attn_Wq", "L1_attn_Wk")
        build_g(scr["g1"])

    if "l1f" in blocks:
        qctr = [0]

        def quant_w(src_ap, dst_ap, rows, cols):
            nrc = rows // 128
            ncc = cols // 128
            dview = dst_ap.rearrange("(c p) o -> p c o", p=128)
            for rci in range(nrc):
                par = rci % 2
                sfb = stage_f if par == 0 else stage_f2
                sf = sfb.view(sfb.ap[:, 0:cols])
                p.dma("S", sf, src_ap[rci * 128:(rci + 1) * 128, :])
                sb_ = stage_b[par].view(stage_b[par].ap[:, 0:cols])
                for q0 in range(0, cols, 512):
                    qn = min(512, cols - q0)
                    qb_, sc_, rc_ = qsets[qctr[0] % 2]
                    qctr[0] += 1
                    emit_quant(p, c, sf[:, q0:q0 + qn], sb_[:, q0:q0 + qn],
                               qb_, sc_, rc_, qn)

                def dv(t, par=par):
                    return tstage[par].view(tstage[par].ap[:, t, :])
                pe_transpose_sb(p, c, sb_, dv, ncc)
                p.dma("Y" if par == 0 else "P",
                      dview[:, 0:ncc, rci * 128:(rci + 1) * 128],
                      tstage[par].view(tstage[par].ap[:, 0:ncc, :]))
        quant_w(w["L1_ffn_w_gate"], scr["wgT"], 2816, D)
        quant_w(w["L1_ffn_w_up"], scr["wuT"], 2816, D)
        quant_w(w["L1_ffn_w_down"], scr["wdT"], D, 2816)



def emit_quant(p, c, src, out, qb, sc, rc, F):
    nb = F // 16
    A = qb[:, 0, 0:F]
    B1 = qb[:, 1, 0:F]
    B2 = qb[:, 2, 0:F]
    scv = sc[:, 0:nb]
    rcv = rc[:, 0:nb]
    s3 = src.rearrange("p (b k) -> p b k", k=16)
    bc = lambda vv: vv.view(vv.ap.unsqueeze(2).broadcast_to([128, nb, 16]))
    p.op("V", "tensor_reduce", scv, s3, axis=AX.X, op=ALU.max,
         apply_absolute_value=True, reads=[src], writes=[sc])
    p.op("V", "tensor_scalar", scv, scv, 1.0 / 6.0, 1e-12, ALU.mult, ALU.max,
         reads=[sc], writes=[sc])
    p.op("V", "reciprocal", rcv, scv, reads=[sc], writes=[rc])
    p.op("S", "activation", A, src, AF.Abs, reads=[src], writes=[qb])
    p.op("V", "tensor_tensor", B1.rearrange("p (b k) -> p b k", k=16),
         A.rearrange("p (b k) -> p b k", k=16), bc(rcv), op=ALU.mult,
         reads=[qb, rc], writes=[qb])
    cst = c["consts"]
    p.op("V", "tensor_scalar", B2, B1, 2.0, 2.0, ALU.min, ALU.mult, reads=[qb], writes=[qb])
    p.op("S", "activation", B2, B2, AF.Identity, bias=cst[:, 6:7],
         reads=[qb, cst], writes=[qb])
    p.op("S", "activation", B2, B2, AF.Identity, scale=cst[:, 9:10], bias=cst[:, 7:8],
         reads=[qb, cst], writes=[qb])
    p.op("V", "tensor_scalar", A, B1, 2.0, 0.0, ALU.subtract, ALU.max, reads=[qb], writes=[qb])
    p.op("S", "activation", A, A, AF.Identity, bias=cst[:, 6:7],
         reads=[qb, cst], writes=[qb])
    p.op("V", "tensor_scalar", A, A, MAGIC, 2.0, ALU.subtract, ALU.min, reads=[qb], writes=[qb])
    p.op("V", "tensor_tensor", B2, B2, A, op=ALU.add, reads=[qb], writes=[qb])
    p.op("V", "tensor_scalar", A, B1, 4.5, 0.0, ALU.subtract, ALU.max, reads=[qb], writes=[qb])
    p.op("V", "tensor_scalar", A, A, 1.2, MAGIC, ALU.min, ALU.add, reads=[qb], writes=[qb])
    p.op("S", "activation", A, A, AF.Identity, scale=cst[:, 10:11], bias=cst[:, 8:9],
         reads=[qb, cst], writes=[qb])
    p.op("V", "tensor_tensor", B2, B2, A, op=ALU.add, reads=[qb], writes=[qb])
    p.op("S", "activation", A, src, AF.Sign, reads=[src], writes=[qb])
    p.op("V", "tensor_tensor", B1.rearrange("p (b k) -> p b k", k=16),
         A.rearrange("p (b k) -> p b k", k=16), bc(scv), op=ALU.mult,
         reads=[qb, sc], writes=[qb])
    p.op("V", "tensor_tensor", out, B2, B1, op=ALU.mult, reads=[qb], writes=[out])


def pe_transpose_sb(p, c, src_v, dst_view1, ncc, par=0):
    pstb = c["pstb"]
    for base in range(0, ncc, 8):
        nb = min(8, ncc - base)
        for b in range(nb):
            p.op("T", "transpose", pstb.view(pstb.ap[:, b * 128:(b + 1) * 128]),
                 src_v.view(src_v.ap[:, (base + b) * 128:(base + b + 1) * 128]), c["id_bf"],
                 reads=[src_v, c["id_bf"]], writes=[pstb])
        for b in range(nb):
            p.op("S", "copy", dst_view1(base + b),
                 pstb.view(pstb.ap[:, b * 128:(b + 1) * 128]),
                 reads=[pstb], writes=[dst_view1(base + b)])


def emit_xT(p, c, xT, src=None, bf_src=None):
    ar = c["ar"]
    mk = ar.mark()
    xbt = ar.bf16("xbt", (1024,))
    src = src if src is not None else c["xr"]
    for t in range(NT):
        if bf_src is None:
            s = xbt
            p.op("V", "tensor_copy", s, src[:, t, :], reads=[src], writes=[xbt])
        else:
            s = bf_src[:, t, :]

        def dv(cc, t=t):
            return xT.view(xT.ap[:, cc, t * 128:(t + 1) * 128])
        pe_transpose_sb(p, c, s, dv, DC)
    ar.off = mk


def emit_ln2(p, c, xsl, bA, bB, par):
    st = c["stat"]
    base = 16 + par * 8
    mu = st[:, base:base + 1]
    msq = st[:, base + 1:base + 2]
    t1 = st[:, base + 2:base + 3]
    var = st[:, base + 3:base + 4]
    rstd = st[:, base + 4:base + 5]
    nb = st[:, base + 5:base + 6]
    t2 = st[:, base + 6:base + 7]
    m2 = st[:, base + 7:base + 8]
    x0 = xsl.view(xsl.ap[:, 0:512])
    x1 = xsl.view(xsl.ap[:, 512:1024])
    p.op("S", "activation", bA, x0, AF.Square, accum_out=msq.ap,
         reads=[xsl], writes=[bA, st])
    p.op("S", "activation", bB, x1, AF.Square, accum_out=t2.ap,
         reads=[xsl], writes=[bB, st])
    p.op("S", "activation", bA, x0, AF.Identity, accum_out=mu.ap,
         reads=[xsl], writes=[bA, st])
    p.op("S", "activation", bB, x1, AF.Identity, accum_out=m2.ap,
         reads=[xsl], writes=[bB, st])
    p.op("V", "tensor_tensor", mu, mu, m2, op=ALU.add, reads=[st], writes=[st])
    p.op("V", "tensor_scalar", mu, mu, 1.0 / D, None, ALU.mult, reads=[st], writes=[st])
    p.op("V", "tensor_tensor", msq, msq, t2, op=ALU.add, reads=[st], writes=[st])
    p.op("V", "tensor_scalar", msq, msq, 1.0 / D, None, ALU.mult, reads=[st], writes=[st])
    p.op("V", "tensor_tensor", t1, mu, mu, op=ALU.mult, reads=[st], writes=[st])
    p.op("V", "tensor_tensor", var, msq, t1, op=ALU.subtract, reads=[st], writes=[st])
    p.op("S", "activation", t1, var, AF.Ln, bias=c["consts"][:, 0:1],
         reads=[st, c["consts"]], writes=[st])
    p.op("S", "activation", rstd, t1, AF.Exp, scale=c["consts"][:, 1:2],
         reads=[st, c["consts"]], writes=[st])
    p.op("V", "tensor_tensor", t1, mu, rstd, op=ALU.mult, reads=[st], writes=[st])
    p.op("V", "tensor_scalar", nb, t1, -1.0, None, ALU.mult, reads=[st], writes=[st])
    p.op("V", "tensor_scalar", xsl, xsl, rstd, nb, ALU.mult, ALU.add,
         reads=[xsl, st], writes=[xsl])



L0A_NT = NT
L0A_BAR = 0
L1F_BAR = 0
DBG_STATICS = False


def emit_l0_attn(p, c):
    ar = c["ar"]
    w = c["w"]
    scr = c["scr"]
    pb = c["pb"]
    pstb = c["pstb"]
    xr = c["xr"]
    st = c["stat"]
    sm = c["smalls"]
    sA = c["stripeA"]
    sC = c["stripeC"]
    rbf = c["rows_bf"]

    qp2 = sA.view(sA.ap[0:2, 0:2048])
    qp3 = sA.view(sA.ap[0:3, 0:2048])
    onesf = sC.view(sC.ap[0:1, 1024:1536])
    r1b = rbf.view(rbf.ap[0:1, 0:2048])
    onesb = rbf.view(rbf.ap[0:1, 2048:2176])
    wfqT = sC.view(sC.ap[0:2, 0:1024])
    btq = st[0:1, 60:62]

    xT = ar.bf16("xT", (DC, TOK))
    v_sb = ar.bf16("v_sb", (NT, D))
    g0s = ar.bf16("g0s", (DC, 1024))
    woTs = ar.bf16("woTs", (DC, 1024))
    off_wvT = (ar.mark() + 1) & ~1
    wvTs = ar.bf16("wvTs", (DC, 1024))
    xt32 = ar.at_f32(off_wvT, "xt32", (DC, 512))

    p.dma("Y", g0s, scr["g0"].rearrange("(c p) o -> p c o", p=128))
    p.dma("S", woTs, scr["woT0"].rearrange("(c p) o -> p c o", p=128))

    for g in range(4):
        for tt in range(4):
            tl = g * 4 + tt
            for cc in range(DC):
                bank = pb[3 + cc // 4]
                p.op("T", "transpose", bank.view(bank.ap[:, (cc % 4) * 128:(cc % 4 + 1) * 128]),
                     xr[:, tl, cc * 128:(cc + 1) * 128], c["id_f"],
                     reads=[xr, c["id_f"]], writes=[bank])
            for cc in range(DC):
                bank = pb[3 + cc // 4]
                p.op("V", "tensor_copy", xt32[:, cc, tt * 128:(tt + 1) * 128],
                     bank.view(bank.ap[:, (cc % 4) * 128:(cc % 4 + 1) * 128]),
                     reads=[bank], writes=[xt32])
        p.op("T", "matmul", pb[2].view(pb[2].ap[0:2, :]), btq, onesf,
             start=True, stop=False, reads=[st, sC], writes=[pb[2]])
        for cc in range(DC):
            p.op("T", "matmul", pb[2].view(pb[2].ap[0:2, :]),
                 c["wtqT"].view(c["wtqT"].ap.rearrange("p (a b) -> p a b", b=2)[:, cc, :]),
                 xt32[:, cc, :], start=False, stop=(cc == DC - 1),
                 reads=[c["wtqT"], xt32], writes=[pb[2]])
        p.op("V", "tensor_copy", qp2.view(qp2.ap[:, g * 512:(g + 1) * 512]),
             pb[2].view(pb[2].ap[0:2, :]), reads=[pb[2]], writes=[sA])

    qtv = c["qt"]
    qt32 = qtv.ap[:, 0:32].rearrange("p (t two) -> p t two", two=2)
    for t in range(NT):
        p.op("T", "transpose", pb[0].view(pb[0].ap[:, 2 * t:2 * t + 2]),
             qp2.view(qp2.ap[:, t * 128:(t + 1) * 128]), c["id_f"][0:2, 0:2],
             reads=[sA, c["id_f"]], writes=[pb[0]])
    p.op("V", "tensor_copy", qtv[:, 0:32], pb[0].view(pb[0].ap[:, 0:32]),
         reads=[pb[0]], writes=[qtv])
    a_t = qtv.view(qt32[:, :, 0])
    b_t = qtv.view(qt32[:, :, 1])
    r_t = qtv[:, 32:48]
    sq2 = st.view(st.ap[:, 16:48].rearrange("p (t two) -> p t two", two=2))
    p.op("V", "tensor_tensor", sq2, qtv.view(qt32), qtv.view(qt32), op=ALU.mult,
         reads=[qtv], writes=[st])
    p.op("V", "tensor_tensor", r_t, sq2.view(sq2.ap[:, :, 0]), sq2.view(sq2.ap[:, :, 1]),
         op=ALU.add, reads=[st], writes=[qtv])
    rstage = ar.at_f32(off_wvT, "rstage", (2048,))
    for t in range(NT):
        p.op("T", "transpose", pb[2].view(pb[2].ap[0:1, 0:128]), r_t[:, t:t + 1],
             c["id_f"], reads=[qtv, c["id_f"]], writes=[pb[2]])
        p.op("V", "tensor_copy", rstage.view(rstage.ap[0:1, t * 128:(t + 1) * 128]),
             pb[2].view(pb[2].ap[0:1, 0:128]), reads=[pb[2]], writes=[rstage])
    for hh in range(2):
        p.dma("Y", sA.view(sA.ap[2:3, hh * 1024:(hh + 1) * 1024]),
              rstage.view(rstage.ap[0:1, hh * 1024:(hh + 1) * 1024]))

    stc = st[:, 48:57]
    tmp16 = st[:, 16:32]
    p.op("V", "tensor_reduce", stc[:, 0:1], a_t, axis=AX.X, op=ALU.add, reads=[qtv], writes=[st])
    p.op("V", "tensor_reduce", stc[:, 1:2], b_t, axis=AX.X, op=ALU.add, reads=[qtv], writes=[st])
    p.op("V", "tensor_reduce", stc[:, 2:3], r_t, axis=AX.X, op=ALU.add, reads=[qtv], writes=[st])
    p.op("V", "tensor_tensor", tmp16, a_t, a_t, op=ALU.mult, reads=[qtv], writes=[st])
    p.op("V", "tensor_reduce", stc[:, 3:4], tmp16, axis=AX.X, op=ALU.add, reads=[st], writes=[st])
    p.op("V", "tensor_tensor", tmp16, b_t, b_t, op=ALU.mult, reads=[qtv], writes=[st])
    p.op("V", "tensor_reduce", stc[:, 4:5], tmp16, axis=AX.X, op=ALU.add, reads=[st], writes=[st])
    p.op("V", "tensor_tensor", tmp16, a_t, b_t, op=ALU.mult, reads=[qtv], writes=[st])
    p.op("V", "tensor_reduce", stc[:, 5:6], tmp16, axis=AX.X, op=ALU.add, reads=[st], writes=[st])
    p.op("V", "tensor_tensor", tmp16, r_t, r_t, op=ALU.mult, reads=[qtv], writes=[st])
    p.op("V", "tensor_reduce", stc[:, 6:7], tmp16, axis=AX.X, op=ALU.add, reads=[st], writes=[st])
    p.op("V", "tensor_tensor", tmp16, r_t, a_t, op=ALU.mult, reads=[qtv], writes=[st])
    p.op("V", "tensor_reduce", stc[:, 7:8], tmp16, axis=AX.X, op=ALU.add, reads=[st], writes=[st])
    p.op("V", "tensor_tensor", tmp16, r_t, b_t, op=ALU.mult, reads=[qtv], writes=[st])
    p.op("V", "tensor_reduce", stc[:, 8:9], tmp16, axis=AX.X, op=ALU.add, reads=[st], writes=[st])
    p.op("T", "matmul", pb[2].view(pb[2].ap[0:1, 0:9]), c["ones128"], stc,
         start=True, stop=True, reads=[c["ones128"], st], writes=[pb[2]])
    Srow = sm[32:33, 0:9]
    p.op("V", "tensor_copy", Srow, pb[2].view(pb[2].ap[0:1, 0:9]), reads=[pb[2]], writes=[sm])
    N = float(TOK)
    M2 = N * N

    def s_(i):
        return sm[32:33, 16 + i:17 + i]

    def S_(i):
        return sm[32:33, i:i + 1]
    for i in range(9):
        p.op("V", "tensor_scalar", S_(i), S_(i), 1.0 / N, None, ALU.mult, reads=[sm], writes=[sm])
    p.op("V", "tensor_tensor", s_(0), S_(0), S_(0), op=ALU.mult, reads=[sm], writes=[sm])
    p.op("V", "tensor_tensor", s_(1), S_(1), S_(1), op=ALU.mult, reads=[sm], writes=[sm])
    p.op("V", "tensor_tensor", s_(2), s_(0), s_(1), op=ALU.add, reads=[sm], writes=[sm])
    p.op("V", "tensor_scalar", s_(3), S_(2), 2.0, None, ALU.mult, reads=[sm], writes=[sm])
    p.op("V", "scalar_tensor_tensor", s_(4), s_(2), -2.0, s_(3), op0=ALU.mult, op1=ALU.add,
         reads=[sm], writes=[sm])
    p.op("V", "tensor_tensor", s_(5), S_(3), S_(3), op=ALU.mult, reads=[sm], writes=[sm])
    p.op("V", "tensor_tensor", s_(6), S_(4), S_(4), op=ALU.mult, reads=[sm], writes=[sm])
    p.op("V", "tensor_tensor", s_(7), S_(5), S_(5), op=ALU.mult, reads=[sm], writes=[sm])
    p.op("V", "tensor_tensor", s_(8), s_(5), s_(6), op=ALU.add, reads=[sm], writes=[sm])
    p.op("V", "scalar_tensor_tensor", s_(8), s_(7), 2.0, s_(8), op0=ALU.mult, op1=ALU.add,
         reads=[sm], writes=[sm])
    p.op("V", "tensor_tensor", s_(9), S_(0), S_(7), op=ALU.mult, reads=[sm], writes=[sm])
    p.op("V", "tensor_tensor", s_(10), S_(1), S_(8), op=ALU.mult, reads=[sm], writes=[sm])
    p.op("V", "tensor_tensor", s_(9), s_(9), s_(10), op=ALU.add, reads=[sm], writes=[sm])
    p.op("V", "tensor_tensor", s_(11), S_(2), S_(2), op=ALU.mult, reads=[sm], writes=[sm])
    p.op("V", "tensor_scalar", s_(12), S_(6), 2.0, None, ALU.mult, reads=[sm], writes=[sm])
    p.op("V", "scalar_tensor_tensor", s_(12), s_(11), 2.0, s_(12), op0=ALU.mult, op1=ALU.add,
         reads=[sm], writes=[sm])
    p.op("V", "scalar_tensor_tensor", s_(12), s_(8), 4.0, s_(12), op0=ALU.mult, op1=ALU.add,
         reads=[sm], writes=[sm])
    p.op("V", "scalar_tensor_tensor", s_(12), s_(9), -8.0, s_(12), op0=ALU.mult, op1=ALU.add,
         reads=[sm], writes=[sm])
    p.op("V", "tensor_tensor", s_(13), s_(4), s_(4), op=ALU.mult, reads=[sm], writes=[sm])
    p.op("V", "tensor_tensor", s_(13), s_(12), s_(13), op=ALU.subtract, reads=[sm], writes=[sm])
    p.op("V", "tensor_scalar", s_(13), s_(13), M2 / (M2 - 1.0), 1e-30, ALU.mult, ALU.max,
         reads=[sm], writes=[sm])
    p.op("S", "activation", s_(14), s_(13), AF.Sqrt, reads=[sm], writes=[sm])
    p.op("V", "tensor_scalar", s_(10), s_(14), 1e-20, None, ALU.max, reads=[sm], writes=[sm])
    p.op("V", "reciprocal", s_(10), s_(10), reads=[sm], writes=[sm])
    p.op("V", "tensor_tensor", s_(10), s_(13), s_(10), op=ALU.mult, reads=[sm], writes=[sm])
    p.op("V", "tensor_tensor", s_(14), s_(14), s_(10), op=ALU.add, reads=[sm], writes=[sm])
    p.op("V", "tensor_scalar", s_(14), s_(14), 0.5, None, ALU.mult, reads=[sm], writes=[sm])
    thr = sm[0:1, 140:141]
    p.op("V", "scalar_tensor_tensor", s_(15), s_(14), 1.25, s_(4),
         op0=ALU.mult, op1=ALU.add, reads=[sm], writes=[sm])
    p.op("V", "tensor_copy", thr, s_(15), reads=[sm], writes=[sm])

    p.op("V", "tensor_scalar", tmp16, r_t, 1e-24, None, ALU.max, reads=[qtv], writes=[st])
    p.op("S", "activation", tmp16, tmp16, AF.Ln, reads=[st], writes=[st])
    p.op("S", "activation", tmp16, tmp16, AF.Exp, scale=c["consts"][:, 1:2],
         reads=[st, c["consts"]], writes=[st])
    p.op("V", "tensor_scalar", c["s01"], tmp16, 0.1, None, ALU.mult,
         reads=[st], writes=[c["s01"]])
    p.op("T", "matmul", pb[2].view(pb[2].ap[:, 0:1]), sC.view(sC.ap[0:1, 1024:1152]),
         sm[0:1, 140:141], start=True, stop=True, reads=[sC, sm], writes=[pb[2]])
    p.op("V", "tensor_copy", st[:, 58:59], pb[2].view(pb[2].ap[:, 0:1]),
         reads=[pb[2]], writes=[st])

    emit_xT(p, c, xT)
    p.dma("Y", wvTs, scr["wvT0"].rearrange("(c p) o -> p c o", p=128),
          writes=[xt32, rstage])

    for t in range(NT):
        for h in range(2):
            bank = pb[5 + h]
            for cc in range(DC):
                p.op("T", "matmul", bank, xT[:, cc, t * 128:(t + 1) * 128],
                     wvTs[:, cc, h * 512:(h + 1) * 512],
                     start=(cc == 0), stop=(cc == DC - 1),
                     reads=[xT, wvTs], writes=[bank])
            p.op("S", "copy", v_sb[:, t, h * 512:(h + 1) * 512], bank,
                 reads=[bank], writes=[v_sb])

    for g in range(4):
        for cc in range(DC):
            p.op("T", "matmul", pb[2].view(pb[2].ap[0:1, :]),
                 c["w1_ppb"][:, cc:cc + 1], xT[:, cc, g * 512:(g + 1) * 512],
                 start=(cc == 0), stop=(cc == DC - 1),
                 reads=[c["w1_ppb"], xT], writes=[pb[2]])
        p.op("V", "tensor_copy", r1b.view(r1b.ap[:, g * 512:(g + 1) * 512]),
             pb[2].view(pb[2].ap[0:1, :]), reads=[pb[2]], writes=[rbf])

    p.barrier()

    o = off_wvT
    PT = ar.at_bf16(o, "PT", (NT, 128)); o += 2048
    q1 = ar.at_bf16(o, "q1", (DC, 128)); o += 1024
    ctx_sb = ar.at_bf16(o, "ctx_sb", (1024,)); o += 1024
    ctxT = ar.at_bf16(o, "ctxT", (DC, 128)); o += 1024
    Pp = [ar.at_bf16(o, "P0", (512,)), ar.at_bf16(o + 512, "P1", (512,))]; o += 1024
    t_sb = ar.at_f32(o, "t_sb", (512,)); o += 1024
    e_sb = ar.at_f32(o, "e_sb", (512,)); o += 1024
    assert o <= off_wvT + 8192

    qpm2t = st.view(st.ap[0:2, 64:192])
    qpm3t = st.view(st.ap[0:3, 64:192])
    r_t16 = c["qt"][:, 32:48]
    p.op("V", "memset", qpm3t, 1.0, writes=[st])

    for qi in range(L0A_NT):
        if L0A_BAR >= 1:
            p.barrier()
        ng = ceil_div(128 * (qi + 1), 512)
        par = qi % 2
        p.op("V", "tensor_scalar", qpm2t, qp2.view(qp2.ap[:, qi * 128:(qi + 1) * 128]),
             -2.0, None, ALU.mult, reads=[sA], writes=[st])
        for d2c in range(DC):
            bank = pb[3 + d2c // 4]
            for d1c in range(DC):
                p.op("T", "matmul", bank.view(bank.ap[:, (d2c % 4) * 128:(d2c % 4 + 1) * 128]),
                     g0s[:, d1c, d2c * 128:(d2c + 1) * 128],
                     xT[:, d1c, qi * 128:(qi + 1) * 128],
                     start=(d2c % 4 == 0 and d1c == 0),
                     stop=(d2c % 4 == 3 and d1c == DC - 1),
                     reads=[g0s, xT], writes=[bank])
        q1f = q1.view(q1.ap.rearrange("p a b -> p (a b)"))
        p.op("S", "copy", q1f.view(q1f.ap[:, 0:512]), pb[3], reads=[pb[3]], writes=[q1])
        p.op("S", "copy", q1f.view(q1f.ap[:, 512:1024]), pb[4], reads=[pb[4]], writes=[q1])

        for kg in range(ng):
            if L0A_BAR >= 2:
                p.barrier()
            sl = slice(kg * 512, (kg + 1) * 512)
            sbank = pb[kg % 2]
            p.op("T", "matmul", pb[2], qpm3t, qp3.view(qp3.ap[:, sl]),
                 start=True, stop=True, reads=[st, sA], writes=[pb[2]])
            for cc in range(DC):
                p.op("T", "matmul", sbank, q1[:, cc, :], xT[:, cc, sl],
                     start=(cc == 0), stop=False, reads=[q1, xT], writes=[sbank])
            p.op("T", "matmul", sbank, onesb.view(onesb.ap[:, 0:128]),
                 r1b.view(r1b.ap[:, sl]), start=False, stop=True,
                 reads=[rbf], writes=[sbank])
            p.op("V", "tensor_scalar", t_sb, pb[2], r_t16[:, qi:qi + 1].ap, st[:, 58:59].ap,
                 ALU.add, ALU.is_le, reads=[pb[2], c["qt"], st], writes=[t_sb])
            p.op("V", "scalar_tensor_tensor", pb[2], t_sb, 4096.0, sbank,
                 op0=ALU.mult, op1=ALU.add, reads=[t_sb, sbank], writes=[pb[2]])
            p.op("S", "activation", Pp[kg % 2], pb[2], AF.Exp,
                 bias=c["consts"][:, 4:5], scale=c["consts"][:, 2:3],
                 reads=[pb[2], c["consts"]], writes=[Pp[kg % 2]])
            if kg == ng - 1:
                dlt = qi - 4 * kg
                p.op("V", "tensor_tensor", Pp[kg % 2], Pp[kg % 2],
                     c["cmask"][:, dlt, :], op=ALU.mult,
                     reads=[Pp[kg % 2], c["cmask"]], writes=[Pp[kg % 2]])
            acc = st[:, par * 2:par * 2 + 1]
            tmp = st[:, par * 2 + 1:par * 2 + 2]
            dst = acc if kg == 0 else tmp
            p.op("V", "tensor_reduce", dst, Pp[kg % 2], axis=AX.X, op=ALU.add,
                 reads=[Pp[kg % 2]], writes=[st])
            if kg > 0:
                p.op("V", "tensor_tensor", acc, acc, tmp, op=ALU.add, reads=[st], writes=[st])
            nch = min(4, qi + 1 - 4 * kg)
            for j in range(nch):
                p.op("T", "transpose", pstb.view(pstb.ap[:, j * 128:(j + 1) * 128]),
                     Pp[kg % 2].view(Pp[kg % 2].ap[:, j * 128:(j + 1) * 128]), c["id_bf"],
                     reads=[Pp[kg % 2], c["id_bf"]], writes=[pstb])
            for j in range(nch):
                p.op("V", "tensor_copy", PT[:, 4 * kg + j, :],
                     pstb.view(pstb.ap[:, j * 128:(j + 1) * 128]),
                     reads=[pstb], writes=[PT])
        p.op("V", "reciprocal", c["crec"][:, qi:qi + 1], st[:, par * 2:par * 2 + 1],
             reads=[st], writes=[c["crec"]])
        for h in range(2):
            bank = pb[3 + h]
            for b in range(qi + 1):
                p.op("T", "matmul", bank, PT[:, b, :], v_sb[:, b, h * 512:(h + 1) * 512],
                     start=(b == 0), stop=(b == qi), reads=[PT, v_sb], writes=[bank])
        p.op("S", "copy", ctx_sb.view(ctx_sb.ap[:, 0:512]), pb[3],
             reads=[pb[3]], writes=[ctx_sb])
        p.op("S", "copy", ctx_sb.view(ctx_sb.ap[:, 512:1024]), pb[4],
             reads=[pb[4]], writes=[ctx_sb])
        for h in range(2):
            p.op("T", "matmul", pb[3 + h], qp2.view(qp2.ap[:, qi * 128:(qi + 1) * 128]),
                 wfqT.view(wfqT.ap[:, h * 512:(h + 1) * 512]), start=True, stop=True,
                 reads=[sA, sC], writes=[pb[3 + h]])
        def dvc(cc):
            return ctxT.view(ctxT.ap[:, cc, :])
        pe_transpose_sb(p, c, ctx_sb, dvc, DC)
        for h in range(2):
            bank = pb[5 + h]
            for cc in range(DC):
                p.op("T", "matmul", bank, ctxT[:, cc, :], woTs[:, cc, h * 512:(h + 1) * 512],
                     start=(cc == 0), stop=(cc == DC - 1),
                     reads=[ctxT, woTs], writes=[bank])
        for h in range(2):
            hsl = slice(h * 512, (h + 1) * 512)
            p.op("V", "scalar_tensor_tensor", e_sb, pb[3 + h], c["s01"][:, qi:qi + 1],
                 c["bc_bc"][:, hsl], op0=ALU.mult, op1=ALU.add,
                 reads=[pb[3 + h], c["s01"], c["bc_bc"]], writes=[e_sb])
            p.op("V", "scalar_tensor_tensor", pb[2], pb[5 + h], c["crec"][:, qi:qi + 1],
                 e_sb, op0=ALU.mult, op1=ALU.add,
                 reads=[pb[5 + h], c["crec"], e_sb], writes=[pb[2]])
            xsl = xr[:, qi, hsl]
            p.op("V", "scalar_tensor_tensor", xsl, xsl, 2.0, pb[2],
                 op0=ALU.mult, op1=ALU.add, reads=[xr, pb[2]], writes=[xr])
        emit_ln2(p, c, xr[:, qi, :], pb[5], pb[6], par)



def emit_l0_ffn(p, c):
    ar = c["ar"]
    w = c["w"]
    pb = c["pb"]
    xr = c["xr"]

    xT = ar.bf16("xT", (DC, TOK))
    u1 = ar.bf16("u1", (DC, 256))
    v1 = ar.bf16("v1", (2, 4096))
    u2 = ar.bf16("u2", (32, 256))
    v2 = ar.bf16("v2", (2, 1024))
    b2r = ar.f32("b2r", (1024,))
    b2_row = b2r[0:1, 0:1024]
    ones1 = c["stripeC"].view(c["stripeC"].ap[0:1, 1024:1152])
    PT1 = ar.bf16("PT1", (2, 512))
    H = ar.bf16("H", (32, 512))
    QT = ar.bf16("QT", (2, 512))

    p.dma("P", u1, w["L0_ffn_U1"].rearrange("(c p) r -> p c r", p=128))
    for rh in range(2):
        p.dma("P", v1[:, rh, :], w["L0_ffn_V1"][rh * 128:(rh + 1) * 128, :])
    u2v = w["L0_ffn_U2"].rearrange("(c p) r -> p c r", p=128)
    for gg in range(4):
        p.dma("P", u2[:, gg * 8:(gg + 1) * 8, :], u2v[:, gg * 8:(gg + 1) * 8, :])
    for rh in range(2):
        p.dma("P", v2[:, rh, :], w["L0_ffn_V2"][rh * 128:(rh + 1) * 128, :])
    p.dma("Y", b2_row, w["L0_ffn_b2"].unsqueeze(0))
    emit_xT(p, c, xT)

    for g in range(4):
        gsl = slice(g * 512, (g + 1) * 512)
        for rh in range(2):
            pw = pb[4 + rh]
            for cc in range(DC):
                p.op("T", "matmul", pw, u1[:, cc, rh * 128:(rh + 1) * 128],
                     xT[:, cc, gsl], start=(cc == 0), stop=(cc == DC - 1),
                     reads=[u1, xT], writes=[pw])
            p.op("S", "copy", PT1[:, rh, :], pw, reads=[pw], writes=[PT1])
        for fc in range(32):
            pw = pb[fc % 4]
            for rh in range(2):
                p.op("T", "matmul", pw, v1[:, rh, fc * 128:(fc + 1) * 128],
                     PT1[:, rh, :], start=(rh == 0), stop=(rh == 1),
                     reads=[v1, PT1], writes=[pw])
            p.op("S", "activation", H[:, fc, :], pw, AF.Gelu_apprx_tanh,
                 bias=c["b1_pp"][:, fc:fc + 1], reads=[pw, c["b1_pp"]], writes=[H])
        for rh in range(2):
            pw = pb[4 + rh]
            for fc in range(32):
                p.op("T", "matmul", pw, u2[:, fc, rh * 128:(rh + 1) * 128],
                     H[:, fc, :], start=(fc == 0), stop=(fc == 31),
                     reads=[u2, H], writes=[pw])
            p.op("S", "copy", QT[:, rh, :], pw, reads=[pw], writes=[QT])
        for tt in range(4):
            t = g * 4 + tt
            for h in range(2):
                pw = pb[(tt % 2) * 2 + h]
                p.op("T", "matmul", pw, ones1, b2_row.view(b2_row.ap[:, h * 512:(h + 1) * 512]),
                     start=True, stop=False, reads=[b2r, c["stripeC"]], writes=[pw])
                for rh in range(2):
                    p.op("T", "matmul", pw, QT[:, rh, tt * 128:(tt + 1) * 128],
                         v2[:, rh, h * 512:(h + 1) * 512], start=False, stop=(rh == 1),
                         reads=[QT, v2], writes=[pw])
                xsl = xr[:, t, h * 512:(h + 1) * 512]
                p.op("V", "tensor_tensor", xsl, xsl, pw, op=ALU.add,
                     reads=[xr, pw], writes=[xr])
            emit_ln2(p, c, xr[:, t, :], pb[4], pb[5], t % 2)



def emit_l1_attn(p, c):
    ar = c["ar"]
    scr = c["scr"]
    pb = c["pb"]
    pstb = c["pstb"]
    xr = c["xr"]
    st = c["stat"]

    xT = ar.bf16("xT", (DC, TOK))
    v_sb = ar.bf16("v_sb", (NT, D))
    g1s = ar.bf16("g1s", (DC, 1024))
    off_wvT = (ar.mark() + 1) & ~1
    wvTs = ar.bf16("wvT1s", (DC, 1024))

    p.dma("Y", g1s, scr["g1"].rearrange("(c p) o -> p c o", p=128))
    p.dma("S", wvTs, scr["wvT1"].rearrange("(c p) o -> p c o", p=128))

    emit_xT(p, c, xT)
    for t in range(NT):
        for h in range(2):
            bank = pb[5 + h]
            for cc in range(DC):
                p.op("T", "matmul", bank, xT[:, cc, t * 128:(t + 1) * 128],
                     wvTs[:, cc, h * 512:(h + 1) * 512],
                     start=(cc == 0), stop=(cc == DC - 1),
                     reads=[xT, wvTs], writes=[bank])
            p.op("S", "copy", v_sb[:, t, h * 512:(h + 1) * 512], bank,
                 reads=[bank], writes=[v_sb])

    p.barrier()
    o = off_wvT
    PT = ar.at_bf16(o, "PT1a", (NT, 128)); o += 2048
    q1 = ar.at_bf16(o, "q1a", (DC, 128)); o += 1024
    Pp = [ar.at_bf16(o, "P0a", (512,)), ar.at_bf16(o + 512, "P1a", (512,))]; o += 1024
    e_sb = ar.at_f32(o, "e1a", (512,)); o += 1024
    assert o <= off_wvT + 8192

    for qi in range(NT):
        par = qi % 2
        for d2c in range(DC):
            bank = pb[3 + d2c // 4]
            for d1c in range(DC):
                p.op("T", "matmul", bank.view(bank.ap[:, (d2c % 4) * 128:(d2c % 4 + 1) * 128]),
                     g1s[:, d1c, d2c * 128:(d2c + 1) * 128],
                     xT[:, d1c, qi * 128:(qi + 1) * 128],
                     start=(d2c % 4 == 0 and d1c == 0),
                     stop=(d2c % 4 == 3 and d1c == DC - 1),
                     reads=[g1s, xT], writes=[bank])
        q1f = q1.view(q1.ap.rearrange("p a b -> p (a b)"))
        p.op("S", "copy", q1f.view(q1f.ap[:, 0:512]), pb[3], reads=[pb[3]], writes=[q1])
        p.op("S", "copy", q1f.view(q1f.ap[:, 512:1024]), pb[4], reads=[pb[4]], writes=[q1])
        for kg in range(4):
            sl = slice(kg * 512, (kg + 1) * 512)
            sbank = pb[kg % 2]
            for cc in range(DC):
                p.op("T", "matmul", sbank, q1[:, cc, :], xT[:, cc, sl],
                     start=(cc == 0), stop=(cc == DC - 1), reads=[q1, xT], writes=[sbank])
            p.op("S", "activation", Pp[kg % 2], sbank, AF.Exp,
                 bias=c["consts"][:, 5:6], scale=c["consts"][:, 2:3],
                 reads=[sbank, c["consts"]], writes=[Pp[kg % 2]])
            acc = st[:, par * 2:par * 2 + 1]
            tmp = st[:, par * 2 + 1:par * 2 + 2]
            dst = acc if kg == 0 else tmp
            p.op("V", "tensor_reduce", dst, Pp[kg % 2], axis=AX.X, op=ALU.add,
                 reads=[Pp[kg % 2]], writes=[st])
            if kg > 0:
                p.op("V", "tensor_tensor", acc, acc, tmp, op=ALU.add, reads=[st], writes=[st])
            for j in range(4):
                p.op("T", "transpose", pstb.view(pstb.ap[:, j * 128:(j + 1) * 128]),
                     Pp[kg % 2].view(Pp[kg % 2].ap[:, j * 128:(j + 1) * 128]), c["id_bf"],
                     reads=[Pp[kg % 2], c["id_bf"]], writes=[pstb])
            for j in range(4):
                p.op("V", "tensor_copy", PT[:, 4 * kg + j, :],
                     pstb.view(pstb.ap[:, j * 128:(j + 1) * 128]),
                     reads=[pstb], writes=[PT])
        p.op("V", "reciprocal", c["crec"][:, qi:qi + 1], st[:, par * 2:par * 2 + 1],
             reads=[st], writes=[c["crec"]])
        for h in range(2):
            bank = pb[5 + h]
            for b in range(NT):
                p.op("T", "matmul", bank, PT[:, b, :], v_sb[:, b, h * 512:(h + 1) * 512],
                     start=(b == 0), stop=(b == NT - 1), reads=[PT, v_sb], writes=[bank])
            p.op("V", "tensor_scalar", e_sb, bank, c["crec"][:, qi:qi + 1], None,
                 ALU.mult, reads=[bank, c["crec"]], writes=[e_sb])
            xsl = xr[:, qi, h * 512:(h + 1) * 512]
            p.op("V", "tensor_tensor", xsl, xsl, e_sb, op=ALU.add, reads=[xr, e_sb], writes=[xr])
        emit_ln2(p, c, xr[:, qi, :], pb[5], pb[6], par)



def emit_l1_ffn(p, c):
    ar = c["ar"]
    scr = c["scr"]
    pb = c["pb"]
    xr = c["xr"]
    NI = 2816
    NCH = [512, 512, 512, 512, 512, 256]

    xqT = ar.bf16("xqT", (DC, TOK))
    hqT = ar.bf16("hqT", (4, 22, 128))
    wg = ar.bf16("wg", (DC, 512))
    wu = ar.bf16("wu", (DC, 512))
    wdq = [ar.bf16("wd0", (22, 256)), ar.bf16("wd1", (22, 256))]
    qtmp = ar.f32("qtmp", (3, 512))
    scq = ar.f32("scq", (32,))
    rcq = ar.f32("rcq", (32,))
    qtmp2 = ar.f32("qtmp2", (3, 512))
    scq2 = ar.f32("scq2", (32,))
    rcq2 = ar.f32("rcq2", (32,))
    qsets = [(qtmp, scq, rcq), (qtmp2, scq2, rcq2)]
    xq_t = ar.bf16("xq_t", (1024,))
    h_sb = [ar.f32("h0", (512,)), ar.f32("h1", (512,))]
    hq_c = [ar.bf16("hq0", (512,)), ar.bf16("hq1", (512,))]

    gview = scr["wgT"].rearrange("(c p) n -> p c n", p=128)
    uview = scr["wuT"].rearrange("(c p) n -> p c n", p=128)
    dview = scr["wdT"].rearrange("(c p) n -> p c n", p=128)

    for t in range(NT):
        for hh in range(2):
            qb_, sc_, rc_ = qsets[hh]
            emit_quant(p, c, xr[:, t, hh * 512:(hh + 1) * 512],
                       xq_t[:, hh * 512:(hh + 1) * 512], qb_, sc_, rc_, 512)

        def dv(cc, t=t):
            return xqT.view(xqT.ap[:, cc, t * 128:(t + 1) * 128])
        pe_transpose_sb(p, c, xq_t, dv, DC)

    if L1F_BAR >= 1:
        p.barrier()
    for Q in range(4):
        if L1F_BAR >= 2:
            p.barrier()
        tiles = [Q * 4 + i for i in range(4)]
        for ngi in range(6):
            if L1F_BAR >= 3:
                p.barrier()
            nn = NCH[ngi]
            n0 = ngi * 512
            for cc in range(DC):
                p.dma("Y", wg[:, cc, 0:nn], gview[:, cc, n0:n0 + nn])
                p.dma("P", wu[:, cc, 0:nn], uview[:, cc, n0:n0 + nn])
            for ti, t in enumerate(tiles):
                pg = pb[4] if ti % 2 == 0 else pb[2]
                pu = pb[5] if ti % 2 == 0 else pb[3]
                pgv = pg.view(pg.ap[:, 0:nn])
                puv = pu.view(pu.ap[:, 0:nn])
                for cc in range(DC):
                    p.op("T", "matmul", pgv, xqT[:, cc, t * 128:(t + 1) * 128],
                         wg[:, cc, 0:nn], start=(cc == 0), stop=(cc == DC - 1),
                         reads=[xqT, wg], writes=[pg])
                for cc in range(DC):
                    p.op("T", "matmul", puv, xqT[:, cc, t * 128:(t + 1) * 128],
                         wu[:, cc, 0:nn], start=(cc == 0), stop=(cc == DC - 1),
                         reads=[xqT, wu], writes=[pu])
                par = ti % 2
                p.op("S", "activation", h_sb[par][:, 0:nn], pgv, AF.Silu,
                     reads=[pg], writes=[h_sb[par]])
                p.op("V", "tensor_tensor", h_sb[par][:, 0:nn], h_sb[par][:, 0:nn], puv,
                     op=ALU.mult, reads=[h_sb[par], pu], writes=[h_sb[par]])
                qb_, sc_, rc_ = qsets[par]
                emit_quant(p, c, h_sb[par][:, 0:nn], hq_c[par][:, 0:nn],
                           qb_, sc_, rc_, nn)

                def dvh(cc, ti=ti, ngi=ngi, par=par):
                    return hqT.view(hqT.ap[:, ti, ngi * 4 + cc, :])
                pe_transpose_sb(p, c, hq_c[par], dvh, nn // 128)
        p.dma("S", wdq[0], dview[:, :, 0:256])
        for q4 in range(4):
            if q4 < 3:
                p.dma("S", wdq[(q4 + 1) % 2],
                      dview[:, :, (q4 + 1) * 256:(q4 + 2) * 256])
            wd = wdq[q4 % 2]
            for ti, t in enumerate(tiles):
                pw = pb[ti % 2]
                pwv = pw.view(pw.ap[:, 0:256])
                for cc in range(22):
                    p.op("T", "matmul", pwv, hqT[:, ti, cc, :], wd[:, cc, :],
                         start=(cc == 0), stop=(cc == 21), reads=[hqT, wd], writes=[pw])
                xsl = xr[:, t, q4 * 256:(q4 + 1) * 256]
                p.op("V", "tensor_tensor", xsl, xsl, pwv, op=ALU.add,
                     reads=[xr, pw], writes=[xr])
        for ti, t in enumerate(tiles):
            emit_ln2(p, c, xr[:, t, :], pb[4], pb[5], t % 2)



UNFUSED = False
_cache = {}


def _get_prog(blocks):
    key = tuple(blocks)
    if key not in _cache:
        _cache[key] = build_program(list(blocks))
    return _cache[key]


def kernel(**inputs):
    x = np.ascontiguousarray(np.asarray(inputs["L0_attn_hidden_states"], np.float32))
    B, T_, D_ = x.shape
    xc = x.reshape(N_CORES, N_ITEMS * T_, D_)
    stages = [["l0a"], ["l0f"], ["l1a"], ["l1f"]] if UNFUSED else [["l0a", "l0f", "l1a", "l1f"]]
    act = xc
    for blocks in stages:
        nc, names = _get_prog(tuple(blocks))
        in_maps = []
        for core in range(N_CORES):
            m = {"x": np.ascontiguousarray(act[core])}
            for nm in names:
                m[nm] = np.ascontiguousarray(np.asarray(inputs[nm], np.float32))
            in_maps.append(m)
        res = run_bass_kernel_spmd(nc, in_maps, core_ids=list(range(N_CORES)))
        act = np.stack([res.results[i]["y"] for i in range(N_CORES)], axis=0)
    return act.reshape(B, T_, D_)


# version 17
# speedup vs baseline: 1.0655x; 1.0249x over previous
import contextlib
import math

import numpy as np

import concourse.bass as bass
import concourse.mybir as mybir
from concourse.bass_utils import run_bass_kernel_spmd

F32 = mybir.dt.float32
BF16 = mybir.dt.bfloat16
AF = mybir.ActivationFunctionType
ALU = mybir.AluOpType
AX = mybir.AxisListType

MAGIC = 12582912.0
TOK = 2048
D = 1024
NT = TOK // 128
DC = D // 128
N_ITEMS = 4
N_CORES = 8

ENGS = ("T", "V", "S", "P", "Y")
ESEM = {e: "e" + e for e in ENGS}
DSEM = {"Y": "dY", "S": "dS", "P": "dG"}
ALLSEMS = [ESEM[e] for e in ENGS] + list(DSEM.values())

_gid = [0]


class Tok:
    __slots__ = ("sem", "gid", "need", "val", "amt")

    def __init__(self, sem, amt):
        self.sem = sem
        self.amt = amt
        self.need = False
        self.val = None
        _gid[0] += 1
        self.gid = _gid[0]


class Buf:
    __slots__ = ("w", "r", "name")

    def __init__(self, name=""):
        self.w = None
        self.r = {}
        self.name = name


class V:
    __slots__ = ("ap", "buf")

    def __init__(self, ap, buf):
        self.ap = ap
        self.buf = buf

    def __getitem__(self, idx):
        return V(self.ap[idx], self.buf)

    def rearrange(self, *a, **k):
        return V(self.ap.rearrange(*a, **k), self.buf)

    def view(self, ap):
        return V(ap, self.buf)


def carve(v, lo, hi, name):
    return V(v.ap[:, lo:hi], Buf(name))


class P:
    def __init__(self, nc):
        self.nc = nc
        self.es = contextlib.ExitStack()
        self.recs = {e: [] for e in ENGS}
        self.semh = {}
        self.last = {}
        self.qlast = {}

    def sb(self, name, shape, dt):
        h = self.es.enter_context(self.nc.sbuf_tensor(name, list(shape), dt))
        return V(h[:], Buf(name)), h

    def ps(self, name, shape, dt):
        h = self.es.enter_context(self.nc.psum_tensor(name, list(shape), dt))
        return V(h[:], Buf(name)), h

    def _deps(self, eng, reads, writes):
        deps = {}
        for v in reads:
            b = v.buf
            if b.w is not None:
                t = b.w
                if t.gid > deps.get(t.sem, (None, -1))[1]:
                    deps[t.sem] = (t, t.gid)
        for v in writes:
            b = v.buf
            cand = ([b.w] if b.w is not None else []) + list(b.r.values())
            for t in cand:
                if t.gid > deps.get(t.sem, (None, -1))[1]:
                    deps[t.sem] = (t, t.gid)
        out = []
        for sem, (t, _) in deps.items():
            if eng == "T" and sem == "eT":
                continue
            out.append(t)
        return out

    def _mark(self, eng, tok, reads, writes):
        for v in writes:
            v.buf.w = tok
            v.buf.r = {}
        for v in reads:
            prev = v.buf.r.get(tok.sem)
            if prev is None or prev.gid < tok.gid:
                v.buf.r[tok.sem] = tok

    def op(self, eng, meth, *args, reads=(), writes=(), **kw):
        for t in self._deps(eng, reads, writes):
            t.need = True
            self.recs[eng].append(("w", t))
        tok = Tok(ESEM[eng], 1)
        args = tuple(a.ap if isinstance(a, V) else a for a in args)
        kw = {k: (v.ap if isinstance(v, V) else v) for k, v in kw.items()}
        self.recs[eng].append(("op", meth, args, kw, tok))
        self.last[tok.sem] = tok
        self._mark(eng, tok, reads, writes)
        return tok

    def dma(self, eng, out, in_, reads=(), writes=(), meth="dma_start", **kw):
        reads = list(reads)
        writes = list(writes)
        if isinstance(out, V):
            writes.append(out)
        if isinstance(in_, V):
            reads.append(in_)
        for t in self._deps(eng, reads, writes):
            t.need = True
            self.recs[eng].append(("w", t))
        q = DSEM[eng]
        if q in self.qlast:
            self.recs[eng].append(("w", self.qlast[q]))
        tok = Tok(DSEM[eng], 16)
        self.qlast[q] = tok
        o = out.ap if isinstance(out, V) else out
        i = in_.ap if isinstance(in_, V) else in_
        self.recs[eng].append(("op", meth, (), {"out": o, "in_": i, **kw}, tok))
        tok.need = True
        self.last[tok.sem] = tok
        self._mark(eng, tok, reads, writes)
        return tok

    def wait_on(self, eng, toks):
        for t in toks:
            t.need = True
            self.recs[eng].append(("w", t))

    def barrier(self):
        toks = [t for t in self.last.values()]
        for e in ENGS:
            self.wait_on(e, toks)

    def finalize_and_replay(self, block):
        cnt = {s: 0 for s in ALLSEMS}
        for e in ENGS:
            for r in self.recs[e]:
                if r[0] == "op":
                    tok = r[-1]
                    if tok.need:
                        cnt[tok.sem] += tok.amt
                        tok.val = cnt[tok.sem]
        for s in ALLSEMS:
            self.semh[s] = self.es.enter_context(self.nc.semaphore(name="sem_" + s))
        secs = {"T": block.tensor, "V": block.vector, "S": block.scalar,
                "P": block.gpsimd, "Y": block.sync}
        for e in ENGS:
            recs = self.recs[e]
            if not recs:
                continue
            hwm = {}

            def run(eng_h, recs=recs, hwm=hwm):
                for r in recs:
                    if r[0] == "w":
                        t = r[1]
                        if t.val is None:
                            continue
                        if hwm.get(t.sem, 0) >= t.val:
                            continue
                        eng_h.wait_ge(self.semh[t.sem], t.val)
                        hwm[t.sem] = t.val
                    else:
                        _, meth, args, kw, tok = r
                        inst = getattr(eng_h, meth)(*args, **kw)
                        if tok.need:
                            inst.then_inc(self.semh[tok.sem], tok.amt)
            secs[e](run)


class Arena:

    def __init__(self, p, name, kb):
        self.p = p
        self.n = kb * 512
        v, h = p.sb(name, [128, self.n], BF16)
        self.t = h
        self.off = 0

    def _take(self, n_bf16, name):
        off = (self.off + 1) & ~1
        assert off + n_bf16 <= self.n, f"arena overflow at {name}: {off}+{n_bf16}>{self.n}"
        self.off = off + n_bf16
        return off

    @staticmethod
    def _shape(ap, shape):
        if len(shape) == 2:
            return ap.rearrange("p (a b) -> p a b", b=shape[-1])
        if len(shape) == 3:
            return ap.rearrange("p (a b c) -> p a b c", b=shape[1], c=shape[2])
        return ap

    def bf16(self, name, shape):
        n = int(np.prod(shape))
        off = self._take(n, name)
        return V(self._shape(self.t[:, off:off + n], shape), Buf(name))

    def f32(self, name, shape):
        n = int(np.prod(shape))
        off = self._take(2 * n, name)
        return V(self._shape(self.t[:, off:off + 2 * n].bitcast(F32), shape), Buf(name))

    def at_bf16(self, off, name, shape):
        n = int(np.prod(shape))
        assert off + n <= self.n
        return V(self._shape(self.t[:, off:off + n], shape), Buf(name))

    def at_f32(self, off, name, shape):
        n = int(np.prod(shape))
        assert off + 2 * n <= self.n
        return V(self._shape(self.t[:, off:off + 2 * n].bitcast(F32), shape), Buf(name))

    def mark(self):
        return self.off

    def reset(self, to=None):
        self.p.barrier()
        self.off = 0 if to is None else to


def ceil_div(a, b):
    return (a + b - 1) // b



WNAMES = {
    "l0a": ["L0_attn_Wq", "L0_attn_bq", "L0_attn_Wk", "L0_attn_bk", "L0_attn_Wv",
            "L0_attn_bv", "L0_attn_Wo", "L0_attn_bo", "L0_attn_Wtq", "L0_attn_btq",
            "L0_attn_Wfq", "L0_attn_bfq"],
    "l0f": ["L0_ffn_U1", "L0_ffn_V1", "L0_ffn_b1", "L0_ffn_U2", "L0_ffn_V2", "L0_ffn_b2"],
    "l1a": ["L1_attn_Wq", "L1_attn_Wk", "L1_attn_Wv"],
    "l1f": ["L1_ffn_w_gate", "L1_ffn_w_up", "L1_ffn_w_down"],
}

WSHAPES = {"L0_attn_Wq": [D, D], "L0_attn_Wk": [D, D], "L0_attn_Wv": [D, D],
           "L0_attn_Wo": [D, D], "L0_attn_Wtq": [2, D], "L0_attn_Wfq": [D, 2],
           "L0_attn_bq": [D], "L0_attn_bk": [D], "L0_attn_bv": [D],
           "L0_attn_bo": [D], "L0_attn_btq": [2], "L0_attn_bfq": [D],
           "L0_ffn_U1": [D, 256], "L0_ffn_V1": [256, 4096], "L0_ffn_b1": [4096],
           "L0_ffn_U2": [4096, 256], "L0_ffn_V2": [256, D], "L0_ffn_b2": [D],
           "L1_attn_Wq": [D, D], "L1_attn_Wk": [D, D], "L1_attn_Wv": [D, D],
           "L1_ffn_w_gate": [2816, D], "L1_ffn_w_up": [2816, D],
           "L1_ffn_w_down": [D, 2816]}


def build_program(blocks, n_items=N_ITEMS):
    nc = bass.Bass("TRN2", target_bir_lowering=False)
    p = P(nc)
    c = {}

    x_in = nc.dram_tensor("x", [n_items * TOK, D], F32, kind="ExternalInput").ap()
    y_out = nc.dram_tensor("y", [n_items * TOK, D], F32, kind="ExternalOutput").ap()
    names = []
    w = {}
    for b in blocks:
        names.extend(WNAMES[b])
    for nm in names:
        w[nm] = nc.dram_tensor(nm, WSHAPES[nm], F32, kind="ExternalInput").ap()
    c["w"] = w

    scr = {}
    if "l0a" in blocks:
        scr["g0"] = nc.dram_tensor("s_g0", [D, D], BF16, kind="Internal").ap()
        scr["wvT0"] = nc.dram_tensor("s_wvT0", [D, D], BF16, kind="Internal").ap()
        scr["woT0"] = nc.dram_tensor("s_woT0", [D, D], BF16, kind="Internal").ap()
    if "l1a" in blocks:
        scr["g1"] = nc.dram_tensor("s_g1", [D, D], BF16, kind="Internal").ap()
        scr["wvT1"] = nc.dram_tensor("s_wvT1", [D, D], BF16, kind="Internal").ap()
    if "l1f" in blocks:
        scr["wgT"] = nc.dram_tensor("s_wgT", [D, 2816], BF16, kind="Internal").ap()
        scr["wuT"] = nc.dram_tensor("s_wuT", [D, 2816], BF16, kind="Internal").ap()
        scr["wdT"] = nc.dram_tensor("s_wdT", [2816, D], BF16, kind="Internal").ap()
    c["scr"] = scr

    c["xr"], _ = p.sb("xr", [128, NT, D], F32)
    c["id_bf"], _ = p.sb("id_bf", [128, 128], BF16)
    c["id_f"], _ = p.sb("id_f", [128, 128], F32)
    c["stat"], _ = p.sb("stat", [128, 192], F32)
    c["qt"], _ = p.sb("qt", [128, 48], F32)
    c["crec"], _ = p.sb("crec", [128, NT], F32)
    c["s01"], _ = p.sb("s01", [128, NT], F32)
    c["consts"], _ = p.sb("consts", [128, 12], F32)
    c["smalls"], _ = p.sb("smalls", [128, 160], F32)
    c["ones128"], _ = p.sb("ones128", [128, 1], F32)
    c["b1_pp"], _ = p.sb("b1_pp", [128, 32], F32)
    c["wtqT"], _ = p.sb("wtqT", [128, 16], F32)
    c["wfq_pp"], _ = p.sb("wfq_pp", [128, 16], F32)
    c["bq_pp"], _ = p.sb("bq_pp", [128, 8], F32)
    c["w1_ppb"], _ = p.sb("w1_ppb", [128, 8], BF16)
    c["bv_ppb"], _ = p.sb("bv_ppb", [128, 8], BF16)
    c["stripeA"], _ = p.sb("stripeA", [128, 2048], F32)
    c["stripeC"], _ = p.sb("stripeC", [128, 1536], F32)
    c["rows_bf"], _ = p.sb("rows_bf", [128, 2176], BF16)
    c["bc_bc"], _ = p.sb("bc_bc", [128, 1024], F32)
    c["cmask"], _ = p.sb("cmask", [128, 4, 512], BF16)

    c["pb"] = []
    for i in range(7):
        v, _ = p.ps("pb%d" % i, [128, 512], F32)
        c["pb"].append(v)
    c["pstb"], _ = p.ps("pstb", [128, 1024], BF16)

    ar = Arena(p, "arena", 114)
    c["ar"] = ar
    c["p"] = p
    c["nc"] = nc

    emit_prep(p, c, blocks)
    ar.reset()

    for item in range(n_items):
        xs = x_in[item * TOK:(item + 1) * TOK, :].rearrange("(t p) d -> p t d", p=128)
        p.dma("Y", c["xr"], xs)
        for b in blocks:
            if b == "l0a":
                emit_l0_attn(p, c)
            elif b == "l0f":
                emit_l0_ffn(p, c)
            elif b == "l1a":
                emit_l1_attn(p, c)
            elif b == "l1f":
                emit_l1_ffn(p, c)
            ar.reset()
        ys = y_out[item * TOK:(item + 1) * TOK, :].rearrange("(t p) d -> p t d", p=128)
        p.dma("Y", ys, c["xr"], writes=())
        ar.reset()

    if DBG_STATICS:
        aap = nc.dram_tensor("dbg_arena", [128, ar.n], BF16, kind="ExternalOutput").ap()
        p.dma("Y", aap, V(ar.t[:], Buf("arena_dump")), writes=())
        for nm in ("stat", "smalls", "stripeA", "crec", "s01", "qt", "rows_bf", "bc_bc"):
            v = c[nm]
            shp = list(v.ap.shape)
            dt = v.ap.dtype
            dap = nc.dram_tensor("dbg_" + nm, shp, dt, kind="ExternalOutput").ap()
            p.dma("Y", dap, v, writes=())
    p.barrier()
    with nc.Block() as block:
        p.finalize_and_replay(block)
    p.es.close()
    return nc, names



def emit_prep(p, c, blocks):
    ar = c["ar"]
    w = c["w"]
    scr = c["scr"]
    pb = c["pb"]
    sA = c["stripeA"]
    sC = c["stripeC"]

    p.op("P", "memset", c["smalls"], 1.0, writes=[c["smalls"]])
    p.op("P", "affine_select", c["id_f"], c["smalls"][:, 0:128], pattern=[[1, 128]],
         compare_op=ALU.is_equal, fill=0.0, base=0, channel_multiplier=-1,
         reads=[c["smalls"]], writes=[c["id_f"]])
    p.op("V", "tensor_copy", c["id_bf"], c["id_f"], reads=[c["id_f"]], writes=[c["id_bf"]])
    p.op("V", "memset", c["ones128"], 1.0, writes=[c["ones128"]])
    for i, val in ((0, 1e-5), (1, -0.5), (2, 1.0 / 32.0), (3, 1.0),
                   (4, -128.0), (5, -32.0), (6, MAGIC), (7, -MAGIC * 0.5),
                   (8, -2.0 * MAGIC), (9, 0.5), (10, 2.0)):
        p.op("V", "memset", c["consts"][:, i:i + 1], val, writes=[c["consts"]])
    p.op("V", "memset", sC.view(sC.ap[0:1, 1024:1536]), 1.0, writes=[sC])
    p.op("V", "memset", c["rows_bf"].view(c["rows_bf"].ap[0:1, 2048:2176]), 1.0,
         writes=[c["rows_bf"]])
    p.op("V", "memset", c["cmask"], 1.0, writes=[c["cmask"]])
    for d in range(4):
        p.op("P", "affine_select", c["cmask"][:, d, :], c["cmask"][:, d, :],
             pattern=[[-1, 512]], compare_op=ALU.is_ge, fill=0.0, base=d * 128,
             channel_multiplier=1, reads=[c["cmask"]], writes=[c["cmask"]])

    stage_f = ar.f32("prep_stage_f", (2816,))
    stage_f2 = ar.f32("prep_stage_f2", (2816,))
    stage_b = [ar.bf16("prep_stage_b0", (2816,)), ar.bf16("prep_stage_b1", (2816,))]
    tstage = [ar.bf16("prep_tstage0", (22, 128)), ar.bf16("prep_tstage1", (22, 128))]
    qsets = []
    for qsi in range(2):
        qsets.append((ar.f32("prep_qb%d" % qsi, (3, 512)),
                      ar.f32("prep_sc%d" % qsi, (32,)),
                      ar.f32("prep_rc%d" % qsi, (32,))))
    qb, sc, rc = qsets[0]
    wqb = ar.bf16("prep_wqb", (DC, 1024))
    wkb = ar.bf16("prep_wkb", (DC, 1024))
    gstage = ar.bf16("prep_gstage", (1024,))

    def wT_prep(src_ap, dst_ap, rows, cols):
        nrc = rows // 128
        ncc = cols // 128
        dview = dst_ap.rearrange("(c p) o -> p c o", p=128)
        for rci in range(nrc):
            par = rci % 2
            st_ = stage_b[par].view(stage_b[par].ap[:, 0:cols])
            p.dma("P", st_, src_ap[rci * 128:(rci + 1) * 128, :])

            def dv(cc, par=par):
                return tstage[par].view(tstage[par].ap[:, cc, :])
            pe_transpose_sb(p, c, st_, dv, ncc)
            p.dma("Y", dview[:, 0:ncc, rci * 128:(rci + 1) * 128],
                  tstage[par].view(tstage[par].ap[:, 0:ncc, :]))

    def load_wqk(nmq, nmk):
        for rc_ in range(DC):
            p.dma("P", wqb[:, rc_, :], w[nmq][rc_ * 128:(rc_ + 1) * 128, :])
            p.dma("P", wkb[:, rc_, :], w[nmk][rc_ * 128:(rc_ + 1) * 128, :])

    def build_g(dst_ap):
        gview = dst_ap.rearrange("(c p) o -> p c o", p=128)
        for d1c in range(DC):
            for h in range(2):
                bank = pb[(d1c % 2) * 2 + h]
                for oc in range(DC):
                    p.op("T", "matmul", bank, wqb[:, oc, d1c * 128:(d1c + 1) * 128],
                         wkb[:, oc, h * 512:(h + 1) * 512],
                         start=(oc == 0), stop=(oc == DC - 1),
                         reads=[wqb, wkb], writes=[bank])
                p.op("V", "tensor_copy", gstage[:, h * 512:(h + 1) * 512], bank,
                     reads=[bank], writes=[gstage])
            p.dma("Y", gview[:, d1c, :], gstage)

    def row_to_pp(src_row, pp_f32, pp_bf):
        for oc in range(DC):
            p.op("T", "transpose", pb[6].view(pb[6].ap[:, oc:oc + 1]),
                 src_row.view(src_row.ap[0:1, oc * 128:(oc + 1) * 128]),
                 c["id_f"][0:1, 0:1], reads=[src_row, c["id_f"]], writes=[pb[6]])
        if pp_f32 is not None:
            p.op("V", "tensor_copy", pp_f32, pb[6].view(pb[6].ap[:, 0:8]),
                 reads=[pb[6]], writes=[pp_f32])
        p.op("V", "tensor_copy", pp_bf, pb[6].view(pb[6].ap[:, 0:8]),
             reads=[pb[6]], writes=[pp_bf])

    if "l0a" in blocks:
        wT_prep(w["L0_attn_Wv"], scr["wvT0"], D, D)
        wT_prep(w["L0_attn_Wo"], scr["woT0"], D, D)
        st_ = stage_f.view(stage_f.ap[0:8, 0:128])
        p.dma("Y", st_, w["L0_attn_bq"].rearrange("(c p) -> c p", p=128))
        p.op("T", "transpose", pb[6].view(pb[6].ap[:, 0:8]), st_, c["id_f"][0:8, 0:8],
             reads=[st_, c["id_f"]], writes=[pb[6]])
        p.op("V", "tensor_copy", c["bq_pp"], pb[6].view(pb[6].ap[:, 0:8]),
             reads=[pb[6]], writes=[c["bq_pp"]])
        p.dma("Y", st_, w["L0_attn_bv"].rearrange("(c p) -> c p", p=128))
        p.op("T", "transpose", pb[6].view(pb[6].ap[:, 0:8]), st_, c["id_f"][0:8, 0:8],
             reads=[st_, c["id_f"]], writes=[pb[6]])
        p.op("V", "tensor_copy", c["bv_ppb"], pb[6].view(pb[6].ap[:, 0:8]),
             reads=[pb[6]], writes=[c["bv_ppb"]])
        load_wqk("L0_attn_Wq", "L0_attn_Wk")
        build_g(scr["g0"])
        bqb = ar.bf16("prep_bqb", (8,))
        p.op("V", "tensor_copy", bqb, c["bq_pp"], reads=[c["bq_pp"]], writes=[bqb])
        for h in range(2):
            bank = pb[4 + h]
            for oc in range(DC):
                p.op("T", "matmul", bank.view(bank.ap[0:1, :]),
                     bqb[:, oc:oc + 1],
                     wkb[:, oc, h * 512:(h + 1) * 512],
                     start=(oc == 0), stop=(oc == DC - 1),
                     reads=[bqb, wkb], writes=[bank])
            p.op("V", "tensor_copy", sA.view(sA.ap[0:1, h * 512:(h + 1) * 512]),
                 bank.view(bank.ap[0:1, :]), reads=[bank], writes=[sA])
        row_to_pp(sA.view(sA.ap[0:1, 0:1024]), None, c["w1_ppb"])
        woTsl = ar.bf16("prep_woTsl", (DC, 1024))
        p.dma("Y", woTsl, scr["woT0"].rearrange("(c p) o -> p c o", p=128),
              writes=[tstage[0], tstage[1]])
        for h in range(2):
            bank = pb[4 + h]
            for oc in range(DC):
                p.op("T", "matmul", bank.view(bank.ap[0:1, :]),
                     c["bv_ppb"][:, oc:oc + 1], woTsl[:, oc, h * 512:(h + 1) * 512],
                     start=(oc == 0), stop=(oc == DC - 1),
                     reads=[c["bv_ppb"], woTsl], writes=[bank])
        p.dma("Y", sA.view(sA.ap[64:65, 0:1024]), w["L0_attn_bfq"].unsqueeze(0))
        p.dma("Y", sA.view(sA.ap[64:65, 1024:2048]), w["L0_attn_bo"].unsqueeze(0))
        p.op("V", "scalar_tensor_tensor", sA.view(sA.ap[0:1, 0:1024]),
             sA.view(sA.ap[64:65, 0:1024]), 0.1, sA.view(sA.ap[64:65, 1024:2048]),
             op0=ALU.mult, op1=ALU.add, reads=[sA], writes=[sA])
        for h in range(2):
            p.op("V", "tensor_tensor", sA.view(sA.ap[0:1, h * 512:(h + 1) * 512]),
                 sA.view(sA.ap[0:1, h * 512:(h + 1) * 512]),
                 pb[4 + h].view(pb[4 + h].ap[0:1, :]), op=ALU.add,
                 reads=[sA, pb[4 + h]], writes=[sA])
        for h in range(2):
            bank = pb[h]
            p.op("T", "matmul", bank, sC.view(sC.ap[0:1, 1024:1152]),
                 sA.view(sA.ap[0:1, h * 512:(h + 1) * 512]), start=True, stop=True,
                 reads=[sC, sA], writes=[bank])
            p.op("V", "tensor_copy", c["bc_bc"][:, h * 512:(h + 1) * 512], bank,
                 reads=[bank], writes=[c["bc_bc"]])
        st2 = stage_f.view(stage_f.ap[0:2, 0:1024])
        p.dma("Y", st2, w["L0_attn_Wtq"])
        for cc in range(DC):
            p.op("T", "transpose", pb[6].view(pb[6].ap[:, 2 * cc:2 * cc + 2]),
                 st2.view(st2.ap[:, cc * 128:(cc + 1) * 128]), c["id_f"][0:2, 0:2],
                 reads=[st2, c["id_f"]], writes=[pb[6]])
        p.op("V", "tensor_copy", c["wtqT"], pb[6].view(pb[6].ap[:, 0:16]),
             reads=[pb[6]], writes=[c["wtqT"]])
        p.dma("Y", c["wfq_pp"].rearrange("p (a b) -> p a b", b=2),
              w["L0_attn_Wfq"].rearrange("(c p) t -> p c t", p=128),
              allow_slow_non_contiguous=True)
        for cc in range(DC):
            p.op("T", "transpose", pb[6].view(pb[6].ap[0:2, 0:128]),
                 c["wfq_pp"].view(c["wfq_pp"].ap.rearrange("p (a b) -> p a b", b=2)[:, cc, :]),
                 c["id_f"], reads=[c["wfq_pp"], c["id_f"]], writes=[pb[6]])
            p.op("V", "tensor_copy", sC.view(sC.ap[0:2, cc * 128:(cc + 1) * 128]),
                 pb[6].view(pb[6].ap[0:2, 0:128]), reads=[pb[6]], writes=[sC])
        p.dma("Y", c["stat"].view(c["stat"].ap[0:1, 60:62]), w["L0_attn_btq"].unsqueeze(0))

    if "l0f" in blocks:
        st_ = stage_f.view(stage_f.ap[0:32, 0:128])
        p.dma("Y", st_, w["L0_ffn_b1"].rearrange("(c p) -> c p", p=128))
        p.op("T", "transpose", pb[6].view(pb[6].ap[:, 0:32]), st_, c["id_f"][0:32, 0:32],
             reads=[st_, c["id_f"]], writes=[pb[6]])
        p.op("V", "tensor_copy", c["b1_pp"], pb[6].view(pb[6].ap[:, 0:32]),
             reads=[pb[6]], writes=[c["b1_pp"]])

    if "l1a" in blocks:
        wT_prep(w["L1_attn_Wv"], scr["wvT1"], D, D)
        load_wqk("L1_attn_Wq", "L1_attn_Wk")
        build_g(scr["g1"])

    if "l1f" in blocks:
        qctr = [0]

        def quant_w(src_ap, dst_ap, rows, cols):
            nrc = rows // 128
            ncc = cols // 128
            dview = dst_ap.rearrange("(c p) o -> p c o", p=128)
            for rci in range(nrc):
                par = rci % 2
                sfb = stage_f if par == 0 else stage_f2
                sf = sfb.view(sfb.ap[:, 0:cols])
                p.dma("S", sf, src_ap[rci * 128:(rci + 1) * 128, :])
                sb_ = stage_b[par].view(stage_b[par].ap[:, 0:cols])
                for q0 in range(0, cols, 512):
                    qn = min(512, cols - q0)
                    qb_, sc_, rc_ = qsets[qctr[0] % 2]
                    qctr[0] += 1
                    emit_quant(p, c, sf[:, q0:q0 + qn], sb_[:, q0:q0 + qn],
                               qb_, sc_, rc_, qn)

                def dv(t, par=par):
                    return tstage[par].view(tstage[par].ap[:, t, :])
                pe_transpose_sb(p, c, sb_, dv, ncc)
                p.dma("Y" if par == 0 else "P",
                      dview[:, 0:ncc, rci * 128:(rci + 1) * 128],
                      tstage[par].view(tstage[par].ap[:, 0:ncc, :]))
        quant_w(w["L1_ffn_w_gate"], scr["wgT"], 2816, D)
        quant_w(w["L1_ffn_w_up"], scr["wuT"], 2816, D)
        quant_w(w["L1_ffn_w_down"], scr["wdT"], D, 2816)



def emit_quant(p, c, src, out, qb, sc, rc, F):
    nb = F // 16
    A = qb[:, 0, 0:F]
    B1 = qb[:, 1, 0:F]
    B2 = qb[:, 2, 0:F]
    scv = sc[:, 0:nb]
    rcv = rc[:, 0:nb]
    s3 = src.rearrange("p (b k) -> p b k", k=16)
    bc = lambda vv: vv.view(vv.ap.unsqueeze(2).broadcast_to([128, nb, 16]))
    p.op("V", "tensor_reduce", scv, s3, axis=AX.X, op=ALU.max,
         apply_absolute_value=True, reads=[src], writes=[sc])
    p.op("V", "tensor_scalar", scv, scv, 1.0 / 6.0, 1e-12, ALU.mult, ALU.max,
         reads=[sc], writes=[sc])
    p.op("V", "reciprocal", rcv, scv, reads=[sc], writes=[rc])
    p.op("S", "activation", A, src, AF.Abs, reads=[src], writes=[qb])
    p.op("V", "tensor_tensor", B1.rearrange("p (b k) -> p b k", k=16),
         A.rearrange("p (b k) -> p b k", k=16), bc(rcv), op=ALU.mult,
         reads=[qb, rc], writes=[qb])
    cst = c["consts"]
    p.op("V", "tensor_scalar", B2, B1, 2.0, 2.0, ALU.min, ALU.mult, reads=[qb], writes=[qb])
    p.op("S", "activation", B2, B2, AF.Identity, bias=cst[:, 6:7],
         reads=[qb, cst], writes=[qb])
    p.op("S", "activation", B2, B2, AF.Identity, scale=cst[:, 9:10], bias=cst[:, 7:8],
         reads=[qb, cst], writes=[qb])
    p.op("V", "tensor_scalar", A, B1, 2.0, 0.0, ALU.subtract, ALU.max, reads=[qb], writes=[qb])
    p.op("S", "activation", A, A, AF.Identity, bias=cst[:, 6:7],
         reads=[qb, cst], writes=[qb])
    p.op("V", "tensor_scalar", A, A, MAGIC, 2.0, ALU.subtract, ALU.min, reads=[qb], writes=[qb])
    p.op("V", "tensor_tensor", B2, B2, A, op=ALU.add, reads=[qb], writes=[qb])
    p.op("V", "tensor_scalar", A, B1, 4.5, 0.0, ALU.subtract, ALU.max, reads=[qb], writes=[qb])
    p.op("V", "tensor_scalar", A, A, 1.2, MAGIC, ALU.min, ALU.add, reads=[qb], writes=[qb])
    p.op("S", "activation", A, A, AF.Identity, scale=cst[:, 10:11], bias=cst[:, 8:9],
         reads=[qb, cst], writes=[qb])
    p.op("V", "tensor_tensor", B2, B2, A, op=ALU.add, reads=[qb], writes=[qb])
    p.op("S", "activation", A, src, AF.Sign, reads=[src], writes=[qb])
    p.op("V", "tensor_tensor", B1.rearrange("p (b k) -> p b k", k=16),
         A.rearrange("p (b k) -> p b k", k=16), bc(scv), op=ALU.mult,
         reads=[qb, sc], writes=[qb])
    p.op("V", "tensor_tensor", out, B2, B1, op=ALU.mult, reads=[qb], writes=[out])


def pe_transpose_sb(p, c, src_v, dst_view1, ncc, par=0):
    pstb = c["pstb"]
    for base in range(0, ncc, 8):
        nb = min(8, ncc - base)
        for b in range(nb):
            p.op("T", "transpose", pstb.view(pstb.ap[:, b * 128:(b + 1) * 128]),
                 src_v.view(src_v.ap[:, (base + b) * 128:(base + b + 1) * 128]), c["id_bf"],
                 reads=[src_v, c["id_bf"]], writes=[pstb])
        for b in range(nb):
            p.op("S", "copy", dst_view1(base + b),
                 pstb.view(pstb.ap[:, b * 128:(b + 1) * 128]),
                 reads=[pstb], writes=[dst_view1(base + b)])


def emit_xT(p, c, xT, src=None, bf_src=None):
    ar = c["ar"]
    mk = ar.mark()
    xbt = ar.bf16("xbt", (1024,))
    src = src if src is not None else c["xr"]
    for t in range(NT):
        if bf_src is None:
            s = xbt
            p.op("V", "tensor_copy", s, src[:, t, :], reads=[src], writes=[xbt])
        else:
            s = bf_src[:, t, :]

        def dv(cc, t=t):
            return xT.view(xT.ap[:, cc, t * 128:(t + 1) * 128])
        pe_transpose_sb(p, c, s, dv, DC)
    ar.off = mk


def emit_ln2(p, c, xsl, bA, bB, par):
    st = c["stat"]
    base = 16 + par * 8
    mu = st[:, base:base + 1]
    msq = st[:, base + 1:base + 2]
    t1 = st[:, base + 2:base + 3]
    var = st[:, base + 3:base + 4]
    rstd = st[:, base + 4:base + 5]
    nb = st[:, base + 5:base + 6]
    t2 = st[:, base + 6:base + 7]
    m2 = st[:, base + 7:base + 8]
    x0 = xsl.view(xsl.ap[:, 0:512])
    x1 = xsl.view(xsl.ap[:, 512:1024])
    p.op("S", "activation", bA, x0, AF.Square, accum_out=msq.ap,
         reads=[xsl], writes=[bA, st])
    p.op("S", "activation", bB, x1, AF.Square, accum_out=t2.ap,
         reads=[xsl], writes=[bB, st])
    p.op("S", "activation", bA, x0, AF.Identity, accum_out=mu.ap,
         reads=[xsl], writes=[bA, st])
    p.op("S", "activation", bB, x1, AF.Identity, accum_out=m2.ap,
         reads=[xsl], writes=[bB, st])
    p.op("V", "tensor_tensor", mu, mu, m2, op=ALU.add, reads=[st], writes=[st])
    p.op("V", "tensor_scalar", mu, mu, 1.0 / D, None, ALU.mult, reads=[st], writes=[st])
    p.op("V", "tensor_tensor", msq, msq, t2, op=ALU.add, reads=[st], writes=[st])
    p.op("V", "tensor_scalar", msq, msq, 1.0 / D, None, ALU.mult, reads=[st], writes=[st])
    p.op("V", "tensor_tensor", t1, mu, mu, op=ALU.mult, reads=[st], writes=[st])
    p.op("V", "tensor_tensor", var, msq, t1, op=ALU.subtract, reads=[st], writes=[st])
    p.op("S", "activation", t1, var, AF.Ln, bias=c["consts"][:, 0:1],
         reads=[st, c["consts"]], writes=[st])
    p.op("S", "activation", rstd, t1, AF.Exp, scale=c["consts"][:, 1:2],
         reads=[st, c["consts"]], writes=[st])
    p.op("V", "tensor_tensor", t1, mu, rstd, op=ALU.mult, reads=[st], writes=[st])
    p.op("V", "tensor_scalar", nb, t1, -1.0, None, ALU.mult, reads=[st], writes=[st])
    p.op("V", "tensor_scalar", xsl, xsl, rstd, nb, ALU.mult, ALU.add,
         reads=[xsl, st], writes=[xsl])



L0A_NT = NT
L0A_BAR = 0
L1F_BAR = 0
DBG_STATICS = False


def emit_l0_attn(p, c):
    ar = c["ar"]
    w = c["w"]
    scr = c["scr"]
    pb = c["pb"]
    pstb = c["pstb"]
    xr = c["xr"]
    st = c["stat"]
    sm = c["smalls"]
    sA = c["stripeA"]
    sC = c["stripeC"]
    rbf = c["rows_bf"]

    qp2 = sA.view(sA.ap[0:2, 0:2048])
    qp3 = sA.view(sA.ap[0:3, 0:2048])
    onesf = sC.view(sC.ap[0:1, 1024:1536])
    r1b = rbf.view(rbf.ap[0:1, 0:2048])
    onesb = rbf.view(rbf.ap[0:1, 2048:2176])
    wfqT = sC.view(sC.ap[0:2, 0:1024])
    btq = st[0:1, 60:62]

    xT = ar.bf16("xT", (DC, TOK))
    v_sb = ar.bf16("v_sb", (NT, D))
    g0s = ar.bf16("g0s", (DC, 1024))
    woTs = ar.bf16("woTs", (DC, 1024))
    off_wvT = (ar.mark() + 1) & ~1
    wvTs = ar.bf16("wvTs", (DC, 1024))
    xt32 = ar.at_f32(off_wvT, "xt32", (DC, 512))

    p.dma("Y", g0s, scr["g0"].rearrange("(c p) o -> p c o", p=128))
    p.dma("S", woTs, scr["woT0"].rearrange("(c p) o -> p c o", p=128))

    for g in range(4):
        for tt in range(4):
            tl = g * 4 + tt
            for cc in range(DC):
                bank = pb[3 + cc // 4]
                p.op("T", "transpose", bank.view(bank.ap[:, (cc % 4) * 128:(cc % 4 + 1) * 128]),
                     xr[:, tl, cc * 128:(cc + 1) * 128], c["id_f"],
                     reads=[xr, c["id_f"]], writes=[bank])
            for cc in range(DC):
                bank = pb[3 + cc // 4]
                p.op("V", "tensor_copy", xt32[:, cc, tt * 128:(tt + 1) * 128],
                     bank.view(bank.ap[:, (cc % 4) * 128:(cc % 4 + 1) * 128]),
                     reads=[bank], writes=[xt32])
        p.op("T", "matmul", pb[2].view(pb[2].ap[0:2, :]), btq, onesf,
             start=True, stop=False, reads=[st, sC], writes=[pb[2]])
        for cc in range(DC):
            p.op("T", "matmul", pb[2].view(pb[2].ap[0:2, :]),
                 c["wtqT"].view(c["wtqT"].ap.rearrange("p (a b) -> p a b", b=2)[:, cc, :]),
                 xt32[:, cc, :], start=False, stop=(cc == DC - 1),
                 reads=[c["wtqT"], xt32], writes=[pb[2]])
        p.op("V", "tensor_copy", qp2.view(qp2.ap[:, g * 512:(g + 1) * 512]),
             pb[2].view(pb[2].ap[0:2, :]), reads=[pb[2]], writes=[sA])

    qtv = c["qt"]
    qt32 = qtv.ap[:, 0:32].rearrange("p (t two) -> p t two", two=2)
    for t in range(NT):
        p.op("T", "transpose", pb[0].view(pb[0].ap[:, 2 * t:2 * t + 2]),
             qp2.view(qp2.ap[:, t * 128:(t + 1) * 128]), c["id_f"][0:2, 0:2],
             reads=[sA, c["id_f"]], writes=[pb[0]])
    p.op("V", "tensor_copy", qtv[:, 0:32], pb[0].view(pb[0].ap[:, 0:32]),
         reads=[pb[0]], writes=[qtv])
    a_t = qtv.view(qt32[:, :, 0])
    b_t = qtv.view(qt32[:, :, 1])
    r_t = qtv[:, 32:48]
    sq2 = st.view(st.ap[:, 16:48].rearrange("p (t two) -> p t two", two=2))
    p.op("V", "tensor_tensor", sq2, qtv.view(qt32), qtv.view(qt32), op=ALU.mult,
         reads=[qtv], writes=[st])
    p.op("V", "tensor_tensor", r_t, sq2.view(sq2.ap[:, :, 0]), sq2.view(sq2.ap[:, :, 1]),
         op=ALU.add, reads=[st], writes=[qtv])
    rstage = ar.at_f32(off_wvT, "rstage", (2048,))
    for t in range(NT):
        p.op("T", "transpose", pb[2].view(pb[2].ap[0:1, 0:128]), r_t[:, t:t + 1],
             c["id_f"], reads=[qtv, c["id_f"]], writes=[pb[2]])
        p.op("V", "tensor_copy", rstage.view(rstage.ap[0:1, t * 128:(t + 1) * 128]),
             pb[2].view(pb[2].ap[0:1, 0:128]), reads=[pb[2]], writes=[rstage])
    for hh in range(2):
        p.dma("Y", sA.view(sA.ap[2:3, hh * 1024:(hh + 1) * 1024]),
              rstage.view(rstage.ap[0:1, hh * 1024:(hh + 1) * 1024]))

    stc = st[:, 48:57]
    tmp16 = st[:, 16:32]
    p.op("V", "tensor_reduce", stc[:, 0:1], a_t, axis=AX.X, op=ALU.add, reads=[qtv], writes=[st])
    p.op("V", "tensor_reduce", stc[:, 1:2], b_t, axis=AX.X, op=ALU.add, reads=[qtv], writes=[st])
    p.op("V", "tensor_reduce", stc[:, 2:3], r_t, axis=AX.X, op=ALU.add, reads=[qtv], writes=[st])
    p.op("V", "tensor_tensor", tmp16, a_t, a_t, op=ALU.mult, reads=[qtv], writes=[st])
    p.op("V", "tensor_reduce", stc[:, 3:4], tmp16, axis=AX.X, op=ALU.add, reads=[st], writes=[st])
    p.op("V", "tensor_tensor", tmp16, b_t, b_t, op=ALU.mult, reads=[qtv], writes=[st])
    p.op("V", "tensor_reduce", stc[:, 4:5], tmp16, axis=AX.X, op=ALU.add, reads=[st], writes=[st])
    p.op("V", "tensor_tensor", tmp16, a_t, b_t, op=ALU.mult, reads=[qtv], writes=[st])
    p.op("V", "tensor_reduce", stc[:, 5:6], tmp16, axis=AX.X, op=ALU.add, reads=[st], writes=[st])
    p.op("V", "tensor_tensor", tmp16, r_t, r_t, op=ALU.mult, reads=[qtv], writes=[st])
    p.op("V", "tensor_reduce", stc[:, 6:7], tmp16, axis=AX.X, op=ALU.add, reads=[st], writes=[st])
    p.op("V", "tensor_tensor", tmp16, r_t, a_t, op=ALU.mult, reads=[qtv], writes=[st])
    p.op("V", "tensor_reduce", stc[:, 7:8], tmp16, axis=AX.X, op=ALU.add, reads=[st], writes=[st])
    p.op("V", "tensor_tensor", tmp16, r_t, b_t, op=ALU.mult, reads=[qtv], writes=[st])
    p.op("V", "tensor_reduce", stc[:, 8:9], tmp16, axis=AX.X, op=ALU.add, reads=[st], writes=[st])
    p.op("T", "matmul", pb[2].view(pb[2].ap[0:1, 0:9]), c["ones128"], stc,
         start=True, stop=True, reads=[c["ones128"], st], writes=[pb[2]])
    Srow = sm[32:33, 0:9]
    p.op("V", "tensor_copy", Srow, pb[2].view(pb[2].ap[0:1, 0:9]), reads=[pb[2]], writes=[sm])
    N = float(TOK)
    M2 = N * N

    def s_(i):
        return sm[32:33, 16 + i:17 + i]

    def S_(i):
        return sm[32:33, i:i + 1]
    for i in range(9):
        p.op("V", "tensor_scalar", S_(i), S_(i), 1.0 / N, None, ALU.mult, reads=[sm], writes=[sm])
    p.op("V", "tensor_tensor", s_(0), S_(0), S_(0), op=ALU.mult, reads=[sm], writes=[sm])
    p.op("V", "tensor_tensor", s_(1), S_(1), S_(1), op=ALU.mult, reads=[sm], writes=[sm])
    p.op("V", "tensor_tensor", s_(2), s_(0), s_(1), op=ALU.add, reads=[sm], writes=[sm])
    p.op("V", "tensor_scalar", s_(3), S_(2), 2.0, None, ALU.mult, reads=[sm], writes=[sm])
    p.op("V", "scalar_tensor_tensor", s_(4), s_(2), -2.0, s_(3), op0=ALU.mult, op1=ALU.add,
         reads=[sm], writes=[sm])
    p.op("V", "tensor_tensor", s_(5), S_(3), S_(3), op=ALU.mult, reads=[sm], writes=[sm])
    p.op("V", "tensor_tensor", s_(6), S_(4), S_(4), op=ALU.mult, reads=[sm], writes=[sm])
    p.op("V", "tensor_tensor", s_(7), S_(5), S_(5), op=ALU.mult, reads=[sm], writes=[sm])
    p.op("V", "tensor_tensor", s_(8), s_(5), s_(6), op=ALU.add, reads=[sm], writes=[sm])
    p.op("V", "scalar_tensor_tensor", s_(8), s_(7), 2.0, s_(8), op0=ALU.mult, op1=ALU.add,
         reads=[sm], writes=[sm])
    p.op("V", "tensor_tensor", s_(9), S_(0), S_(7), op=ALU.mult, reads=[sm], writes=[sm])
    p.op("V", "tensor_tensor", s_(10), S_(1), S_(8), op=ALU.mult, reads=[sm], writes=[sm])
    p.op("V", "tensor_tensor", s_(9), s_(9), s_(10), op=ALU.add, reads=[sm], writes=[sm])
    p.op("V", "tensor_tensor", s_(11), S_(2), S_(2), op=ALU.mult, reads=[sm], writes=[sm])
    p.op("V", "tensor_scalar", s_(12), S_(6), 2.0, None, ALU.mult, reads=[sm], writes=[sm])
    p.op("V", "scalar_tensor_tensor", s_(12), s_(11), 2.0, s_(12), op0=ALU.mult, op1=ALU.add,
         reads=[sm], writes=[sm])
    p.op("V", "scalar_tensor_tensor", s_(12), s_(8), 4.0, s_(12), op0=ALU.mult, op1=ALU.add,
         reads=[sm], writes=[sm])
    p.op("V", "scalar_tensor_tensor", s_(12), s_(9), -8.0, s_(12), op0=ALU.mult, op1=ALU.add,
         reads=[sm], writes=[sm])
    p.op("V", "tensor_tensor", s_(13), s_(4), s_(4), op=ALU.mult, reads=[sm], writes=[sm])
    p.op("V", "tensor_tensor", s_(13), s_(12), s_(13), op=ALU.subtract, reads=[sm], writes=[sm])
    p.op("V", "tensor_scalar", s_(13), s_(13), M2 / (M2 - 1.0), 1e-30, ALU.mult, ALU.max,
         reads=[sm], writes=[sm])
    p.op("S", "activation", s_(14), s_(13), AF.Sqrt, reads=[sm], writes=[sm])
    p.op("V", "tensor_scalar", s_(10), s_(14), 1e-20, None, ALU.max, reads=[sm], writes=[sm])
    p.op("V", "reciprocal", s_(10), s_(10), reads=[sm], writes=[sm])
    p.op("V", "tensor_tensor", s_(10), s_(13), s_(10), op=ALU.mult, reads=[sm], writes=[sm])
    p.op("V", "tensor_tensor", s_(14), s_(14), s_(10), op=ALU.add, reads=[sm], writes=[sm])
    p.op("V", "tensor_scalar", s_(14), s_(14), 0.5, None, ALU.mult, reads=[sm], writes=[sm])
    thr = sm[0:1, 140:141]
    p.op("V", "scalar_tensor_tensor", s_(15), s_(14), 1.25, s_(4),
         op0=ALU.mult, op1=ALU.add, reads=[sm], writes=[sm])
    p.op("V", "tensor_copy", thr, s_(15), reads=[sm], writes=[sm])

    p.op("V", "tensor_scalar", tmp16, r_t, 1e-24, None, ALU.max, reads=[qtv], writes=[st])
    p.op("S", "activation", tmp16, tmp16, AF.Ln, reads=[st], writes=[st])
    p.op("S", "activation", tmp16, tmp16, AF.Exp, scale=c["consts"][:, 1:2],
         reads=[st, c["consts"]], writes=[st])
    p.op("V", "tensor_scalar", c["s01"], tmp16, 0.1, None, ALU.mult,
         reads=[st], writes=[c["s01"]])
    p.op("T", "matmul", pb[2].view(pb[2].ap[:, 0:1]), sC.view(sC.ap[0:1, 1024:1152]),
         sm[0:1, 140:141], start=True, stop=True, reads=[sC, sm], writes=[pb[2]])
    p.op("V", "tensor_copy", st[:, 58:59], pb[2].view(pb[2].ap[:, 0:1]),
         reads=[pb[2]], writes=[st])

    emit_xT(p, c, xT)
    p.dma("Y", wvTs, scr["wvT0"].rearrange("(c p) o -> p c o", p=128),
          writes=[xt32, rstage])

    for t in range(NT):
        for h in range(2):
            bank = pb[5 + h]
            for cc in range(DC):
                p.op("T", "matmul", bank, xT[:, cc, t * 128:(t + 1) * 128],
                     wvTs[:, cc, h * 512:(h + 1) * 512],
                     start=(cc == 0), stop=(cc == DC - 1),
                     reads=[xT, wvTs], writes=[bank])
            p.op("S", "copy", v_sb[:, t, h * 512:(h + 1) * 512], bank,
                 reads=[bank], writes=[v_sb])

    for g in range(4):
        for cc in range(DC):
            p.op("T", "matmul", pb[2].view(pb[2].ap[0:1, :]),
                 c["w1_ppb"][:, cc:cc + 1], xT[:, cc, g * 512:(g + 1) * 512],
                 start=(cc == 0), stop=(cc == DC - 1),
                 reads=[c["w1_ppb"], xT], writes=[pb[2]])
        p.op("V", "tensor_copy", r1b.view(r1b.ap[:, g * 512:(g + 1) * 512]),
             pb[2].view(pb[2].ap[0:1, :]), reads=[pb[2]], writes=[rbf])

    p.barrier()

    o = off_wvT
    PT = ar.at_bf16(o, "PT", (NT, 128)); o += 2048
    q1 = ar.at_bf16(o, "q1", (DC, 128)); o += 1024
    ctx_sb = ar.at_bf16(o, "ctx_sb", (1024,)); o += 1024
    ctxT = ar.at_bf16(o, "ctxT", (DC, 128)); o += 1024
    Pp = [ar.at_bf16(o, "P0", (512,)), ar.at_bf16(o + 512, "P1", (512,))]; o += 1024
    t_sb = ar.at_f32(o, "t_sb", (512,)); o += 1024
    e_sb = ar.at_f32(o, "e_sb", (512,)); o += 1024
    assert o <= off_wvT + 8192

    qpm2t = st.view(st.ap[0:2, 64:192])
    qpm3t = st.view(st.ap[0:3, 64:192])
    r_t16 = c["qt"][:, 32:48]
    p.op("V", "memset", qpm3t, 1.0, writes=[st])

    for qi in range(L0A_NT):
        if L0A_BAR >= 1:
            p.barrier()
        ng = ceil_div(128 * (qi + 1), 512)
        par = qi % 2
        p.op("V", "tensor_scalar", qpm2t, qp2.view(qp2.ap[:, qi * 128:(qi + 1) * 128]),
             -2.0, None, ALU.mult, reads=[sA], writes=[st])
        for d2c in range(DC):
            bank = pb[3 + d2c // 4]
            for d1c in range(DC):
                p.op("T", "matmul", bank.view(bank.ap[:, (d2c % 4) * 128:(d2c % 4 + 1) * 128]),
                     g0s[:, d1c, d2c * 128:(d2c + 1) * 128],
                     xT[:, d1c, qi * 128:(qi + 1) * 128],
                     start=(d2c % 4 == 0 and d1c == 0),
                     stop=(d2c % 4 == 3 and d1c == DC - 1),
                     reads=[g0s, xT], writes=[bank])
        q1f = q1.view(q1.ap.rearrange("p a b -> p (a b)"))
        p.op("S", "copy", q1f.view(q1f.ap[:, 0:512]), pb[3], reads=[pb[3]], writes=[q1])
        p.op("S", "copy", q1f.view(q1f.ap[:, 512:1024]), pb[4], reads=[pb[4]], writes=[q1])

        for kg in range(ng):
            if L0A_BAR >= 2:
                p.barrier()
            sl = slice(kg * 512, (kg + 1) * 512)
            sbank = pb[kg % 2]
            p.op("T", "matmul", pb[2], qpm3t, qp3.view(qp3.ap[:, sl]),
                 start=True, stop=True, reads=[st, sA], writes=[pb[2]])
            for cc in range(DC):
                p.op("T", "matmul", sbank, q1[:, cc, :], xT[:, cc, sl],
                     start=(cc == 0), stop=False, reads=[q1, xT], writes=[sbank])
            p.op("T", "matmul", sbank, onesb.view(onesb.ap[:, 0:128]),
                 r1b.view(r1b.ap[:, sl]), start=False, stop=True,
                 reads=[rbf], writes=[sbank])
            p.op("V", "tensor_scalar", t_sb, pb[2], r_t16[:, qi:qi + 1].ap, st[:, 58:59].ap,
                 ALU.add, ALU.is_le, reads=[pb[2], c["qt"], st], writes=[t_sb])
            p.op("V", "scalar_tensor_tensor", pb[2], t_sb, 4096.0, sbank,
                 op0=ALU.mult, op1=ALU.add, reads=[t_sb, sbank], writes=[pb[2]])
            p.op("S", "activation", Pp[kg % 2], pb[2], AF.Exp,
                 bias=c["consts"][:, 4:5], scale=c["consts"][:, 2:3],
                 reads=[pb[2], c["consts"]], writes=[Pp[kg % 2]])
            if kg == ng - 1:
                dlt = qi - 4 * kg
                p.op("V", "tensor_tensor", Pp[kg % 2], Pp[kg % 2],
                     c["cmask"][:, dlt, :], op=ALU.mult,
                     reads=[Pp[kg % 2], c["cmask"]], writes=[Pp[kg % 2]])
            acc = st[:, par * 2:par * 2 + 1]
            tmp = st[:, par * 2 + 1:par * 2 + 2]
            dst = acc if kg == 0 else tmp
            p.op("V", "tensor_reduce", dst, Pp[kg % 2], axis=AX.X, op=ALU.add,
                 reads=[Pp[kg % 2]], writes=[st])
            if kg > 0:
                p.op("V", "tensor_tensor", acc, acc, tmp, op=ALU.add, reads=[st], writes=[st])
            nch = min(4, qi + 1 - 4 * kg)
            for j in range(nch):
                p.op("T", "transpose", pstb.view(pstb.ap[:, j * 128:(j + 1) * 128]),
                     Pp[kg % 2].view(Pp[kg % 2].ap[:, j * 128:(j + 1) * 128]), c["id_bf"],
                     reads=[Pp[kg % 2], c["id_bf"]], writes=[pstb])
            for j in range(nch):
                p.op("V", "tensor_copy", PT[:, 4 * kg + j, :],
                     pstb.view(pstb.ap[:, j * 128:(j + 1) * 128]),
                     reads=[pstb], writes=[PT])
        p.op("V", "reciprocal", c["crec"][:, qi:qi + 1], st[:, par * 2:par * 2 + 1],
             reads=[st], writes=[c["crec"]])
        for h in range(2):
            bank = pb[3 + h]
            for b in range(qi + 1):
                p.op("T", "matmul", bank, PT[:, b, :], v_sb[:, b, h * 512:(h + 1) * 512],
                     start=(b == 0), stop=(b == qi), reads=[PT, v_sb], writes=[bank])
        p.op("S", "copy", ctx_sb.view(ctx_sb.ap[:, 0:512]), pb[3],
             reads=[pb[3]], writes=[ctx_sb])
        p.op("S", "copy", ctx_sb.view(ctx_sb.ap[:, 512:1024]), pb[4],
             reads=[pb[4]], writes=[ctx_sb])
        for h in range(2):
            p.op("T", "matmul", pb[3 + h], qp2.view(qp2.ap[:, qi * 128:(qi + 1) * 128]),
                 wfqT.view(wfqT.ap[:, h * 512:(h + 1) * 512]), start=True, stop=True,
                 reads=[sA, sC], writes=[pb[3 + h]])
        def dvc(cc):
            return ctxT.view(ctxT.ap[:, cc, :])
        pe_transpose_sb(p, c, ctx_sb, dvc, DC)
        for h in range(2):
            bank = pb[5 + h]
            for cc in range(DC):
                p.op("T", "matmul", bank, ctxT[:, cc, :], woTs[:, cc, h * 512:(h + 1) * 512],
                     start=(cc == 0), stop=(cc == DC - 1),
                     reads=[ctxT, woTs], writes=[bank])
        for h in range(2):
            hsl = slice(h * 512, (h + 1) * 512)
            p.op("V", "scalar_tensor_tensor", e_sb, pb[3 + h], c["s01"][:, qi:qi + 1],
                 c["bc_bc"][:, hsl], op0=ALU.mult, op1=ALU.add,
                 reads=[pb[3 + h], c["s01"], c["bc_bc"]], writes=[e_sb])
            p.op("V", "scalar_tensor_tensor", pb[2], pb[5 + h], c["crec"][:, qi:qi + 1],
                 e_sb, op0=ALU.mult, op1=ALU.add,
                 reads=[pb[5 + h], c["crec"], e_sb], writes=[pb[2]])
            xsl = xr[:, qi, hsl]
            p.op("V", "scalar_tensor_tensor", xsl, xsl, 2.0, pb[2],
                 op0=ALU.mult, op1=ALU.add, reads=[xr, pb[2]], writes=[xr])
        emit_ln2(p, c, xr[:, qi, :], pb[5], pb[6], par)



def emit_l0_ffn(p, c):
    ar = c["ar"]
    w = c["w"]
    pb = c["pb"]
    xr = c["xr"]

    xT = ar.bf16("xT", (DC, TOK))
    u1 = ar.bf16("u1", (DC, 256))
    v1 = ar.bf16("v1", (2, 4096))
    u2 = ar.bf16("u2", (32, 256))
    v2 = ar.bf16("v2", (2, 1024))
    b2r = ar.f32("b2r", (1024,))
    b2_row = b2r[0:1, 0:1024]
    ones1 = c["stripeC"].view(c["stripeC"].ap[0:1, 1024:1152])
    PT1 = ar.bf16("PT1", (2, 512))
    H = ar.bf16("H", (32, 512))
    QT = ar.bf16("QT", (2, 512))

    p.dma("P", u1, w["L0_ffn_U1"].rearrange("(c p) r -> p c r", p=128))
    p.dma("P", v1, w["L0_ffn_V1"].rearrange("(rh p) f -> p rh f", p=128))
    u2v = w["L0_ffn_U2"].rearrange("(c p) r -> p c r", p=128)
    p.dma("P", u2, u2v)
    p.dma("P", v2, w["L0_ffn_V2"].rearrange("(rh p) f -> p rh f", p=128))
    p.dma("Y", b2_row, w["L0_ffn_b2"].unsqueeze(0))
    emit_xT(p, c, xT)

    for g in range(4):
        gsl = slice(g * 512, (g + 1) * 512)
        for rh in range(2):
            pw = pb[4 + rh]
            for cc in range(DC):
                p.op("T", "matmul", pw, u1[:, cc, rh * 128:(rh + 1) * 128],
                     xT[:, cc, gsl], start=(cc == 0), stop=(cc == DC - 1),
                     reads=[u1, xT], writes=[pw])
            p.op("S", "copy", PT1[:, rh, :], pw, reads=[pw], writes=[PT1])
        for fc in range(32):
            pw = pb[fc % 4]
            for rh in range(2):
                p.op("T", "matmul", pw, v1[:, rh, fc * 128:(fc + 1) * 128],
                     PT1[:, rh, :], start=(rh == 0), stop=(rh == 1),
                     reads=[v1, PT1], writes=[pw])
            p.op("S", "activation", H[:, fc, :], pw, AF.Gelu_apprx_tanh,
                 bias=c["b1_pp"][:, fc:fc + 1], reads=[pw, c["b1_pp"]], writes=[H])
        for rh in range(2):
            pw = pb[4 + rh]
            for fc in range(32):
                p.op("T", "matmul", pw, u2[:, fc, rh * 128:(rh + 1) * 128],
                     H[:, fc, :], start=(fc == 0), stop=(fc == 31),
                     reads=[u2, H], writes=[pw])
            p.op("S", "copy", QT[:, rh, :], pw, reads=[pw], writes=[QT])
        for tt in range(4):
            t = g * 4 + tt
            for h in range(2):
                pw = pb[(tt % 2) * 2 + h]
                p.op("T", "matmul", pw, ones1, b2_row.view(b2_row.ap[:, h * 512:(h + 1) * 512]),
                     start=True, stop=False, reads=[b2r, c["stripeC"]], writes=[pw])
                for rh in range(2):
                    p.op("T", "matmul", pw, QT[:, rh, tt * 128:(tt + 1) * 128],
                         v2[:, rh, h * 512:(h + 1) * 512], start=False, stop=(rh == 1),
                         reads=[QT, v2], writes=[pw])
                xsl = xr[:, t, h * 512:(h + 1) * 512]
                p.op("V", "tensor_tensor", xsl, xsl, pw, op=ALU.add,
                     reads=[xr, pw], writes=[xr])
            emit_ln2(p, c, xr[:, t, :], pb[4], pb[5], t % 2)



def emit_l1_attn(p, c):
    ar = c["ar"]
    scr = c["scr"]
    pb = c["pb"]
    pstb = c["pstb"]
    xr = c["xr"]
    st = c["stat"]

    xT = ar.bf16("xT", (DC, TOK))
    v_sb = ar.bf16("v_sb", (NT, D))
    g1s = ar.bf16("g1s", (DC, 1024))
    off_wvT = (ar.mark() + 1) & ~1
    wvTs = ar.bf16("wvT1s", (DC, 1024))

    p.dma("Y", g1s, scr["g1"].rearrange("(c p) o -> p c o", p=128))
    p.dma("S", wvTs, scr["wvT1"].rearrange("(c p) o -> p c o", p=128))

    emit_xT(p, c, xT)
    for t in range(NT):
        for h in range(2):
            bank = pb[5 + h]
            for cc in range(DC):
                p.op("T", "matmul", bank, xT[:, cc, t * 128:(t + 1) * 128],
                     wvTs[:, cc, h * 512:(h + 1) * 512],
                     start=(cc == 0), stop=(cc == DC - 1),
                     reads=[xT, wvTs], writes=[bank])
            p.op("S", "copy", v_sb[:, t, h * 512:(h + 1) * 512], bank,
                 reads=[bank], writes=[v_sb])

    p.barrier()
    o = off_wvT
    PT = ar.at_bf16(o, "PT1a", (NT, 128)); o += 2048
    q1 = ar.at_bf16(o, "q1a", (DC, 128)); o += 1024
    Pp = [ar.at_bf16(o, "P0a", (512,)), ar.at_bf16(o + 512, "P1a", (512,))]; o += 1024
    e_sb = ar.at_f32(o, "e1a", (512,)); o += 1024
    assert o <= off_wvT + 8192

    for qi in range(NT):
        par = qi % 2
        for d2c in range(DC):
            bank = pb[3 + d2c // 4]
            for d1c in range(DC):
                p.op("T", "matmul", bank.view(bank.ap[:, (d2c % 4) * 128:(d2c % 4 + 1) * 128]),
                     g1s[:, d1c, d2c * 128:(d2c + 1) * 128],
                     xT[:, d1c, qi * 128:(qi + 1) * 128],
                     start=(d2c % 4 == 0 and d1c == 0),
                     stop=(d2c % 4 == 3 and d1c == DC - 1),
                     reads=[g1s, xT], writes=[bank])
        q1f = q1.view(q1.ap.rearrange("p a b -> p (a b)"))
        p.op("S", "copy", q1f.view(q1f.ap[:, 0:512]), pb[3], reads=[pb[3]], writes=[q1])
        p.op("S", "copy", q1f.view(q1f.ap[:, 512:1024]), pb[4], reads=[pb[4]], writes=[q1])
        for kg in range(4):
            sl = slice(kg * 512, (kg + 1) * 512)
            sbank = pb[kg % 2]
            for cc in range(DC):
                p.op("T", "matmul", sbank, q1[:, cc, :], xT[:, cc, sl],
                     start=(cc == 0), stop=(cc == DC - 1), reads=[q1, xT], writes=[sbank])
            p.op("S", "activation", Pp[kg % 2], sbank, AF.Exp,
                 bias=c["consts"][:, 5:6], scale=c["consts"][:, 2:3],
                 reads=[sbank, c["consts"]], writes=[Pp[kg % 2]])
            acc = st[:, par * 2:par * 2 + 1]
            tmp = st[:, par * 2 + 1:par * 2 + 2]
            dst = acc if kg == 0 else tmp
            p.op("V", "tensor_reduce", dst, Pp[kg % 2], axis=AX.X, op=ALU.add,
                 reads=[Pp[kg % 2]], writes=[st])
            if kg > 0:
                p.op("V", "tensor_tensor", acc, acc, tmp, op=ALU.add, reads=[st], writes=[st])
            for j in range(4):
                p.op("T", "transpose", pstb.view(pstb.ap[:, j * 128:(j + 1) * 128]),
                     Pp[kg % 2].view(Pp[kg % 2].ap[:, j * 128:(j + 1) * 128]), c["id_bf"],
                     reads=[Pp[kg % 2], c["id_bf"]], writes=[pstb])
            for j in range(4):
                p.op("V", "tensor_copy", PT[:, 4 * kg + j, :],
                     pstb.view(pstb.ap[:, j * 128:(j + 1) * 128]),
                     reads=[pstb], writes=[PT])
        p.op("V", "reciprocal", c["crec"][:, qi:qi + 1], st[:, par * 2:par * 2 + 1],
             reads=[st], writes=[c["crec"]])
        for h in range(2):
            bank = pb[5 + h]
            for b in range(NT):
                p.op("T", "matmul", bank, PT[:, b, :], v_sb[:, b, h * 512:(h + 1) * 512],
                     start=(b == 0), stop=(b == NT - 1), reads=[PT, v_sb], writes=[bank])
            p.op("V", "tensor_scalar", e_sb, bank, c["crec"][:, qi:qi + 1], None,
                 ALU.mult, reads=[bank, c["crec"]], writes=[e_sb])
            xsl = xr[:, qi, h * 512:(h + 1) * 512]
            p.op("V", "tensor_tensor", xsl, xsl, e_sb, op=ALU.add, reads=[xr, e_sb], writes=[xr])
        emit_ln2(p, c, xr[:, qi, :], pb[5], pb[6], par)



def emit_l1_ffn(p, c):
    ar = c["ar"]
    scr = c["scr"]
    pb = c["pb"]
    xr = c["xr"]
    NI = 2816
    NCH = [512, 512, 512, 512, 512, 256]

    xqT = ar.bf16("xqT", (DC, TOK))
    hqT = ar.bf16("hqT", (4, 22, 128))
    wg = ar.bf16("wg", (DC, 512))
    wu = ar.bf16("wu", (DC, 512))
    wdq = [ar.bf16("wd0", (22, 256)), ar.bf16("wd1", (22, 256))]
    qtmp = ar.f32("qtmp", (3, 512))
    scq = ar.f32("scq", (32,))
    rcq = ar.f32("rcq", (32,))
    qtmp2 = ar.f32("qtmp2", (3, 512))
    scq2 = ar.f32("scq2", (32,))
    rcq2 = ar.f32("rcq2", (32,))
    qsets = [(qtmp, scq, rcq), (qtmp2, scq2, rcq2)]
    xq_t = ar.bf16("xq_t", (1024,))
    h_sb = [ar.f32("h0", (512,)), ar.f32("h1", (512,))]
    hq_c = [ar.bf16("hq0", (512,)), ar.bf16("hq1", (512,))]

    gview = scr["wgT"].rearrange("(c p) n -> p c n", p=128)
    uview = scr["wuT"].rearrange("(c p) n -> p c n", p=128)
    dview = scr["wdT"].rearrange("(c p) n -> p c n", p=128)

    for t in range(NT):
        for hh in range(2):
            qb_, sc_, rc_ = qsets[hh]
            emit_quant(p, c, xr[:, t, hh * 512:(hh + 1) * 512],
                       xq_t[:, hh * 512:(hh + 1) * 512], qb_, sc_, rc_, 512)

        def dv(cc, t=t):
            return xqT.view(xqT.ap[:, cc, t * 128:(t + 1) * 128])
        pe_transpose_sb(p, c, xq_t, dv, DC)

    if L1F_BAR >= 1:
        p.barrier()
    for Q in range(4):
        if L1F_BAR >= 2:
            p.barrier()
        tiles = [Q * 4 + i for i in range(4)]
        for ngi in range(6):
            if L1F_BAR >= 3:
                p.barrier()
            nn = NCH[ngi]
            n0 = ngi * 512
            p.dma("Y", wg[:, :, 0:nn], gview[:, :, n0:n0 + nn])
            p.dma("P", wu[:, :, 0:nn], uview[:, :, n0:n0 + nn])
            for ti, t in enumerate(tiles):
                pg = pb[4] if ti % 2 == 0 else pb[2]
                pu = pb[5] if ti % 2 == 0 else pb[3]
                pgv = pg.view(pg.ap[:, 0:nn])
                puv = pu.view(pu.ap[:, 0:nn])
                for cc in range(DC):
                    p.op("T", "matmul", pgv, xqT[:, cc, t * 128:(t + 1) * 128],
                         wg[:, cc, 0:nn], start=(cc == 0), stop=(cc == DC - 1),
                         reads=[xqT, wg], writes=[pg])
                for cc in range(DC):
                    p.op("T", "matmul", puv, xqT[:, cc, t * 128:(t + 1) * 128],
                         wu[:, cc, 0:nn], start=(cc == 0), stop=(cc == DC - 1),
                         reads=[xqT, wu], writes=[pu])
                par = ti % 2
                p.op("S", "activation", h_sb[par][:, 0:nn], pgv, AF.Silu,
                     reads=[pg], writes=[h_sb[par]])
                p.op("V", "tensor_tensor", h_sb[par][:, 0:nn], h_sb[par][:, 0:nn], puv,
                     op=ALU.mult, reads=[h_sb[par], pu], writes=[h_sb[par]])
                qb_, sc_, rc_ = qsets[par]
                emit_quant(p, c, h_sb[par][:, 0:nn], hq_c[par][:, 0:nn],
                           qb_, sc_, rc_, nn)

                def dvh(cc, ti=ti, ngi=ngi, par=par):
                    return hqT.view(hqT.ap[:, ti, ngi * 4 + cc, :])
                pe_transpose_sb(p, c, hq_c[par], dvh, nn // 128)
        p.dma("S", wdq[0], dview[:, :, 0:256])
        for q4 in range(4):
            if q4 < 3:
                p.dma("S", wdq[(q4 + 1) % 2],
                      dview[:, :, (q4 + 1) * 256:(q4 + 2) * 256])
            wd = wdq[q4 % 2]
            for ti, t in enumerate(tiles):
                pw = pb[ti % 2]
                pwv = pw.view(pw.ap[:, 0:256])
                for cc in range(22):
                    p.op("T", "matmul", pwv, hqT[:, ti, cc, :], wd[:, cc, :],
                         start=(cc == 0), stop=(cc == 21), reads=[hqT, wd], writes=[pw])
                xsl = xr[:, t, q4 * 256:(q4 + 1) * 256]
                p.op("V", "tensor_tensor", xsl, xsl, pwv, op=ALU.add,
                     reads=[xr, pw], writes=[xr])
        for ti, t in enumerate(tiles):
            emit_ln2(p, c, xr[:, t, :], pb[4], pb[5], t % 2)



UNFUSED = False
_cache = {}


def _get_prog(blocks):
    key = tuple(blocks)
    if key not in _cache:
        _cache[key] = build_program(list(blocks))
    return _cache[key]


def kernel(**inputs):
    x = np.ascontiguousarray(np.asarray(inputs["L0_attn_hidden_states"], np.float32))
    B, T_, D_ = x.shape
    xc = x.reshape(N_CORES, N_ITEMS * T_, D_)
    stages = [["l0a"], ["l0f"], ["l1a"], ["l1f"]] if UNFUSED else [["l0a", "l0f", "l1a", "l1f"]]
    act = xc
    for blocks in stages:
        nc, names = _get_prog(tuple(blocks))
        in_maps = []
        for core in range(N_CORES):
            m = {"x": np.ascontiguousarray(act[core])}
            for nm in names:
                m[nm] = np.ascontiguousarray(np.asarray(inputs[nm], np.float32))
            in_maps.append(m)
        res = run_bass_kernel_spmd(nc, in_maps, core_ids=list(range(N_CORES)))
        act = np.stack([res.results[i]["y"] for i in range(N_CORES)], axis=0)
    return act.reshape(B, T_, D_)
